# version 02
# baseline (speedup 1.0000x reference)
import numpy as np
from contextlib import ExitStack

import concourse.bass as bass
import concourse.mybir as mybir
from concourse.bass_utils import run_bass_kernel_spmd

F32 = mybir.dt.float32
BF16 = mybir.dt.bfloat16
I32 = mybir.dt.int32
AF = mybir.ActivationFunctionType
ALU = mybir.AluOpType
AX = mybir.AxisListType

NCORE = 8
D = 2048
TSEQ = 8192
BATCH = 2
BAND = 512
NBANDS = 16
LBANDS = 4
LT = LBANDS * BAND
NDT = D // 128
H0 = 16
BLOCK = 256
NEG_BIG = -1.0e30
LN_EPS = 1e-5
RMS_EPS = 1e-6

ENGS = ("sync", "gpsimd", "scalar", "vector", "tensor")
SEMS = ("ld", "wt", "st", "t", "v", "s", "g")
DMA_SEMS = {"ld", "wt", "st"}
DEFAULT_SEM = {"sync": "ld", "gpsimd": "g", "scalar": "s", "vector": "v",
               "tensor": "t"}


class Step:
    __slots__ = ("eng", "fn", "deps", "sem", "inc", "val", "name")

    def __init__(self, eng, fn, deps, sem, inc, name):
        self.eng = eng
        self.fn = fn
        self.deps = deps
        self.sem = sem
        self.inc = inc
        self.val = None
        self.name = name


class Sched:
    def __init__(self):
        self.steps = []
        self.counts = {s: 0 for s in SEMS}

    def add(self, eng, fn, deps=(), sem=None, name=""):
        if sem is None:
            sem = DEFAULT_SEM[eng]
        inc = 16 if sem in DMA_SEMS else 1
        dd = []
        for d in deps:
            if d is None:
                continue
            if isinstance(d, (list, tuple)):
                dd.extend(x for x in d if x is not None)
            else:
                dd.append(d)
        st = Step(eng, fn, dd, sem, inc, name)
        self.counts[sem] += inc
        st.val = self.counts[sem]
        self.steps.append(st)
        return st

    def dma(self, eng, sem, out, in_, deps=(), name=""):
        return self.add(eng, lambda e: e.dma_start(out=out, in_=in_), deps,
                        sem=sem, name=name)

    def emit(self, nc, sems):
        per_eng = {en: [s for s in self.steps if s.eng == en] for en in ENGS}
        with nc.Block() as block:
            for en in ENGS:
                def make(en):
                    def body(raw):
                        guaranteed = {s: 0 for s in SEMS}
                        for st in per_eng[en]:
                            for d in st.deps:
                                if (d.eng == en == "tensor"
                                        and d.sem not in DMA_SEMS):
                                    continue
                                if d.val <= guaranteed[d.sem]:
                                    continue
                                raw.wait_ge(sems[d.sem], d.val)
                                guaranteed[d.sem] = d.val
                            inst = st.fn(raw)
                            assert inst is not None, f"step {st.name}"
                            inst.then_inc(sems[st.sem], st.inc)
                    return body
                getattr(block, en)(make(en))


class Banks:

    def __init__(self, n=16):
        self.last_reader = {}
        self.last_writer = {}

    def rd(self, bank, step):
        self.last_reader[bank] = step

    def wr(self, bank, step):
        self.last_writer[bank] = step

    def dep_w(self, bank):
        return [self.last_reader.get(bank), self.last_writer.get(bank)]

    def dep_r(self, bank):
        return [self.last_writer.get(bank)]



def build_launch1():
    nc = bass.Bass("TRN2", target_bir_lowering=False, debug=False)

    i_x = nc.dram_tensor("x_t", [D, LT], F32, kind="ExternalInput").ap()
    i_wk = nc.dram_tensor("wk_t", [D, D], F32, kind="ExternalInput").ap()
    i_wv = nc.dram_tensor("wv_t", [D, D], F32, kind="ExternalInput").ap()
    i_slope = nc.dram_tensor("slope", [1, H0], F32, kind="ExternalInput").ap()
    i_gatew = nc.dram_tensor("gate_w", [D, 8], F32, kind="ExternalInput").ap()
    i_wg = nc.dram_tensor("wg", [8, D, 1408], F32, kind="ExternalInput").ap()
    i_wu = nc.dram_tensor("wu", [8, D, 1408], F32, kind="ExternalInput").ap()
    i_wd = nc.dram_tensor("wd", [8, 1408, D], F32, kind="ExternalInput").ap()
    i_swg = nc.dram_tensor("swg", [D, 2816], F32, kind="ExternalInput").ap()
    i_swu = nc.dram_tensor("swu", [D, 2816], F32, kind="ExternalInput").ap()
    i_swd = nc.dram_tensor("swd", [2816, D], F32, kind="ExternalInput").ap()

    o_xn = nc.dram_tensor("xn0_o", [D, LT], BF16, kind="ExternalOutput").ap()
    o_k = nc.dram_tensor("k_o", [D, LT], BF16, kind="ExternalOutput").ap()
    o_v = nc.dram_tensor("v_o", [D, LT], BF16, kind="ExternalOutput").ap()
    o_x1p = nc.dram_tensor("x1p_o", [D, LT], F32, kind="ExternalOutput").ap()
    o_sum = nc.dram_tensor("sum_o", [128, H0, LBANDS, 128], F32,
                           kind="ExternalOutput").ap()
    d_comb = nc.dram_tensor("comb_d", [1, 8192], F32, kind="Internal").ap()

    x_r = i_x.rearrange("(k p) t -> p k t", p=128)
    wk_r = i_wk.rearrange("(k p) e -> p k e", p=128)
    wv_r = i_wv.rearrange("(k p) e -> p k e", p=128)
    gw_r = i_gatew.rearrange("(k p) e -> p k e", p=128)
    o_xn_r = o_xn.rearrange("(k p) t -> p k t", p=128)
    o_k_r = o_k.rearrange("(k p) t -> p k t", p=128)
    o_v_r = o_v.rearrange("(k p) t -> p k t", p=128)
    o_x1p_r = o_x1p.rearrange("(k p) t -> p k t", p=128)

    sc = Sched()
    bk = Banks()

    with ExitStack() as stk:
        ec = stk.enter_context
        xb = ec(nc.sbuf_tensor("xb", [128, NDT, 1024], F32))
        xnb = ec(nc.sbuf_tensor("xnb", [128, NDT, 1024], BF16))
        regA = ec(nc.sbuf_tensor("regA", [128, 16384], BF16))
        kT = regA[:, 0:8192].rearrange("p (h t) -> p h t", h=H0)
        vT = regA[:, 8192:16384].rearrange("p (h t) -> p h t", h=H0)
        wdb = regA[:, 0:1536].rearrange("p (s j t) -> p s j t", s=2, j=6)
        gS = regA[:, 4096:16384].rearrange("p (j t) -> p j t", j=12)
        wga = ec(nc.sbuf_tensor("wga", [128, 2, NDT, 128], BF16))
        wub = ec(nc.sbuf_tensor("wub", [128, 2, NDT, 128], BF16))
        gwb = ec(nc.sbuf_tensor("gwb", [128, NDT, 8], BF16))
        xsq = ec(nc.sbuf_tensor("xsq", [128, 2, BAND], F32))
        nat8 = ec(nc.sbuf_tensor("nat8", [128, 8, 128], BF16))
        sums = ec(nc.sbuf_tensor("sums", [128, H0, 128], F32))
        sumt = ec(nc.sbuf_tensor("sumt", [128, 128], F32))
        combB = ec(nc.sbuf_tensor("combB", [128, 2, BAND], F32))
        sgb = ec(nc.sbuf_tensor("sgb", [128, BAND], BF16))
        gtmp = ec(nc.sbuf_tensor("gtmp", [128, BAND], BF16))
        ktmp = ec(nc.sbuf_tensor("ktmp", [128, 2, BLOCK], BF16))
        kdec = ec(nc.sbuf_tensor("kdec", [128, H0, BLOCK], BF16))
        ones_f = ec(nc.sbuf_tensor("ones_f", [128, 1], F32))
        ones128 = ec(nc.sbuf_tensor("ones128", [128, 128], F32))
        ident_bf = ec(nc.sbuf_tensor("ident_bf", [128, 128], BF16))
        ident_f = ec(nc.sbuf_tensor("ident_f", [128, 128], F32))
        iota_pf = ec(nc.sbuf_tensor("iota_pf", [128, 128], F32))
        iota255 = ec(nc.sbuf_tensor("iota255", [128, BLOCK], F32))
        scols = ec(nc.sbuf_tensor("scols", [128, H0], F32))
        bdcol = ec(nc.sbuf_tensor("bdcol", [128, H0], F32))
        zrhs = ec(nc.sbuf_tensor("zrhs", [128, 2, BAND], F32))
        zrhsM = ec(nc.sbuf_tensor("zrhsM", [128, BAND], F32))
        ccol = ec(nc.sbuf_tensor("ccol", [128, 4], F32))
        slp = ec(nc.sbuf_tensor("slp", [1, H0], F32))
        rows = ec(nc.sbuf_tensor("rows", [1, 4 * BAND], F32))
        gsA = ec(nc.sbuf_tensor("gsA", [128, 8], F32))
        gsB = ec(nc.sbuf_tensor("gsB", [128, 8], F32))
        gcol = ec(nc.sbuf_tensor("gcol", [128, 4], F32))
        logit = ec(nc.sbuf_tensor("logit", [8, 1024], F32))
        combT = ec(nc.sbuf_tensor("combT", [8, 1024], F32))
        xtmp = ec(nc.sbuf_tensor("xtmp", [128, BAND], F32))

        ps = [ec(nc.psum_tensor(f"ps{i}", [128, BAND], F32)) for i in range(5)]
        ps_bf = ec(nc.psum_tensor("ps_bf", [128, 1024], BF16))
        ps_ra = ec(nc.psum_tensor("ps_ra", [1, BAND], F32))
        ps_rb = ec(nc.psum_tensor("ps_rb", [1, BAND], F32))

        sems = {}
        for n in SEMS:
            sems[n] = ec(nc.semaphore(n))

        ld_slope = sc.dma("sync", "ld", slp[:], i_slope, name="ld_slope")
        ld_gw = sc.dma("gpsimd", "wt", gwb[:], gw_r, name="ld_gw")

        def fn_iota(e):
            e.iota(iota_pf[:], pattern=[[-1, 128]], base=0,
                   channel_multiplier=1, allow_small_or_imprecise_dtypes=True)
            return e.iota(iota255[:], pattern=[[1, BLOCK]], base=-255,
                          channel_multiplier=0,
                          allow_small_or_imprecise_dtypes=True)
        s_iota = sc.add("gpsimd", fn_iota, name="iota")

        def fn_consts(e):
            e.memset(ones_f[:], 1.0)
            e.memset(ones128[:], 1.0)
            e.memset(zrhs[:], 0.0)
            e.memset(zrhsM[:], 0.0)
            e.memset(ccol[:, 0:1], LN_EPS)
            e.memset(ccol[:, 1:2], RMS_EPS)
            e.tensor_single_scalar(ident_bf[:], iota_pf[:], 0.0,
                                   op=ALU.is_equal)
            e.tensor_single_scalar(ident_f[:], iota_pf[:], 0.0,
                                   op=ALU.is_equal)
            return e.tensor_copy(zrhs[0:1, 0, 0:H0], slp[:])
        s_consts = sc.add("vector", fn_consts, [s_iota, ld_slope],
                          name="consts")

        s_scol_mm = sc.add(
            "tensor", lambda e: e.matmul(ps[0][:, 0:H0], ones128[:],
                                         zrhs[:, 0, 0:H0], start=True,
                                         stop=True), [s_consts], name="scolmm")
        bk.wr(0, s_scol_mm)

        def fn_scols(e):
            e.tensor_copy(scols[:], ps[0][:, 0:H0])
            return e.memset(zrhs[0:1, 0, 0:H0], 0.0)
        s_scols = sc.add("vector", fn_scols, [s_scol_mm], name="scols")
        bk.rd(0, s_scols)

        def fn_tables(e):
            e.activation(bdcol[:], scols[:], AF.Exp, scale=-256.0)
            i = None
            for h in range(H0):
                i = e.activation(kdec[:, h, :], iota255[:], AF.Exp,
                                 scale=scols[:, h:h + 1])
            return i
        s_tables = sc.add("scalar", fn_tables, [s_scols], name="tables")

        regA_readers = [s_tables]
        wbuf_guard = {0: None, 1: None}
        xb_guard = {0: [], 1: []}
        prev_sum_store = [None]
        prev_ln = {"sq": [None, None], "rstd": None, "bc": None}

        def band_phase(l):
            off = (l % 2) * BAND
            cols = slice(l * BAND, (l + 1) * BAND)
            ldx = sc.dma("sync", "ld", xb[:, :, off:off + BAND],
                         x_r[:, :, cols], xb_guard[l % 2], name=f"ldx{l}")
            xb_guard[l % 2] = []

            xbt = lambda kt: xb[:, kt, off:off + BAND]
            xnt = lambda kt: xnb[:, kt, off:off + BAND]
            sq = []
            mm2 = []
            for kt in range(NDT):
                deps = [ldx]
                if kt >= 2:
                    deps.append(mm2[kt - 2])
                elif prev_ln["sq"][kt] is not None:
                    deps.append(prev_ln["sq"][kt])
                s = sc.add("scalar", (lambda kt: lambda e: e.square(
                    xsq[:, kt % 2, :], xbt(kt)))(kt), deps, name=f"sq{l}_{kt}")
                sq.append(s)
                m = sc.add("tensor", (lambda kt: lambda e: e.matmul(
                    ps_rb[:], ones_f[:], xsq[:, kt % 2, :], start=(kt == 0),
                    stop=(kt == NDT - 1)))(kt),
                    [s] + (bk.dep_w(7) if kt == 0 else []),
                    name=f"sx2_{l}_{kt}")
                mm2.append(m)
            prev_ln["sq"] = [mm2[-2], mm2[-1]]
            bk.wr(7, mm2[-1])

            def mk_sx(e):
                i = None
                for kt in range(NDT):
                    i = e.matmul(ps_ra[:], ones_f[:], xbt(kt),
                                 start=(kt == 0), stop=(kt == NDT - 1))
                return i
            mm1 = sc.add("tensor", mk_sx, [ldx] + bk.dep_w(6), name=f"sx_{l}")
            bk.wr(6, mm1)

            mrow = rows[0:1, 0:BAND]
            vrow = rows[0:1, BAND:2 * BAND]
            rrow = rows[0:1, 2 * BAND:3 * BAND]
            mrr = rows[0:1, 3 * BAND:4 * BAND]

            def st_stats(e):
                e.tensor_single_scalar(mrow, ps_ra[:], 1.0 / D, op=ALU.mult)
                e.tensor_single_scalar(vrow, ps_rb[:], 1.0 / D, op=ALU.mult)
                e.tensor_mul(mrr, mrow, mrow)
                return e.tensor_sub(vrow, vrow, mrr)
            s_stats = sc.add("vector", st_stats,
                             [mm1, mm2[-1], prev_ln["rstd"]],
                             name=f"stats{l}")
            bk.rd(6, s_stats)
            bk.rd(7, s_stats)

            s_rstdA = sc.add("scalar", lambda e: e.activation(
                rrow, vrow, AF.Ln, bias=ccol[0:1, 0:1]), [s_stats],
                name=f"rstdA{l}")
            s_rstd = sc.add("scalar", lambda e: e.activation(
                rrow, rrow, AF.Exp, scale=-0.5), [s_rstdA], name=f"rstd{l}")
            prev_ln["rstd"] = s_rstd

            def st_rows(e):
                e.scalar_tensor_tensor(mrr, mrow, -1.0, rrow, op0=ALU.mult,
                                       op1=ALU.mult)
                e.tensor_copy(zrhs[0:1, 0, :], rrow)
                return e.tensor_copy(zrhs[0:1, 1, :], mrr)
            s_rows = sc.add("vector", st_rows, [s_rstd, prev_ln["bc"]],
                            name=f"lnrows{l}")

            def st_bc(e):
                e.matmul(ps[3][:], ones128[:], zrhs[:, 0, :], start=True,
                         stop=True)
                return e.matmul(ps[4][:], ones128[:], zrhs[:, 1, :],
                                start=True, stop=True)
            s_bc = sc.add("tensor", st_bc,
                          [s_rows] + bk.dep_w(3) + bk.dep_w(4),
                          name=f"lnbc{l}")
            bk.wr(3, s_bc)
            bk.wr(4, s_bc)
            prev_ln["bc"] = s_bc

            xn_steps = []
            for kt in range(NDT):
                def fn(e, kt=kt):
                    e.tensor_mul(xtmp[:], xbt(kt), ps[3][:])
                    return e.tensor_add(xnt(kt), xtmp[:], ps[4][:])
                xn_steps.append(sc.add("vector", fn, [s_bc],
                                       name=f"xn{l}_{kt}"))
            bk.rd(3, xn_steps[-1])
            bk.rd(4, xn_steps[-1])
            sc.dma("sync", "st", o_xn_r[:, :, cols],
                   xnb[:, :, off:off + BAND], xn_steps, name=f"st_xn{l}")

            kv_evac = []
            prev_group = {0: wbuf_guard[0], 1: wbuf_guard[1]}
            for j in range(2 * NDT):
                src = wk_r if j < NDT else wv_r
                jj = j % NDT
                slot = j % 2
                ldeps = [prev_group[slot]]
                if j < 2:
                    ldeps += regA_readers
                ldw = sc.dma("gpsimd", "wt", wga[:, slot],
                             src[:, :, jj * 128:(jj + 1) * 128], ldeps,
                             name=f"ldw{l}_{j}")

                def grp(e, slot=slot):
                    i = None
                    for kt in range(NDT):
                        i = e.matmul(ps[slot][:], wga[:, slot, kt, :],
                                     xnb[:, kt, off:off + BAND],
                                     start=(kt == 0), stop=(kt == NDT - 1))
                    return i
                g = sc.add("tensor", grp,
                           [ldw] + xn_steps + bk.dep_w(slot),
                           name=f"kvp{l}_{j}")
                prev_group[slot] = g
                bk.wr(slot, g)
                if j < NDT:
                    ev = sc.add("scalar", (lambda j, slot: lambda e:
                                e.activation(kT[:, j, :], ps[slot][:],
                                             AF.Silu))(j, slot), [g],
                                name=f"kev{l}_{j}")
                else:
                    ev = sc.add("scalar", (lambda j, slot: lambda e:
                                e.copy(vT[:, j - NDT, :], ps[slot][:])
                                )(j, slot), [g], name=f"vev{l}_{j}")
                bk.rd(slot, ev)
                kv_evac.append(ev)
                wbuf_guard[slot] = g
            st_k = sc.dma("sync", "st", o_k_r[:, :, cols], regA[:, 0:8192],
                          kv_evac[:NDT], name=f"st_k{l}")
            st_v = sc.dma("sync", "st", o_v_r[:, :, cols],
                          regA[:, 8192:16384], kv_evac[NDT:], name=f"st_v{l}")

            prev_cp1 = None
            sum_steps = []
            for h in range(H0):
                km0 = sc.add("vector", (lambda h: lambda e: e.tensor_mul(
                    ktmp[:, 0, :], kT[:, h, 0:BLOCK], kdec[:, h, :]))(h),
                    [kv_evac[h], s_tables], name=f"km0_{l}_{h}")
                km1 = sc.add("vector", (lambda h: lambda e: e.tensor_mul(
                    ktmp[:, 1, :], kT[:, h, BLOCK:2 * BLOCK], kdec[:, h, :])
                    )(h), [kv_evac[h], s_tables], name=f"km1_{l}_{h}")

                def tr0(e, h=h):
                    e.transpose(ps_bf[:, 0:128], ktmp[:, 0, 0:128],
                                ident_bf[:])
                    e.transpose(ps_bf[:, 128:256], ktmp[:, 0, 128:256],
                                ident_bf[:])
                    e.transpose(ps_bf[:, 256:384], vT[:, h, 0:128],
                                ident_bf[:])
                    return e.transpose(ps_bf[:, 384:512], vT[:, h, 128:256],
                                       ident_bf[:])
                t0s = sc.add("tensor", tr0,
                             [km0, kv_evac[NDT + h]] + bk.dep_w(5),
                             name=f"tr0_{l}_{h}")
                cp0 = sc.add("vector", lambda e: e.tensor_copy(
                    regA_nat(0), ps_bf[:, 0:512]), [t0s], name=f"cp0_{l}_{h}")

                def tr1(e, h=h):
                    e.transpose(ps_bf[:, 512:640], ktmp[:, 1, 0:128],
                                ident_bf[:])
                    e.transpose(ps_bf[:, 640:768], ktmp[:, 1, 128:256],
                                ident_bf[:])
                    e.transpose(ps_bf[:, 768:896],
                                vT[:, h, BLOCK:BLOCK + 128], ident_bf[:])
                    return e.transpose(ps_bf[:, 896:1024],
                                       vT[:, h, BLOCK + 128:2 * BLOCK],
                                       ident_bf[:])
                t1s = sc.add("tensor", tr1, [km1, cp0], name=f"tr1_{l}_{h}")
                cp1 = sc.add("vector", lambda e: e.tensor_copy(
                    regA_nat(1), ps_bf[:, 512:1024]), [t1s],
                    name=f"cp1_{l}_{h}")
                bk.wr(5, t1s)
                bk.rd(5, cp1)
                prev_cp1 = cp1

                def mmA(e):
                    e.matmul(ps[2][:, 0:128], nat8[:, 0, :], nat8[:, 2, :],
                             start=True, stop=False)
                    return e.matmul(ps[2][:, 0:128], nat8[:, 1, :],
                                    nat8[:, 3, :], start=False, stop=True)

                def mmB(e):
                    e.matmul(ps[3][:, 0:128], nat8[:, 4, :], nat8[:, 6, :],
                             start=True, stop=False)
                    return e.matmul(ps[3][:, 0:128], nat8[:, 5, :],
                                    nat8[:, 7, :], start=False, stop=True)
                a1 = sc.add("tensor", mmA, [cp0, cp1] + bk.dep_w(2),
                            name=f"A1_{l}_{h}")
                a2 = sc.add("tensor", mmB, [cp1] + bk.dep_w(3),
                            name=f"A2_{l}_{h}")
                bk.wr(2, a1)
                bk.wr(3, a2)

                def ev(e, h=h):
                    e.tensor_copy(sumt[:], ps[2][:, 0:128])
                    return e.scalar_tensor_tensor(
                        sums[:, h, :], sumt[:], bdcol[:, h:h + 1],
                        ps[3][:, 0:128], op0=ALU.mult, op1=ALU.add)
                sdeps = [a1, a2]
                if h == 0 and prev_sum_store[0] is not None:
                    sdeps.append(prev_sum_store[0])
                sev = sc.add("vector", ev, sdeps, name=f"sum_{l}_{h}")
                bk.rd(2, sev)
                bk.rd(3, sev)
                sum_steps.append(sev)
            stsum = sc.dma("sync", "st", o_sum[:, :, l, :],
                           sums.rearrange("p h t -> p (h t)"), sum_steps,
                           name=f"st_sum{l}")
            prev_sum_store[0] = stsum
            regA_readers[:] = sum_steps + [st_k, st_v]
            return xn_steps, ldx

        def regA_nat(half):
            v = nat8.rearrange("p j t -> p (j t)")
            return v[:, half * 512:(half + 1) * 512]

        def moe_phase(hh, xn_all):
            gmm = []
            for band in range(2):
                def g(e, band=band):
                    i = None
                    for kt in range(NDT):
                        i = e.matmul(ps[band][0:8, :], gwb[:, kt, :],
                                     xnb[:, kt, band * BAND:(band + 1) * BAND],
                                     start=(kt == 0), stop=(kt == NDT - 1))
                    return i
                gm = sc.add("tensor", g, xn_all + [ld_gw] + bk.dep_w(band),
                            name=f"gmm{hh}_{band}")
                bk.wr(band, gm)
                gmm.append(gm)

            def lv(e):
                e.tensor_copy(logit[0:8, 0:BAND], ps[0][0:8, :])
                return e.tensor_copy(logit[0:8, BAND:2 * BAND], ps[1][0:8, :])
            lev = sc.add("vector", lv, gmm, name=f"lev{hh}")
            bk.rd(0, lev)
            bk.rd(1, lev)

            comb_done = []
            prev_cev = None
            prev_vt = None
            prev_btr = None
            prev_sexp = None
            for tt in range(8):
                c0 = tt * 128
                ftr = sc.add("tensor", (lambda c0: lambda e: e.transpose(
                    ps[2][0:128, 0:8], logit[0:8, c0:c0 + 128],
                    ident_f[0:8, 0:8]))(c0), [lev, prev_cev] + bk.dep_w(2),
                    name=f"ftr{hh}_{tt}")
                bk.wr(2, ftr)

                smax0 = sc.add("vector", lambda e: e.reduce_max(
                    gcol[:, 0:1], ps[2][0:128, 0:8], axis=AX.X),
                    [ftr, prev_sexp], name=f"vmax0{hh}_{tt}")
                smax = sc.add("vector", lambda e: e.tensor_single_scalar(
                    gcol[:, 1:2], gcol[:, 0:1], -1.0, op=ALU.mult), [smax0],
                    name=f"vmax{hh}_{tt}")
                sexp = sc.add("scalar", lambda e: e.activation(
                    gsA[:], ps[2][0:128, 0:8], AF.Exp, bias=gcol[:, 1:2],
                    accum_out=gcol[:, 2:3]), [smax, prev_vt],
                    name=f"sexp{hh}_{tt}")
                prev_sexp = sexp
                bk.rd(2, sexp)

                v1 = sc.add("vector", lambda e: e.reciprocal(
                    gcol[:, 2:3], gcol[:, 2:3]), [sexp, prev_btr],
                    name=f"vt1{hh}_{tt}")
                v2 = sc.add("vector", lambda e: e.tensor_single_scalar(
                    gsA[:], gsA[:], gcol[:, 2:3], op=ALU.mult), [v1],
                    name=f"vt2{hh}_{tt}")
                v3 = sc.add("vector", lambda e: e.reduce_max(
                    gcol[:, 0:1], gsA[:], axis=AX.X), [v2],
                    name=f"vt3{hh}_{tt}")
                v4 = sc.add("vector", lambda e: e.tensor_single_scalar(
                    gsB[:], gsA[:], gcol[:, 0:1], op=ALU.is_equal), [v3],
                    name=f"vt4{hh}_{tt}")
                v5 = sc.add("vector", lambda e: e.scalar_tensor_tensor(
                    gsB[:], gsB[:], NEG_BIG, gsA[:], op0=ALU.mult,
                    op1=ALU.add), [v4], name=f"vt5{hh}_{tt}")
                v6 = sc.add("vector", lambda e: e.reduce_max(
                    gcol[:, 3:4], gsB[:], axis=AX.X), [v5],
                    name=f"vt6{hh}_{tt}")
                v7 = sc.add("vector", lambda e: e.tensor_single_scalar(
                    gsB[:], gsA[:], gcol[:, 3:4], op=ALU.is_ge), [v6],
                    name=f"vt7{hh}_{tt}")
                vt = sc.add("vector", lambda e: e.tensor_mul(
                    gsB[:], gsB[:], gsA[:]), [v7], name=f"vtop{hh}_{tt}")
                prev_vt = vt
                btr = sc.add("tensor", lambda e: e.transpose(
                    ps[3][0:8, 0:128], gsB[:], ident_f[:]),
                    [vt] + bk.dep_w(3), name=f"btr{hh}_{tt}")
                bk.wr(3, btr)
                prev_btr = btr
                cev = sc.add("vector", (lambda c0: lambda e: e.tensor_copy(
                    combT[0:8, c0:c0 + 128], ps[3][0:8, 0:128]))(c0), [btr],
                    name=f"cev{hh}_{tt}")
                bk.rd(3, cev)
                prev_cev = cev
                comb_done.append(cev)
            st_c = sc.dma("sync", "st",
                          d_comb.rearrange("o (e t) -> (o e) t", e=8),
                          combT[:], comb_done, name=f"stc{hh}")
            prev_cbm = [None]

            x_add_steps = []
            gS_guard = {0: None, 1: None}
            last_dn = [None]
            prev_gev = [None]
            for e_idx in range(9):
                shared = (e_idx == 8)
                if shared:
                    hfs = [list(range(0, 6)), list(range(6, 12)),
                           list(range(12, 18)), list(range(18, 22))]
                    wg_rr = i_swg.rearrange("(k p) h -> p k h", p=128)
                    wu_rr = i_swu.rearrange("(k p) h -> p k h", p=128)
                    wd_rr = i_swd.rearrange("(j p) d -> p j d", p=128)
                else:
                    hfs = [list(range(0, 6)), list(range(6, 11))]
                    wg_rr = i_wg[e_idx].rearrange("(k p) h -> p k h", p=128)
                    wu_rr = i_wu[e_idx].rearrange("(k p) h -> p k h", p=128)
                    wd_rr = i_wd[e_idx].rearrange("(j p) d -> p j d", p=128)

                cb = []
                if not shared:
                    for band in range(2):
                        srcap = d_comb[0:1, e_idx * 1024 + band * BAND:
                                       e_idx * 1024 + (band + 1) * BAND]
                        cp = sc.dma("sync", "ld", zrhsM[0:1, :], srcap,
                                    [st_c, prev_cbm[0]],
                                    name=f"cbc{hh}_{e_idx}_{band}")
                        mm = sc.add("tensor", lambda e: e.matmul(
                            ps[4][:], ones128[:], zrhsM[:], start=True,
                            stop=True), [cp] + bk.dep_w(4),
                            name=f"cbm{hh}_{e_idx}_{band}")
                        prev_cbm[0] = mm
                        bk.wr(4, mm)
                        ev = sc.add("vector", (lambda band: lambda e:
                                    e.tensor_copy(combB[:, band, :], ps[4][:])
                                    )(band), [mm],
                                    name=f"cbe{hh}_{e_idx}_{band}")
                        bk.rd(4, ev)
                        cb.append(ev)

                gS_ready = {}
                prev_ld = {0: wbuf_guard[0], 1: wbuf_guard[1]}
                for hf_i, jl in enumerate(hfs):
                    for jj, j in enumerate(jl):
                        slot = j % 2
                        xdep = regA_readers if (e_idx == 0 and j < 2) else []
                        ldg = sc.dma("gpsimd", "wt", wga[:, slot],
                                     wg_rr[:, :, j * 128:(j + 1) * 128],
                                     [prev_ld[slot]] + xdep,
                                     name=f"lg{hh}_{e_idx}_{j}")
                        ldu = sc.dma("gpsimd", "wt", wub[:, slot],
                                     wu_rr[:, :, j * 128:(j + 1) * 128],
                                     [prev_ld[slot]] + xdep,
                                     name=f"lu{hh}_{e_idx}_{j}")
                        glast = None
                        for band in range(2):
                            bg = band * 2
                            bu = band * 2 + 1

                            def gg(e, slot=slot, bg=bg, band=band):
                                i = None
                                for kt in range(NDT):
                                    i = e.matmul(
                                        ps[bg][:], wga[:, slot, kt, :],
                                        xnb[:, kt,
                                            band * BAND:(band + 1) * BAND],
                                        start=(kt == 0), stop=(kt == NDT - 1))
                                return i

                            def gu(e, slot=slot, bu=bu, band=band):
                                i = None
                                for kt in range(NDT):
                                    i = e.matmul(
                                        ps[bu][:], wub[:, slot, kt, :],
                                        xnb[:, kt,
                                            band * BAND:(band + 1) * BAND],
                                        start=(kt == 0), stop=(kt == NDT - 1))
                                return i
                            depg = [ldg] + xn_all + bk.dep_w(bg)
                            depu = [ldu] + bk.dep_w(bu)
                            g1 = sc.add("tensor", gg, depg,
                                        name=f"g{hh}_{e_idx}_{j}_{band}")
                            bk.wr(bg, g1)
                            g2 = sc.add("tensor", gu, depu,
                                        name=f"u{hh}_{e_idx}_{j}_{band}")
                            bk.wr(bu, g2)
                            glast = g2
                            sil = sc.add("scalar", (lambda bg: lambda e:
                                         e.activation(sgb[:], ps[bg][:],
                                                      AF.Silu))(bg),
                                         [g1, prev_gev[0]],
                                         name=f"sil{hh}_{e_idx}_{j}_{band}")
                            bk.rd(bg, sil)

                            def gev(e, jj=jj, band=band, bu=bu,
                                    shared=shared):
                                e.tensor_mul(gtmp[:], sgb[:], ps[bu][:])
                                if shared:
                                    return e.tensor_copy(
                                        gS[:, jj, band * BAND:
                                           (band + 1) * BAND], gtmp[:])
                                return e.tensor_mul(
                                    gS[:, jj, band * BAND:(band + 1) * BAND],
                                    gtmp[:], combB[:, band, :])
                            gevs = sc.add("vector", gev,
                                          [sil, g2, gS_guard[band]] + cb,
                                          name=f"ge{hh}_{e_idx}_{j}_{band}")
                            bk.rd(bu, gevs)
                            prev_gev[0] = gevs
                            gS_ready[(hf_i, jj, band)] = gevs
                        prev_ld[slot] = glast
                        wbuf_guard[slot] = glast

                    jn = len(jl)
                    prev_wd = {0: None, 1: None}
                    for dt in range(NDT):
                        slot = dt % 2
                        lw = sc.dma(
                            "gpsimd", "wt", wdb[:, slot, 0:jn, :],
                            wd_rr[:, hf_i * 6:hf_i * 6 + jn,
                                  dt * 128:(dt + 1) * 128],
                            [prev_wd[slot]] +
                            (regA_readers if
                             (e_idx == 0 and hf_i == 0 and dt < 2) else []),
                            name=f"lw{hh}_{e_idx}_{hf_i}_{dt}")
                        for band in range(2):
                            bank = band

                            def dn(e, slot=slot, jn=jn, band=band, bank=bank,
                                   hf_i=hf_i):
                                i = None
                                for jj in range(jn):
                                    i = e.matmul(
                                        ps[bank][:], wdb[:, slot, jj, :],
                                        gS[:, jj,
                                           band * BAND:(band + 1) * BAND],
                                        start=(jj == 0), stop=(jj == jn - 1))
                                return i
                            ddeps = [lw] + bk.dep_w(bank) + [
                                gS_ready[(hf_i, jj, band)] for jj in range(jn)]
                            dg = sc.add("tensor", dn, ddeps,
                                        name=f"d{hh}_{e_idx}_{hf_i}_{dt}_{band}")
                            bk.wr(bank, dg)
                            if dt == NDT - 1:
                                gS_guard[band] = dg
                            last_dn[0] = dg

                            def xadd(e, dt=dt, band=band, bank=bank):
                                return e.tensor_add(
                                    xb[:, dt, band * BAND:(band + 1) * BAND],
                                    xb[:, dt, band * BAND:(band + 1) * BAND],
                                    ps[bank][:])
                            xa = sc.add("vector", xadd, [dg],
                                        name=f"xa{hh}_{e_idx}_{hf_i}_{dt}_{band}")
                            bk.rd(bank, xa)
                            x_add_steps.append(xa)
                        prev_wd[slot] = dg
            return x_add_steps

        xn0, ld0 = band_phase(0)
        xn1, ld1 = band_phase(1)
        xa_h0 = moe_phase(0, xn0 + xn1)
        stx0 = sc.dma("sync", "st", o_x1p_r[:, :, 0:1024], xb[:, :, :],
                      xa_h0, name="st_x1p0")
        xb_guard[0] = [stx0]
        xb_guard[1] = [stx0]
        regA_readers[:] = [stx0]
        xn2, ld2 = band_phase(2)
        xn3, ld3 = band_phase(3)
        xa_h1 = moe_phase(1, xn2 + xn3)
        stx1 = sc.dma("sync", "st", o_x1p_r[:, :, 1024:2048], xb[:, :, :],
                      xa_h1, name="st_x1p1")

        fin = sc.add("vector", lambda e: e.engine_nop(),
                     [stx0, stx1, prev_sum_store[0]], name="final")

        sc.emit(nc, sems)

    return nc



def _prep_launch1_inputs(inputs):
    x = np.asarray(inputs["L0_attn_x"], np.float32)
    wqkvu = np.asarray(inputs["L0_attn_W_qkvu"], np.float32)
    wk_t = np.ascontiguousarray(wqkvu[D:2 * D].T)
    wv_t = np.ascontiguousarray(wqkvu[2 * D:3 * D].T)
    slope = np.ascontiguousarray(
        np.asarray(inputs["L0_attn_slope_rate"], np.float32).reshape(1, H0))
    gate_w = np.ascontiguousarray(np.asarray(inputs["L0_ffn_gate_w"],
                                             np.float32))
    wg = np.ascontiguousarray(np.asarray(inputs["L0_ffn_w_gate"], np.float32))
    wu = np.ascontiguousarray(np.asarray(inputs["L0_ffn_w_up"], np.float32))
    wd = np.ascontiguousarray(np.asarray(inputs["L0_ffn_w_down"], np.float32))
    swg = np.ascontiguousarray(np.asarray(inputs["L0_ffn_sw_gate"],
                                          np.float32))
    swu = np.ascontiguousarray(np.asarray(inputs["L0_ffn_sw_up"], np.float32))
    swd = np.ascontiguousarray(np.asarray(inputs["L0_ffn_sw_down"],
                                          np.float32))
    in_maps = []
    for c in range(NCORE):
        b, r = c // 4, c % 4
        bands = [r + 4 * i for i in range(LBANDS)]
        xc = np.concatenate([x[b, g * BAND:(g + 1) * BAND] for g in bands], 0)
        in_maps.append({
            "x_t": np.ascontiguousarray(xc.T),
            "wk_t": wk_t, "wv_t": wv_t, "slope": slope, "gate_w": gate_w,
            "wg": wg, "wu": wu, "wd": wd, "swg": swg, "swu": swu, "swd": swd,
        })
    return in_maps


def run_launch1(inputs):
    nc = build_launch1()
    in_maps = _prep_launch1_inputs(inputs)
    res = run_bass_kernel_spmd(nc, in_maps, core_ids=list(range(NCORE)))
    return res.results


def kernel(**inputs):
    raise NotImplementedError("launches 2/3 under construction")



L2_PARTS = "all"


def build_launch2():
    nc = bass.Bass("TRN2", target_bir_lowering=False, debug=False)

    i_x1p = nc.dram_tensor("x1p_t", [D, LT], F32, kind="ExternalInput").ap()
    i_xn0 = nc.dram_tensor("xn0_t", [D, LT], BF16, kind="ExternalInput").ap()
    i_k0 = nc.dram_tensor("k0_t", [D, LT], BF16, kind="ExternalInput").ap()
    i_v0 = nc.dram_tensor("v0_t", [D, LT], BF16, kind="ExternalInput").ap()
    i_sum = nc.dram_tensor("sum_in", [LBANDS, 4, 128, H0 * 128], F32,
                           kind="ExternalInput").ap()
    i_wq = nc.dram_tensor("wq_t", [D, D], F32, kind="ExternalInput").ap()
    i_wu = nc.dram_tensor("wu_t", [D, D], F32, kind="ExternalInput").ap()
    i_wo = nc.dram_tensor("wo_t", [D, D], F32, kind="ExternalInput").ap()
    i_slope = nc.dram_tensor("slope", [1, H0], F32, kind="ExternalInput").ap()
    i_wsh = nc.dram_tensor("wsh_t", [D, D], F32, kind="ExternalInput").ap()
    i_gate = nc.dram_tensor("gate_c", [128, NDT], F32,
                            kind="ExternalInput").ap()
    i_lng = nc.dram_tensor("lng_c", [128, NDT], F32,
                           kind="ExternalInput").ap()
    i_lnb = nc.dram_tensor("lnb_c", [128, NDT], F32,
                           kind="ExternalInput").ap()
    i_wkey = nc.dram_tensor("wkey_t", [D, 64], F32, kind="ExternalInput").ap()
    i_wval = nc.dram_tensor("wval_t", [D, 64], F32, kind="ExternalInput").ap()
    i_td = nc.dram_tensor("td_c", [64, 1], F32, kind="ExternalInput").ap()
    i_tf = nc.dram_tensor("tf_c", [64, 1], F32, kind="ExternalInput").ap()
    i_anw = nc.dram_tensor("anw_c", [128, NDT], F32,
                           kind="ExternalInput").ap()
    i_qw = nc.dram_tensor("qw_t", [D, D], F32, kind="ExternalInput").ap()
    i_kw = nc.dram_tensor("kw_t", [D, 512], F32, kind="ExternalInput").ap()
    i_vw = nc.dram_tensor("vw_t", [D, 512], F32, kind="ExternalInput").ap()
    i_qn = nc.dram_tensor("qn_c", [128, 1], F32, kind="ExternalInput").ap()
    i_kn = nc.dram_tensor("kn_c", [128, 1], F32, kind="ExternalInput").ap()
    i_bpos = nc.dram_tensor("bpos_c", [128, LBANDS], F32,
                            kind="ExternalInput").ap()

    o_x1 = nc.dram_tensor("x1_o", [D, LT], F32, kind="ExternalOutput").ap()
    o_xn1 = nc.dram_tensor("xn1_o", [D, LT], BF16, kind="ExternalOutput").ap()
    o_q = nc.dram_tensor("q_o", [D, LT], BF16, kind="ExternalOutput").ap()
    o_kf = nc.dram_tensor("kf_o", [512, LT], BF16, kind="ExternalOutput").ap()
    o_vf = nc.dram_tensor("vf_o", [4, LT, 128], BF16,
                          kind="ExternalOutput").ap()
    o_ksq = nc.dram_tensor("ksq_o", [4, LT], F32, kind="ExternalOutput").ap()
    o_sloc = nc.dram_tensor("sloc_o", [64, LT], F32,
                            kind="ExternalOutput").ap()
    o_dbg2 = nc.dram_tensor("dbg2_o", [128, 2048], BF16,
                            kind="ExternalOutput").ap()

    x1p_r = i_x1p.rearrange("(k p) t -> p k t", p=128)
    xn0_r = i_xn0.rearrange("(k p) t -> p k t", p=128)
    k0_r = i_k0.rearrange("(k p) t -> p k t", p=128)
    v0_r = i_v0.rearrange("(k p) t -> p k t", p=128)
    wq_r = i_wq.rearrange("(k p) e -> p k e", p=128)
    wu_r = i_wu.rearrange("(k p) e -> p k e", p=128)
    wo_r = i_wo.rearrange("(k p) e -> p k e", p=128)
    wsh_r = i_wsh.rearrange("(k p) e -> p k e", p=128)
    wq2_r = i_qw.rearrange("(k p) e -> p k e", p=128)
    wk2_r = i_kw.rearrange("(k p) e -> p k e", p=128)
    wv2_r = i_vw.rearrange("(k p) e -> p k e", p=128)
    wkey_r = i_wkey.rearrange("(k p) e -> p k e", p=128)
    wval_r = i_wval.rearrange("(k p) e -> p k e", p=128)
    o_x1_r = o_x1.rearrange("(k p) t -> p k t", p=128)
    o_xn1_r = o_xn1.rearrange("(k p) t -> p k t", p=128)
    o_q_r = o_q.rearrange("(k p) t -> p k t", p=128)
    o_kf_r = o_kf.rearrange("(k p) t -> p k t", p=128)

    sc = Sched()
    bk = Banks()

    with ExitStack() as stk:
        ec = stk.enter_context
        REG = ec(nc.sbuf_tensor("REG", [128, 32768], BF16))
        REGF = ec(nc.sbuf_tensor("REGF", [128, 8192], F32))
        xn1s = ec(nc.sbuf_tensor("xn1s", [128, NDT, BAND], BF16))
        kTb = REG[:, 0:8192].rearrange("p (h t) -> p h t", h=H0)
        vTb = REG[:, 8192:16384].rearrange("p (h t) -> p h t", h=H0)
        qTb = REG[:, 16384:24576].rearrange("p (h t) -> p h t", h=H0)
        oTb = REG[:, 24576:32768].rearrange("p (h t) -> p h t", h=H0)
        xn1c = REG[:, 0:8192].rearrange("p (h t) -> p h t", h=NDT)
        hTb = REG[:, 0:8192].rearrange("p (h t) -> p h t", h=NDT)
        q2b = REG[:, 8192:16384].rearrange("p (h t) -> p h t", h=H0)
        kf4 = REG[:, 16384:18432].rearrange("p (h t) -> p h t", h=4)
        vn4 = REG[:, 18432:20480].rearrange("p (h t) -> p h t", h=H0)
        xb = REGF[:, 0:8192].rearrange("p (k t) -> p k t", k=NDT)
        xsB = REGF[:, 0:8192].rearrange("p (k t) -> p k t", k=NDT)
        KVb = ec(nc.sbuf_tensor("KVb", [128, H0, 128], F32))
        sumS = ec(nc.sbuf_tensor("sumS", [128, 1, H0 * 128], F32))
        wga2 = ec(nc.sbuf_tensor("wga2", [128, 2, NDT, 128], BF16))
        wkv2 = ec(nc.sbuf_tensor("wkv2", [128, 2, NDT, 64], BF16))
        qdecS = ec(nc.sbuf_tensor("qdecS", [128, BLOCK], BF16))
        kdecS = ec(nc.sbuf_tensor("kdecS", [128, BLOCK], BF16))
        diagS = ec(nc.sbuf_tensor("diagS", [128, 2, BLOCK], BF16))
        emask = ec(nc.sbuf_tensor("emask", [128, 2, BLOCK], F32))
        iotaQ = ec(nc.sbuf_tensor("iotaQ", [128, BLOCK], F32))
        qtmp = ec(nc.sbuf_tensor("qtmp", [128, BLOCK], BF16))
        ktmp2 = ec(nc.sbuf_tensor("ktmp2", [128, BLOCK], BF16))
        qkt = ec(nc.sbuf_tensor("qkt", [128, 2, BLOCK], BF16))
        natv = ec(nc.sbuf_tensor("natv", [128, 2, 128], BF16))
        natk = ec(nc.sbuf_tensor("natk", [128, 2, 128], BF16))
        kvbf = ec(nc.sbuf_tensor("kvbf", [128, 128], BF16))
        kvstep = ec(nc.sbuf_tensor("kvstep", [128, 128], F32))
        kvstep_bf = ec(nc.sbuf_tensor("kvstep_bf", [128, 128], BF16))
        osq = ec(nc.sbuf_tensor("osq", [128, BAND], BF16))
        rbS = ec(nc.sbuf_tensor("rbS", [128, BAND], F32))
        xtmp2 = ec(nc.sbuf_tensor("xtmp2", [128, BAND], F32))
        xsq2 = ec(nc.sbuf_tensor("xsq2", [128, 2, BAND], F32))
        zrhs2 = ec(nc.sbuf_tensor("zrhs2", [128, 2, BAND], F32))
        rows2 = ec(nc.sbuf_tensor("rows2", [1, 4 * BAND], F32))
        ones_f2 = ec(nc.sbuf_tensor("ones_f2", [128, 1], F32))
        ones_b2 = ec(nc.sbuf_tensor("ones_b2", [128, 1], BF16))
        ones1282 = ec(nc.sbuf_tensor("ones1282", [128, 128], F32))
        identb2 = ec(nc.sbuf_tensor("identb2", [128, 128], BF16))
        iotapf2 = ec(nc.sbuf_tensor("iotapf2", [128, 128], F32))
        scols2 = ec(nc.sbuf_tensor("scols2", [128, H0], F32))
        nscols2 = ec(nc.sbuf_tensor("nscols2", [128, H0], F32))
        s254 = ec(nc.sbuf_tensor("s254", [128, H0], F32))
        intS = ec(nc.sbuf_tensor("intS", [128, BAND], I32))
        bdcol2 = ec(nc.sbuf_tensor("bdcol2", [128, H0], F32))
        bd2col = ec(nc.sbuf_tensor("bd2col", [128, H0], F32))
        slp2 = ec(nc.sbuf_tensor("slp2", [1, H0], F32))
        ccol2 = ec(nc.sbuf_tensor("ccol2", [128, 8], F32))
        gateC = ec(nc.sbuf_tensor("gateC", [128, NDT], F32))
        gm1C = ec(nc.sbuf_tensor("gm1C", [128, NDT], F32))
        lngC = ec(nc.sbuf_tensor("lngC", [128, NDT], F32))
        lnbC = ec(nc.sbuf_tensor("lnbC", [128, NDT], F32))
        anwC = ec(nc.sbuf_tensor("anwC", [128, NDT], F32))
        qnC = ec(nc.sbuf_tensor("qnC", [128, 1], F32))
        knC = ec(nc.sbuf_tensor("knC", [128, 1], F32))
        tdC = ec(nc.sbuf_tensor("tdC", [64, 1], F32))
        tfC = ec(nc.sbuf_tensor("tfC", [64, 1], F32))
        wcolC = ec(nc.sbuf_tensor("wcolC", [64, 1], F32))
        nuC = ec(nc.sbuf_tensor("nuC", [64, 1], F32))
        bposC = ec(nc.sbuf_tensor("bposC", [128, LBANDS], F32))
        freqC = ec(nc.sbuf_tensor("freqC", [128, 1], F32))
        angT = ec(nc.sbuf_tensor("angT", [128, BAND], F32))
        cosF = ec(nc.sbuf_tensor("cosF", [128, BAND], F32))
        sinF = ec(nc.sbuf_tensor("sinF", [128, BAND], F32))
        tcos = angT
        tsin = ec(nc.sbuf_tensor("tsin", [128, BAND], F32))
        swapP = ec(nc.sbuf_tensor("swapP", [128, 128], BF16))
        sgnC = ec(nc.sbuf_tensor("sgnC", [128, 1], F32))
        sinN = ec(nc.sbuf_tensor("sinN", [128, BAND], F32))
        iotaT = ec(nc.sbuf_tensor("iotaT", [128, BAND], F32))
        xn0bb = ec(nc.sbuf_tensor("xn0bb", [128, NDT, BAND], BF16))
        sigS = ec(nc.sbuf_tensor("sigS", [64, BAND], F32))
        wtile = cosF[0:64, :]
        wkvS = tsin[0:64, :]
        slocB = sigS
        qsqS = osq

        ps = [ec(nc.psum_tensor(f"p2_{i}", [128, BAND], F32)) for i in range(5)]
        ps_bf = ec(nc.psum_tensor("p2_bf", [128, 1024], BF16))
        ps_ra = ec(nc.psum_tensor("p2_ra", [1, BAND], F32))
        ps_rb = ec(nc.psum_tensor("p2_rb", [1, BAND], F32))

        sems = {}
        for n in SEMS:
            sems[n] = ec(nc.semaphore(n))

        ld_misc = []
        ld_misc.append(sc.dma("sync", "ld", slp2[:], i_slope, name="lslp"))
        ld_misc.append(sc.dma("sync", "ld", gateC[:], i_gate, name="lgate"))
        ld_misc.append(sc.dma("sync", "ld", lngC[:], i_lng, name="llng"))
        ld_misc.append(sc.dma("sync", "ld", lnbC[:], i_lnb, name="llnb"))
        ld_misc.append(sc.dma("sync", "ld", anwC[:], i_anw, name="lanw"))
        ld_misc.append(sc.dma("sync", "ld", qnC[:], i_qn, name="lqn"))
        ld_misc.append(sc.dma("sync", "ld", knC[:], i_kn, name="lkn"))
        ld_misc.append(sc.dma("sync", "ld", tdC[:], i_td, name="ltd"))
        ld_misc.append(sc.dma("sync", "ld", tfC[:], i_tf, name="ltf"))
        ld_misc.append(sc.dma("sync", "ld", bposC[:], i_bpos, name="lbp"))
        ld_wkey = sc.dma("gpsimd", "wt", wkv2[:, 0], wkey_r, name="lwkey")
        ld_wval = sc.dma("gpsimd", "wt", wkv2[:, 1], wval_r, name="lwval")

        def fn_iota2(e):
            e.iota(iotapf2[:], pattern=[[-1, 128]], base=0,
                   channel_multiplier=1, allow_small_or_imprecise_dtypes=True)
            e.iota(iotaQ[:], pattern=[[-1, BLOCK]], base=-1,
                   channel_multiplier=0, allow_small_or_imprecise_dtypes=True)
            e.iota(emask[:, 0, :], pattern=[[1, BLOCK]], base=0,
                   channel_multiplier=-1, allow_small_or_imprecise_dtypes=True)
            e.iota(emask[:, 1, :], pattern=[[1, BLOCK]], base=-128,
                   channel_multiplier=-1, allow_small_or_imprecise_dtypes=True)
            e.iota(iotaT[:], pattern=[[1, BAND]], base=0,
                   channel_multiplier=0, allow_small_or_imprecise_dtypes=True)
            return e.iota(freqC[:], pattern=[[0, 1]], base=0,
                          channel_multiplier=1,
                          allow_small_or_imprecise_dtypes=True)
        s_iota2 = sc.add("gpsimd", fn_iota2, name="iota2")

        def fn_con2(e):
            e.memset(ones_f2[:], 1.0)
            e.memset(ones_b2[:], 1.0)
            e.memset(ones1282[:], 1.0)
            e.memset(wtile[:], 1.0)
            e.memset(zrhs2[:], 0.0)
            e.memset(ccol2[:, 0:1], LN_EPS)
            e.memset(ccol2[:, 1:2], RMS_EPS)
            e.memset(ccol2[:, 2:3], -np.pi)
            return e.tensor_single_scalar(identb2[:], iotapf2[:], 0.0,
                                          op=ALU.is_equal)
        s_con2 = sc.add("vector", fn_con2, [s_iota2], name="con2")

        sw1 = sc.add("vector", lambda e: e.tensor_mul(
            angT[:, 0:128], iotapf2[:], iotapf2[:]), [s_con2], name="sw1")
        sw2 = sc.add("vector", lambda e: e.tensor_single_scalar(
            swapP[:], angT[:, 0:128], 4096.0, op=ALU.is_equal), [sw1],
            name="sw2")
        sw3 = sc.add("vector", lambda e: e.tensor_single_scalar(
            sgnC[:], freqC[:], 63.5, op=ALU.is_gt), [s_con2, s_iota2],
            name="sw3")
        s_swp = sc.add("vector", lambda e: e.tensor_scalar(
            sgnC[:], sgnC[:], 2.0, -1.0, op0=ALU.mult, op1=ALU.add),
            [sw3, sw2], name="swp")

        em1 = sc.add("vector", lambda e: e.tensor_single_scalar(
            angT[:, 0:BLOCK], emask[:, 0, :], 0.0, op=ALU.is_lt),
            [s_con2, s_swp], name="em1")
        em2 = sc.add("vector", lambda e: e.scalar_tensor_tensor(
            emask[:, 0, :], angT[:, 0:BLOCK], 1.0e38, emask[:, 0, :],
            op0=ALU.mult, op1=ALU.add), [em1], name="em2")
        em3 = sc.add("vector", lambda e: e.tensor_single_scalar(
            angT[:, 0:BLOCK], emask[:, 1, :], 0.0, op=ALU.is_lt), [em2],
            name="em3")
        s_em = sc.add("vector", lambda e: e.scalar_tensor_tensor(
            emask[:, 1, :], angT[:, 0:BLOCK], 1.0e38, emask[:, 1, :],
            op0=ALU.mult, op1=ALU.add), [em3], name="emask")

        cp_s = sc.add("vector", lambda e: e.tensor_copy(
            zrhs2[0:1, 0, 0:H0], slp2[:]), [s_con2, ld_misc[0]], name="cps")
        mm_s = sc.add("tensor", lambda e: e.matmul(
            ps[0][:, 0:H0], ones1282[:], zrhs2[:, 0, 0:H0], start=True,
            stop=True), [cp_s], name="mms")
        bk.wr(0, mm_s)
        ev_s = sc.add("vector", lambda e: (e.tensor_copy(
            scols2[:], ps[0][:, 0:H0])), [mm_s], name="evs")
        bk.rd(0, ev_s)
        ng_s = sc.add("vector", lambda e: (
            e.tensor_single_scalar(nscols2[:], scols2[:], -1.0, op=ALU.mult),
            e.tensor_single_scalar(s254[:], scols2[:], -256.0, op=ALU.mult)
            )[-1], [ev_s], name="ngs")
        z_s = sc.add("vector", lambda e: e.memset(zrhs2[0:1, 0, 0:H0], 0.0),
                     [mm_s], name="zs")

        def fn_tab2(e):
            e.activation(bdcol2[:], scols2[:], AF.Exp, scale=-256.0)
            e.activation(bd2col[:], scols2[:], AF.Exp, scale=-512.0)
            e.activation(wcolC[:], tdC[:], AF.Exp)
            return e.activation(nuC[:], tfC[:], AF.Exp)
        s_tab2a = sc.add("scalar", fn_tab2, [ev_s, ld_misc[7], ld_misc[8],
                                             s_con2], name="tab2a")
        s_tab2 = sc.add("scalar", lambda e: e.activation(
            wtile[:], wtile[:], AF.Copy, scale=wcolC[:]), [s_tab2a],
            name="tab2")
        nu_s = sc.add("vector", lambda e: e.tensor_single_scalar(
            nuC[:], nuC[:], -1.0, op=ALU.mult), [s_tab2], name="nus")
        sgate = sc.add("scalar", lambda e: e.activation(
            gateC[:], gateC[:], AF.Sigmoid), [ld_misc[1], s_con2],
            name="sgate")
        gm1_s = sc.add("vector", lambda e: e.scalar_tensor_tensor(
            gm1C[:], gateC[:], -1.0, ones1282[:, 0:NDT], op0=ALU.mult,
            op1=ALU.add), [sgate], name="gm1")
        fr0 = sc.add("vector", lambda e: e.tensor_single_scalar(
            xtmp2[:, 0:1], freqC[:], 63.5, op=ALU.is_gt),
            [s_iota2, s_con2, s_swp], name="fr0")
        fr1 = sc.add("vector", lambda e: e.scalar_tensor_tensor(
            freqC[:], xtmp2[:, 0:1], -64.0, freqC[:], op0=ALU.mult,
            op1=ALU.add), [fr0], name="fr1")
        fr2 = sc.add("scalar", lambda e: e.activation(
            freqC[:], freqC[:], AF.Exp,
            scale=float(-2.0 * np.log(1.0e6) / 128.0)), [fr1], name="fr2")

        prev_band = {}
        x1_stores = []
        xn1_steps_all = []
        prefix_last = [s_con2]
        kv_zero = sc.add("vector", lambda e: e.memset(
            KVb.rearrange("p h t -> p (h t)"), 0.0), [], name="kvz")
        ldxn0 = {}

        def xn0b_t(kt, l):
            return xn0bb[:, kt, :]

        for l in range(LBANDS):
            cols = slice(l * BAND, (l + 1) * BAND)
            g0 = [prev_band.get("st_x1")]
            ldk = sc.dma("sync", "ld", kTb[:].rearrange("p h t -> p (h t)"),
                         k0_r[:, :, cols], [prev_band.get("okq")],
                         name=f"ldk{l}")
            ldv = sc.dma("sync", "ld", vTb[:].rearrange("p h t -> p (h t)"),
                         v0_r[:, :, cols], [prev_band.get("okq")],
                         name=f"ldv{l}")
            ldx = sc.dma("sync", "ld", xb[:], x1p_r[:, :, cols], g0,
                         name=f"ldx2_{l}")
            ldxn0[l] = sc.dma("sync", "ld", xn0bb[:], xn0_r[:, :, cols],
                              [prev_band.get("xn0_guard")], name=f"ldxn0_{l}")
            ldsum = []
            pref_steps = []
            pref_by_h = {}
            for k2 in range(4):
                sdeps = [prefix_last[0]]
                if len(ldsum) >= 1:
                    sdeps = [pref_steps[-1]]
                s_ = sc.dma("sync", "ld", sumS[:, 0],
                            i_sum[l, k2].rearrange("p e -> p e"),
                            sdeps, name=f"ldsum{l}_{k2}")
                ldsum.append(s_)
                for h in range(H0):
                    def pf(e, k2=k2, h=h):
                        return e.scalar_tensor_tensor(
                            KVb[:, h, :], KVb[:, h, :], bd2col[:, h:h + 1],
                            sumS[:, 0, h * 128:(h + 1) * 128],
                            op0=ALU.mult, op1=ALU.add)
                    deps = [s_, s_tab2, pref_by_h.get(h)]
                    if l == 0 and k2 == 0:
                        deps.append(kv_zero)
                    stp = sc.add("vector", pf, deps, name=f"pf{l}_{k2}_{h}")
                    pref_steps.append(stp)
                    pref_by_h[h] = stp
            prefix_last[0] = pref_steps[-1]
            proj_ev = {}
            prev_ldw = {0: prev_band.get("wga0"), 1: prev_band.get("wga1")}
            for j in range(NDT):
                slot = j % 2
                ldw = sc.dma("gpsimd", "wt", wga2[:, slot],
                             wq_r[:, :, j * 128:(j + 1) * 128],
                             [prev_ldw[slot]], name=f"lw2_{l}_{j}")

                def grp(e, slot=slot):
                    i = None
                    for kt in range(NDT):
                        i = e.matmul(ps[slot][:], wga2[:, slot, kt, :],
                                     xn0b_t(kt, l), start=(kt == 0),
                                     stop=(kt == NDT - 1))
                    return i
                gdeps = [ldw, ldxn0[l]] + bk.dep_w(slot)
                g = sc.add("tensor", grp, gdeps, name=f"qup{l}_{j}")
                bk.wr(slot, g)
                prev_ldw[slot] = g
                ev = sc.add("scalar", (lambda j, slot: lambda e:
                            e.activation(qTb[:, j, :], ps[slot][:],
                                         AF.Silu))(j, slot), [g],
                            name=f"qev{l}_{j}")
                bk.rd(slot, ev)
                proj_ev[j] = ev
            prev_band["wga0"] = prev_ldw[0]
            prev_band["wga1"] = prev_ldw[1]
            prev_band["xn0_guard"] = prev_ldw[1]

            o_ev = []
            for h in range(H0):
                hd = [proj_ev[h], ldk, ldv]
                tb = sc.add("scalar", (lambda h: lambda e: (
                    e.activation(qdecS[:], iotaQ[:], AF.Exp,
                                 scale=scols2[:, h:h + 1]),
                    e.activation(kdecS[:], iotaQ[:], AF.Exp,
                                 scale=nscols2[:, h:h + 1],
                                 bias=s254[:, h:h + 1]),
                    e.activation(diagS[:, 0, :], emask[:, 0, :], AF.Exp,
                                 scale=nscols2[:, h:h + 1]),
                    e.activation(diagS[:, 1, :], emask[:, 1, :], AF.Exp,
                                 scale=nscols2[:, h:h + 1]))[-1])(h),
                    [s_em, ng_s] + ([o_ev[-1]] if o_ev else
                                    [prev_band.get("okq")]),
                    name=f"tb{l}_{h}")
                kvc = sc.add("vector", (lambda h: lambda e: e.tensor_copy(
                    kvbf[:], KVb[:, h, :]))(h),
                    [pref_steps[3 * H0 + h]], name=f"kvc{l}_{h}")
                blk_o = []
                for blk in range(2):
                    bc = slice(blk * BLOCK, (blk + 1) * BLOCK)
                    lhs_kv = kvbf if blk == 0 else kvstep_bf
                    kv_dep = kvc if blk == 0 else blk_o[-1]
                    qm = sc.add("vector", (lambda h, bc: lambda e:
                                e.tensor_mul(qtmp[:], qTb[:, h, bc],
                                             qdecS[:]))(h, bc), [tb, kv_dep],
                                name=f"qm{l}_{h}_{blk}")
                    ond = sc.add("tensor", (lambda lhs_kv, h, bc: lambda e:
                                 e.matmul(ps[2][:, 0:BLOCK], lhs_kv[:],
                                          qtmp[:], start=True, stop=False))(
                                     lhs_kv, h, bc),
                                 [qm, kv_dep] + bk.dep_w(2),
                                 name=f"ond{l}_{h}_{blk}")
                    bk.wr(2, ond)
                    qka = sc.add("tensor", (lambda h, bc: lambda e: e.matmul(
                        ps[3][:, 0:BLOCK], kTb[:, h, bc][:, 0:128],
                        qTb[:, h, bc], start=True, stop=True))(h, bc),
                        [hd[0], ldk] + bk.dep_w(3), name=f"qka{l}_{h}_{blk}")
                    bk.wr(3, qka)
                    qkb = sc.add("tensor", (lambda h, bc: lambda e: e.matmul(
                        ps[4][:, 0:BLOCK], kTb[:, h, bc][:, 128:256],
                        qTb[:, h, bc], start=True, stop=True))(h, bc),
                        [hd[0], ldk] + bk.dep_w(4), name=f"qkb{l}_{h}_{blk}")
                    bk.wr(4, qkb)
                    dm0 = sc.add("vector", lambda e: e.tensor_mul(
                        qkt[:, 0, :], ps[3][:, 0:BLOCK], diagS[:, 0, :]),
                        [qka, tb], name=f"dm0_{l}_{h}_{blk}")
                    bk.rd(3, dm0)
                    dm1 = sc.add("vector", lambda e: e.tensor_mul(
                        qkt[:, 1, :], ps[4][:, 0:BLOCK], diagS[:, 1, :]),
                        [qkb, tb], name=f"dm1_{l}_{h}_{blk}")
                    bk.rd(4, dm1)
                    tv = sc.add("tensor", (lambda h, bc: lambda e: (
                        e.transpose(ps_bf[:, 0:128],
                                    vTb[:, h, bc][:, 0:128], identb2[:]),
                        e.transpose(ps_bf[:, 128:256],
                                    vTb[:, h, bc][:, 128:256], identb2[:])
                        )[-1])(h, bc), [ldv] + bk.dep_w(5),
                        name=f"tv{l}_{h}_{blk}")
                    bk.wr(5, tv)
                    ev_v = sc.add("vector", lambda e: e.tensor_copy(
                        natv.rearrange("p j t -> p (j t)"),
                        ps_bf[:, 0:256]), [tv], name=f"evv{l}_{h}_{blk}")
                    bk.rd(5, ev_v)
                    od = sc.add("tensor", lambda e: (
                        e.matmul(ps[2][:, 0:BLOCK], natv[:, 0, :],
                                 qkt[:, 0, :], start=False, stop=False),
                        e.matmul(ps[2][:, 0:BLOCK], natv[:, 1, :],
                                 qkt[:, 1, :], start=False, stop=True))[-1],
                        [dm0, dm1, ev_v, ond], name=f"od{l}_{h}_{blk}")
                    bk.wr(2, od)
                    oev = sc.add("vector", (lambda h, bc: lambda e:
                                 e.tensor_copy(oTb[:, h, bc],
                                               ps[2][:, 0:BLOCK]))(h, bc),
                                 [od], name=f"oev{l}_{h}_{blk}")
                    bk.rd(2, oev)
                    o_ev.append(oev)
                    if blk == 0:
                        km = sc.add("vector", (lambda h, bc: lambda e:
                                    e.tensor_mul(ktmp2[:], kTb[:, h, bc],
                                                 kdecS[:]))(h, bc),
                                    [tb, ldk], name=f"km2_{l}_{h}")
                        tk = sc.add("tensor", lambda e: (
                            e.transpose(ps_bf[:, 256:384], ktmp2[:, 0:128],
                                        identb2[:]),
                            e.transpose(ps_bf[:, 384:512], ktmp2[:, 128:256],
                                        identb2[:]))[-1],
                            [km, ev_v] + bk.dep_w(5), name=f"tk{l}_{h}")
                        bk.wr(5, tk)
                        ev_k = sc.add("vector", lambda e: e.tensor_copy(
                            natk.rearrange("p j t -> p (j t)"),
                            ps_bf[:, 256:512]), [tk], name=f"evk{l}_{h}")
                        bk.rd(5, ev_k)
                        am = sc.add("tensor", lambda e: (
                            e.matmul(ps[0][:, 0:128], natk[:, 0, :],
                                     natv[:, 0, :], start=True, stop=False),
                            e.matmul(ps[0][:, 0:128], natk[:, 1, :],
                                     natv[:, 1, :], start=False, stop=True)
                            )[-1], [ev_k, ev_v] + bk.dep_w(0),
                            name=f"am{l}_{h}")
                        bk.wr(0, am)
                        ks1 = sc.add("vector", (lambda h: lambda e:
                                     e.scalar_tensor_tensor(
                                         kvstep[:], KVb[:, h, :],
                                         bdcol2[:, h:h + 1], ps[0][:, 0:128],
                                         op0=ALU.mult, op1=ALU.add))(h),
                                     [am, kvc], name=f"ks1_{l}_{h}")
                        bk.rd(0, ks1)
                        ks2 = sc.add("vector", lambda e: e.tensor_copy(
                            kvstep_bf[:], kvstep[:]), [ks1],
                            name=f"ks2_{l}_{h}")
                        blk_o.append(ks2)
                    else:
                        blk_o.append(oev)

            osum_mm = []
            for j in range(H0):
                sq = sc.add("scalar", (lambda j: lambda e: e.square(
                    osq[:], oTb[:, j, :]))(j),
                    [o_ev[2 * j + 1], osum_mm[-1] if osum_mm else None],
                    name=f"osq{l}_{j}")
                m = sc.add("tensor", (lambda j: lambda e: e.matmul(
                    ps_ra[:], ones_b2[:], osq[:], start=(j == 0),
                    stop=(j == H0 - 1)))(j), [sq] + (bk.dep_w(6) if j == 0
                                                     else []),
                    name=f"osmm{l}_{j}")
                osum_mm.append(m)
            bk.wr(6, osum_mm[-1])
            roA = sc.add("scalar", lambda e: e.activation(
                rows2[0:1, 0:BAND], ps_ra[:], AF.Ln,
                scale=1.0 / D, bias=ccol2[0:1, 1:2]), [osum_mm[-1]],
                name=f"roA{l}")
            ro = sc.add("scalar", lambda e: e.activation(
                rows2[0:1, 0:BAND], rows2[0:1, 0:BAND],
                AF.Exp, scale=-0.5), [roA], name=f"ro{l}")
            bk.rd(6, ro)
            zc = sc.add("vector", lambda e: e.tensor_copy(
                zrhs2[0:1, 0, :], rows2[0:1, 0:BAND]), [ro], name=f"zc{l}")
            rb_mm = sc.add("tensor", lambda e: e.matmul(
                ps[3][:], ones1282[:], zrhs2[:, 0, :], start=True,
                stop=True), [zc] + bk.dep_w(3), name=f"rbm{l}")
            bk.wr(3, rb_mm)
            rb_ev = sc.add("vector", lambda e: e.tensor_copy(rbS[:], ps[3][:]),
                           [rb_mm], name=f"rbe{l}")
            bk.rd(3, rb_ev)
            z_steps = []
            for j in range(H0):
                slot = j % 2
                ldu = sc.dma("gpsimd", "wt", wga2[:, slot],
                             wu_r[:, :, j * 128:(j + 1) * 128],
                             [prev_band["wga0"] if slot == 0 else
                              prev_band["wga1"]] if j < 2 else
                             [z_steps[j - 2]], name=f"lu2_{l}_{j}")

                def ugrp(e, slot=slot):
                    i = None
                    for kt in range(NDT):
                        i = e.matmul(ps[slot][:], wga2[:, slot, kt, :],
                                     xn0b_t(kt, l), start=(kt == 0),
                                     stop=(kt == NDT - 1))
                    return i
                ug = sc.add("tensor", ugrp,
                            [ldu, ldxn0[l]] + bk.dep_w(slot),
                            name=f"ug{l}_{j}")
                bk.wr(slot, ug)
                z_ = sc.add("vector", (lambda j, slot: lambda e:
                            e.tensor_mul(oTb[:, j, :], oTb[:, j, :],
                                         ps[slot][:]))(j, slot),
                            [osum_mm[j], ug], name=f"zmul{l}_{j}")
                bk.rd(slot, z_)
                z_steps.append(z_)
            prev_band["wga0"] = ug if H0 % 2 == 1 else z_steps[-2]
            prev_band["wga1"] = z_steps[-1]
            prev_band["xn0_guard"] = ug
            xadds = []
            wog = []
            for j in range(NDT):
                slot = j % 2
                ldw = sc.dma("gpsimd", "wt", wga2[:, slot],
                             wo_r[:, :, j * 128:(j + 1) * 128],
                             [prev_band["wga0"] if slot == 0 else
                              prev_band["wga1"]] if j < 2 else
                             [wog[j - 2]],
                             name=f"lwo{l}_{j}")

                def grp(e, slot=slot):
                    i = None
                    for kt in range(NDT):
                        i = e.matmul(ps[slot][:], wga2[:, slot, kt, :],
                                     oTb[:, kt, :], start=(kt == 0),
                                     stop=(kt == NDT - 1))
                    return i
                g = sc.add("tensor", grp,
                           [ldw] + z_steps + bk.dep_w(slot),
                           name=f"wog{l}_{j}")
                bk.wr(slot, g)
                wog.append(g)
                xa1 = sc.add("vector", (lambda j, slot: lambda e:
                             e.tensor_mul(xtmp2[:], ps[slot][:], rbS[:])
                             )(j, slot), [g, rb_ev, ldx], name=f"xm{l}_{j}")
                bk.rd(slot, xa1)
                xa2 = sc.add("vector", (lambda j: lambda e: e.tensor_add(
                    xb[:, j, :], xb[:, j, :], xtmp2[:]))(j), [xa1],
                    name=f"xa2_{l}_{j}")
                xadds.append(xa1)
                xadds.append(xa2)
            prev_band["wga0"] = wog[-2]
            prev_band["wga1"] = wog[-1]
            st_x1 = sc.dma("sync", "st", o_x1_r[:, :, cols], xb[:],
                           xadds, name=f"stx1_{l}")
            prev_band["st_x1"] = st_x1
            x1_stores.append(st_x1)

            sq2 = []
            mm2b = []
            for kt in range(NDT):
                deps = [xadds[2 * kt + 1]]
                if kt >= 2:
                    deps.append(mm2b[kt - 2])
                elif prev_band.get("lnsq") is not None:
                    deps.append(prev_band["lnsq"][kt])
                s_ = sc.add("scalar", (lambda kt: lambda e: e.square(
                    xsq2[:, kt % 2, :], xb[:, kt, :]))(kt), deps,
                    name=f"sq2_{l}_{kt}")
                sq2.append(s_)
                m_ = sc.add("tensor", (lambda kt: lambda e: e.matmul(
                    ps_rb[:], ones_f2[:], xsq2[:, kt % 2, :],
                    start=(kt == 0), stop=(kt == NDT - 1)))(kt),
                    [s_] + (bk.dep_w(7) if kt == 0 else []),
                    name=f"smm2_{l}_{kt}")
                mm2b.append(m_)
            prev_band["lnsq"] = [mm2b[-2], mm2b[-1]]
            bk.wr(7, mm2b[-1])

            def mk_sx2(e):
                i = None
                for kt in range(NDT):
                    i = e.matmul(ps_ra[:], ones_f2[:], xb[:, kt, :],
                                 start=(kt == 0), stop=(kt == NDT - 1))
                return i
            mm1b = sc.add("tensor", mk_sx2, xadds + bk.dep_w(6),
                          name=f"sx2_{l}")
            bk.wr(6, mm1b)
            mrow = rows2[0:1, 0:BAND]
            vrow = rows2[0:1, BAND:2 * BAND]
            rrow = rows2[0:1, 2 * BAND:3 * BAND]
            mrr = rows2[0:1, 3 * BAND:4 * BAND]

            def st_stats2(e):
                e.tensor_single_scalar(mrow, ps_ra[:], 1.0 / D, op=ALU.mult)
                e.tensor_single_scalar(vrow, ps_rb[:], 1.0 / D, op=ALU.mult)
                e.tensor_mul(mrr, mrow, mrow)
                return e.tensor_sub(vrow, vrow, mrr)
            s_st2 = sc.add("vector", st_stats2,
                           [mm1b, mm2b[-1], prev_band.get("rstd")],
                           name=f"st2_{l}")
            bk.rd(6, s_st2)
            bk.rd(7, s_st2)
            s_rstd2A = sc.add("scalar", lambda e: e.activation(
                rrow, vrow, AF.Ln, bias=ccol2[0:1, 0:1]), [s_st2],
                name=f"rstd2A_{l}")
            s_rstd2 = sc.add("scalar", lambda e: e.activation(
                rrow, rrow, AF.Exp, scale=-0.5), [s_rstd2A],
                name=f"rstd2_{l}")
            prev_band["rstd"] = s_rstd2
            s_rows2 = sc.add("vector", lambda e: (
                e.scalar_tensor_tensor(mrr, mrow, -1.0, rrow, op0=ALU.mult,
                                       op1=ALU.mult),
                e.tensor_copy(zrhs2[0:1, 0, :], rrow),
                e.tensor_copy(zrhs2[0:1, 1, :], mrr))[-1],
                [s_rstd2, prev_band.get("lnbc")], name=f"rows2_{l}")
            s_bc2 = sc.add("tensor", lambda e: (
                e.matmul(ps[3][:], ones1282[:], zrhs2[:, 0, :], start=True,
                         stop=True),
                e.matmul(ps[4][:], ones1282[:], zrhs2[:, 1, :], start=True,
                         stop=True))[-1],
                [s_rows2] + bk.dep_w(3) + bk.dep_w(4), name=f"bc2_{l}")
            bk.wr(3, s_bc2)
            bk.wr(4, s_bc2)
            prev_band["lnbc"] = s_bc2
            xn1_steps = []
            for kt in range(NDT):
                def fn(e, kt=kt):
                    e.tensor_mul(xtmp2[:], xb[:, kt, :], ps[3][:])
                    return e.tensor_add(xn1s[:, kt, :], xtmp2[:],
                                        ps[4][:])
                xn1_steps.append(sc.add("vector", fn,
                                        [s_bc2, xadds[2 * kt + 1],
                                         prev_band.get("xn1s_rd")],
                                        name=f"xn1_{l}_{kt}"))
            bk.rd(3, xn1_steps[-1])
            bk.rd(4, xn1_steps[-1])
            stxn1 = sc.dma("sync", "st", o_xn1_r[:, :, cols],
                           xn1s[:], xn1_steps, name=f"stxn1_{l}")
            prev_band["xn1s_rd"] = stxn1
            prev_band["stxn1"] = stxn1
            xn1_steps_all.append(stxn1)
            prev_band["okq"] = o_ev[-1]

        if L2_PARTS == "bands":
            fin0 = sc.add("vector", lambda e: e.engine_nop(),
                          x1_stores + [prev_band["stxn1"], prev_band["okq"]],
                          name="fin0")
            sc.emit(nc, sems)
            return nc
        xnS_guard = [None]
        prev_sh = {0: prev_band["wga0"], 1: prev_band["wga1"]}
        rw_done = []
        rw_regA_guard = [prev_band["okq"], prev_band.get("stxn1")]
        for l in range(LBANDS):
            cols = slice(l * BAND, (l + 1) * BAND)
            catg = (l + 2) % 4
            ldc = sc.dma("sync", "ld", REG[:, 0:8192],
                         o_xn1_r[:, :, catg * BAND:(catg + 1) * BAND],
                         rw_regA_guard + [xn1_steps_all[catg]],
                         name=f"ldxc{l}")
            lds = sc.dma("sync", "ld", xn1s[:],
                         o_xn1_r[:, :, cols],
                         [xn1_steps_all[l], prev_band.get("xn1s_rd")],
                         name=f"ldxs{l}")
            prev_band["xn1s_rd"] = None
            xs_steps = []
            shg = []
            for j in range(NDT):
                slot = j % 2
                ldw = sc.dma("gpsimd", "wt", wga2[:, slot],
                             wsh_r[:, :, j * 128:(j + 1) * 128],
                             [prev_sh[slot]], name=f"lsh{l}_{j}")

                def grp(e, slot=slot):
                    i = None
                    for kt in range(NDT):
                        i = e.matmul(ps[slot][:], wga2[:, slot, kt, :],
                                     xn1c[:, kt, :], start=(kt == 0),
                                     stop=(kt == NDT - 1))
                    return i
                g = sc.add("tensor", grp,
                           [ldw, ldc, lds] + bk.dep_w(slot),
                           name=f"shg{l}_{j}")
                bk.wr(slot, g)
                shg.append(g)
                prev_sh[slot] = g

                def ev1(e, j=j, slot=slot):
                    return e.tensor_single_scalar(
                        xtmp2[:], ps[slot][:], gateC[:, j:j + 1], op=ALU.mult)
                e1 = sc.add("vector", ev1, [g, sgate,
                                            xs_steps[-1] if xs_steps else
                                            xnS_guard[0],
                                            prev_band.get("st_x1")],
                            name=f"xse1_{l}_{j}")
                bk.rd(slot, e1)

                def ev2(e, j=j):
                    return e.scalar_tensor_tensor(
                        xsB[:, j, :], xn1s[:, j, :], gm1C[:, j:j + 1],
                        xtmp2[:], op0=ALU.mult, op1=ALU.add)
                e2 = sc.add("vector", ev2, [e1, gm1_s], name=f"xse2_{l}_{j}")
                xs_steps.append(e2)
            sqr = []
            mmr = []
            for kt in range(NDT):
                deps = [xs_steps[kt]]
                if kt >= 2:
                    deps.append(mmr[kt - 2])
                elif prev_band.get("rwsq") is not None:
                    deps.append(prev_band["rwsq"][kt])
                s_ = sc.add("scalar", (lambda kt: lambda e: e.square(
                    xsq2[:, kt % 2, :], xsB[:, kt, :]))(kt), deps,
                    name=f"rsq{l}_{kt}")
                sqr.append(s_)
                m_ = sc.add("tensor", (lambda kt: lambda e: e.matmul(
                    ps_rb[:], ones_f2[:], xsq2[:, kt % 2, :],
                    start=(kt == 0), stop=(kt == NDT - 1)))(kt),
                    [s_] + (bk.dep_w(7) if kt == 0 else []),
                    name=f"rsmm{l}_{kt}")
                mmr.append(m_)
            prev_band["rwsq"] = [mmr[-2], mmr[-1]]
            bk.wr(7, mmr[-1])

            def mk_sxr(e):
                i = None
                for kt in range(NDT):
                    i = e.matmul(ps_ra[:], ones_f2[:], xsB[:, kt, :],
                                 start=(kt == 0), stop=(kt == NDT - 1))
                return i
            mm1r = sc.add("tensor", mk_sxr, xs_steps + bk.dep_w(6),
                          name=f"rsx{l}")
            bk.wr(6, mm1r)
            mrow = rows2[0:1, 0:BAND]
            vrow = rows2[0:1, BAND:2 * BAND]
            rrow = rows2[0:1, 2 * BAND:3 * BAND]
            mrr = rows2[0:1, 3 * BAND:4 * BAND]
            s_str = sc.add("vector", lambda e: (
                e.tensor_single_scalar(mrow, ps_ra[:], 1.0 / D, op=ALU.mult),
                e.tensor_single_scalar(vrow, ps_rb[:], 1.0 / D, op=ALU.mult),
                e.tensor_mul(mrr, mrow, mrow),
                e.tensor_sub(vrow, vrow, mrr))[-1],
                [mm1r, mmr[-1], prev_band.get("rstd")], name=f"rst{l}")
            bk.rd(6, s_str)
            bk.rd(7, s_str)
            s_rstdrA = sc.add("scalar", lambda e: e.activation(
                rrow, vrow, AF.Ln, bias=ccol2[0:1, 0:1]), [s_str],
                name=f"rrstdA{l}")
            s_rstdr = sc.add("scalar", lambda e: e.activation(
                rrow, rrow, AF.Exp, scale=-0.5), [s_rstdrA],
                name=f"rrstd{l}")
            prev_band["rstd"] = s_rstdr
            s_rowsr = sc.add("vector", lambda e: (
                e.scalar_tensor_tensor(mrr, mrow, -1.0, rrow, op0=ALU.mult,
                                       op1=ALU.mult),
                e.tensor_copy(zrhs2[0:1, 0, :], rrow),
                e.tensor_copy(zrhs2[0:1, 1, :], mrr))[-1],
                [s_rstdr, prev_band.get("lnbc")], name=f"rrows{l}")
            s_bcr = sc.add("tensor", lambda e: (
                e.matmul(ps[3][:], ones1282[:], zrhs2[:, 0, :], start=True,
                         stop=True),
                e.matmul(ps[4][:], ones1282[:], zrhs2[:, 1, :], start=True,
                         stop=True))[-1],
                [s_rowsr] + bk.dep_w(3) + bk.dep_w(4), name=f"rbc{l}")
            bk.wr(3, s_bcr)
            bk.wr(4, s_bcr)
            prev_band["lnbc"] = s_bcr
            xns_steps = []
            for kt in range(NDT):
                def fn(e, kt=kt):
                    e.tensor_mul(xtmp2[:], xsB[:, kt, :], ps[3][:])
                    e.tensor_add(xtmp2[:], xtmp2[:], ps[4][:])
                    return e.tensor_scalar(xn0bb[:, kt, :], xtmp2[:],
                                           lngC[:, kt:kt + 1],
                                           lnbC[:, kt:kt + 1],
                                           op0=ALU.mult, op1=ALU.add)
                xns_steps.append(sc.add("vector", fn,
                                        [s_bcr, xs_steps[kt],
                                         ldxn0[3] if l == 0 else None,
                                         ld_misc[2], ld_misc[3]],
                                        name=f"xns{l}_{kt}"))
            bk.rd(3, xns_steps[-1])
            bk.rd(4, xns_steps[-1])
            xnS_guard[0] = xns_steps[-1]
            def kgrp(e):
                i = None
                for kt in range(NDT):
                    i = e.matmul(ps[0][0:64, :], wkv2[:, 0, kt, :],
                                 xn0bb[:, kt, :], start=(kt == 0),
                                 stop=(kt == NDT - 1))
                return i
            kg = sc.add("tensor", kgrp,
                        [ld_wkey] + xns_steps + bk.dep_w(0), name=f"rkg{l}")
            bk.wr(0, kg)

            def vgrp(e):
                i = None
                for kt in range(NDT):
                    i = e.matmul(ps[1][0:64, :], wkv2[:, 1, kt, :],
                                 xn0bb[:, kt, :], start=(kt == 0),
                                 stop=(kt == NDT - 1))
                return i
            vg = sc.add("tensor", vgrp,
                        [ld_wval] + xns_steps + bk.dep_w(1), name=f"rvg{l}")
            bk.wr(1, vg)
            sg1 = sc.add("scalar", lambda e: e.activation(
                sigS[:], ps[0][0:64, :], AF.Sigmoid), [kg, rw_done[-1] if
                                                       rw_done else None],
                name=f"rsig{l}")
            bk.rd(0, sg1)
            sg2 = sc.add("scalar", lambda e: e.activation(
                sigS[:], sigS[:], AF.Exp, scale=nuC[:]), [sg1, nu_s],
                name=f"rexp{l}")
            wk1 = sc.add("vector", lambda e: e.tensor_mul(
                wkvS[:], sigS[:], ps[1][0:64, :]), [sg2, vg], name=f"rwkv{l}")
            bk.rd(1, wk1)
            scn = sc.add("vector", lambda e: e.tensor_tensor_scan(
                slocB[:], wtile[:], wkvS[:], 0.0, op0=ALU.mult, op1=ALU.add),
                [wk1, s_tab2], name=f"rscan{l}")
            std = sc.dma("sync", "st", o_sloc[:, cols], slocB[:], [scn],
                         name=f"stsloc{l}")
            rw_done.append(std)
            rw_regA_guard = [shg[-1], shg[-2]]

        if L2_PARTS == "bands+rwkv":
            fin0 = sc.add("vector", lambda e: e.engine_nop(),
                          x1_stores + rw_done, name="fin0")
            sc.emit(nc, sems)
            return nc
        prev_l1 = {0: prev_sh[0], 1: prev_sh[1]}
        hswap = [rw_regA_guard[0], kg, vg, wk1]
        for l in range(LBANDS):
            cols = slice(l * BAND, (l + 1) * BAND)
            lds1 = sc.dma("sync", "ld", xn1s[:], o_xn1_r[:, :, cols],
                          [xn1_steps_all[l], prev_band.get("l1xn_rd")],
                          name=f"l1xs{l}")
            sqh = []
            mmh = []
            for kt in range(NDT):
                deps = [lds1]
                if kt >= 2:
                    deps.append(mmh[kt - 2])
                elif prev_band.get("l1sq") is not None:
                    deps.append(prev_band["l1sq"][kt])
                s_ = sc.add("scalar", (lambda kt: lambda e: e.square(
                    qsqS[:], xn1s[:, kt, :]))(kt),
                    deps + [mmh[-1] if mmh else prev_band.get("mmk")],
                    name=f"hsq{l}_{kt}")
                sqh.append(s_)
                m_ = sc.add("tensor", (lambda kt: lambda e: e.matmul(
                    ps_ra[:], ones_b2[:], qsqS[:], start=(kt == 0),
                    stop=(kt == NDT - 1)))(kt),
                    [s_] + (bk.dep_w(6) if kt == 0 else []),
                    name=f"hmm{l}_{kt}")
                mmh.append(m_)
            prev_band["l1sq"] = [mmh[-2], mmh[-1]]
            bk.wr(6, mmh[-1])
            rhA = sc.add("scalar", lambda e: e.activation(
                rows2[0:1, 0:BAND], ps_ra[:], AF.Ln,
                scale=1.0 / D, bias=ccol2[0:1, 1:2]), [mmh[-1]],
                name=f"rhA{l}")
            rh = sc.add("scalar", lambda e: e.activation(
                rows2[0:1, 0:BAND], rows2[0:1, 0:BAND],
                AF.Exp, scale=-0.5), [rhA], name=f"rh{l}")
            bk.rd(6, rh)
            zch = sc.add("vector", lambda e: e.tensor_copy(
                zrhs2[0:1, 0, :], rows2[0:1, 0:BAND]),
                [rh, prev_band.get("lnbc")], name=f"zch{l}")
            bch = sc.add("tensor", lambda e: e.matmul(
                ps[3][:], ones1282[:], zrhs2[:, 0, :], start=True,
                stop=True), [zch] + bk.dep_w(3), name=f"bch{l}")
            bk.wr(3, bch)
            prev_band["lnbc"] = bch
            rbh = sc.add("vector", lambda e: e.tensor_copy(rbS[:], ps[3][:]),
                         [bch], name=f"rbh{l}")
            bk.rd(3, rbh)
            h_steps = []
            for kt in range(NDT):
                def fn(e, kt=kt):
                    e.tensor_mul(xtmp2[:], xn1s[:, kt, :], rbS[:])
                    return e.tensor_single_scalar(hTb[:, kt, :], xtmp2[:],
                                                  anwC[:, kt:kt + 1],
                                                  op=ALU.mult)
                h_steps.append(sc.add("vector", fn,
                                      [rbh, ld_misc[4]] + hswap,
                                      name=f"hst{l}_{kt}"))
            rt1a = sc.add("vector", (lambda l: lambda e:
                e.tensor_single_scalar(angT[:], iotaT[:],
                                       bposC[:, l:l + 1], op=ALU.add))(l),
                [fr2, ld_misc[9], prev_band.get("ropeg"), rw_done[-1]],
                name=f"rt1a_{l}")
            rt1 = sc.add("vector", lambda e: e.tensor_single_scalar(
                angT[:], angT[:], freqC[:], op=ALU.mult), [rt1a],
                name=f"rt1_{l}")
            inv2pi = float(1.0 / (2.0 * np.pi))
            twopi = float(2.0 * np.pi)

            r2a = sc.add("vector", lambda e: (
                e.tensor_single_scalar(sinF[:], angT[:], inv2pi,
                                       op=ALU.mult)), [rt1],
                name=f"r2a_{l}")
            r2b = sc.add("vector", lambda e: e.tensor_copy(intS[:], sinF[:]),
                         [r2a], name=f"r2b_{l}")
            r2c = sc.add("vector", lambda e: e.tensor_copy(sinF[:], intS[:]),
                         [r2b], name=f"r2c_{l}")
            r2d = sc.add("vector", lambda e: e.scalar_tensor_tensor(
                sinF[:], sinF[:], -twopi, angT[:], op0=ALU.mult,
                op1=ALU.add), [r2c], name=f"r2d_{l}")
            r2e = sc.add("vector", lambda e: e.tensor_single_scalar(
                cosF[:], angT[:], float(0.5 * np.pi), op=ALU.add),
                [rt1, r2d], name=f"r2e_{l}")
            r2f = sc.add("vector", lambda e: e.tensor_single_scalar(
                tsin[:], cosF[:], inv2pi, op=ALU.mult), [r2e],
                name=f"r2f_{l}")
            r2g = sc.add("vector", lambda e: e.tensor_copy(intS[:], tsin[:]),
                         [r2f], name=f"r2g_{l}")
            r2h = sc.add("vector", lambda e: e.tensor_copy(tsin[:], intS[:]),
                         [r2g], name=f"r2h_{l}")
            rt2 = sc.add("vector", lambda e: e.scalar_tensor_tensor(
                cosF[:], tsin[:], -twopi, cosF[:], op0=ALU.mult,
                op1=ALU.add), [r2h], name=f"rt2_{l}")
            rt3 = sc.add("scalar", lambda e: (
                e.activation(cosF[:], cosF[:], AF.Sin),
                e.activation(sinF[:], sinF[:], AF.Sin))[-1], [rt2],
                name=f"rt3_{l}")
            rt4 = sc.add("vector", lambda e: e.tensor_single_scalar(
                sinN[:], sinF[:], sgnC[:], op=ALU.mult), [rt3, s_swp],
                name=f"rt4_{l}")

            def proj_head(tag, wsrc, n_heads, dest, evdeps=()):
                outs = []
                for j in range(n_heads):
                    slot = j % 2
                    ldw = sc.dma("gpsimd", "wt", wga2[:, slot],
                                 wsrc[:, :, j * 128:(j + 1) * 128],
                                 [prev_l1[slot]], name=f"{tag}w{l}_{j}")

                    def grp(e, slot=slot):
                        i = None
                        for kt in range(NDT):
                            i = e.matmul(ps[slot][:], wga2[:, slot, kt, :],
                                         hTb[:, kt, :], start=(kt == 0),
                                         stop=(kt == NDT - 1))
                        return i
                    g = sc.add("tensor", grp,
                               [ldw] + h_steps + bk.dep_w(slot),
                               name=f"{tag}g{l}_{j}")
                    bk.wr(slot, g)
                    prev_l1[slot] = g
                    ev = sc.add("scalar", (lambda j, slot: lambda e: e.copy(
                        dest[:, j, :], ps[slot][:]))(j, slot),
                        [g] + list(evdeps), name=f"{tag}e{l}_{j}")
                    bk.rd(slot, ev)
                    outs.append(ev)
                return outs

            qev = proj_head("q", wq2_r, H0, q2b,
                            evdeps=[prev_band.get("stq")])
            qrope = []
            d1 = d2 = None
            for j in range(H0):
                sq_ = sc.add("scalar", (lambda j: lambda e: e.square(
                    qsqS[:], q2b[:, j, :]))(j),
                    [qev[j], qrope[-1] if qrope else (mmh[-1]),
                     prev_band.get("qn_mm")],
                    name=f"qsq{l}_{j}")
                mm_ = sc.add("tensor", (lambda j: lambda e: e.matmul(
                    ps_rb[:], ones_b2[:], qsqS[:], start=True, stop=True)
                    )(j), [sq_] + bk.dep_w(7), name=f"qn_mm{l}_{j}")
                bk.wr(7, mm_)
                prev_band["qn_mm"] = mm_
                rrA_ = sc.add("scalar", lambda e: e.activation(
                    rows2[0:1, BAND:2 * BAND], ps_rb[:], AF.Ln,
                    scale=1.0 / 128.0, bias=ccol2[0:1, 1:2]), [mm_],
                    name="A")
                rr_ = sc.add("scalar", lambda e: e.activation(
                    rows2[0:1, BAND:2 * BAND],
                    rows2[0:1, BAND:2 * BAND], AF.Exp,
                    scale=-0.5), [rrA_], name=f"qnr{l}_{j}")
                bk.rd(7, rr_)
                zc_ = sc.add("vector", lambda e: e.tensor_copy(
                    zrhs2[0:1, 1, :], rows2[0:1, BAND:2 * BAND]), [rr_],
                    name=f"qnz{l}_{j}")
                bc_ = sc.add("tensor", lambda e: e.matmul(
                    ps[4][:], ones1282[:], zrhs2[:, 1, :], start=True,
                    stop=True), [zc_] + bk.dep_w(4), name=f"qnb{l}_{j}")
                bk.wr(4, bc_)

                def norm_fn(e, j=j):
                    e.tensor_mul(xtmp2[:], q2b[:, j, :], ps[4][:])
                    e.tensor_single_scalar(xtmp2[:], xtmp2[:], qnC[:],
                                           op=ALU.mult)
                    return e.tensor_copy(q2b[:, j, :], xtmp2[:])
                nf = sc.add("vector", norm_fn, [bc_, ld_misc[5]],
                            name=f"qnorm{l}_{j}")
                bk.rd(4, nf)
                sw = sc.add("tensor", (lambda j: lambda e: e.matmul(
                    ps[2][:], swapP[:], q2b[:, j, :], start=True, stop=True)
                    )(j), [nf, s_swp] + bk.dep_w(2), name=f"qsw{l}_{j}")
                bk.wr(2, sw)

                def rope_fn(e, j=j):
                    e.tensor_mul(tcos[:], q2b[:, j, :], cosF[:])
                    e.tensor_mul(tsin[:], ps[2][:], sinN[:])
                    return e.tensor_add(q2b[:, j, :], tcos[:], tsin[:])
                rp = sc.add("vector", rope_fn,
                            [sw, rt4] + ([d2] if l == 0 and j == 0 else []),
                            name=f"qrope{l}_{j}")
                bk.rd(2, rp)
                if l == 0 and j == 0:
                    sc.dma("sync", "st", o_dbg2[:, 1024:1536],
                           q2b[:, 0, :], [rp], name="dbg_rope")
                    dcps = sc.add("vector", lambda e: e.tensor_copy(
                        qtmp[:, 0:BAND].rearrange("p (a b) -> p (a b)") if False else osq[:], sinF[:]), [rt3, rp], name="dbgcp")
                    sc.dma("sync", "st", o_dbg2[:, 1536:2048],
                           osq[:], [dcps], name="dbgsin")
                qrope.append(rp)
            stq_step = sc.dma("sync", "st", o_q_r[:, :, cols],
                              q2b[:].rearrange("p h t -> p (h t)"), qrope,
                              name=f"stq{l}")
            prev_band["stq"] = stq_step
            kev = proj_head("k2", wk2_r, 4, kf4,
                            evdeps=[prev_band.get("stkf"),
                                    prev_band.get("vtr_last")])
            krope = []
            for j in range(4):
                sq_ = sc.add("scalar", (lambda j: lambda e: e.square(
                    qsqS[:], kf4[:, j, :]))(j),
                    [kev[j], krope[-1] if krope else qrope[-1],
                     prev_band.get("mmk")],
                    name=f"ksq{l}_{j}")
                mm_ = sc.add("tensor", (lambda j: lambda e: e.matmul(
                    ps_rb[:], ones_b2[:], qsqS[:], start=True, stop=True)
                    )(j), [sq_] + bk.dep_w(7), name=f"kn_mm{l}_{j}")
                bk.wr(7, mm_)
                prev_band["qn_mm"] = mm_
                rrA_ = sc.add("scalar", lambda e: e.activation(
                    rows2[0:1, BAND:2 * BAND], ps_rb[:], AF.Ln,
                    scale=1.0 / 128.0, bias=ccol2[0:1, 1:2]), [mm_],
                    name="A")
                rr_ = sc.add("scalar", lambda e: e.activation(
                    rows2[0:1, BAND:2 * BAND],
                    rows2[0:1, BAND:2 * BAND], AF.Exp,
                    scale=-0.5), [rrA_], name=f"knr{l}_{j}")
                bk.rd(7, rr_)
                zc_ = sc.add("vector", lambda e: e.tensor_copy(
                    zrhs2[0:1, 1, :], rows2[0:1, BAND:2 * BAND]), [rr_],
                    name=f"knz{l}_{j}")
                bc_ = sc.add("tensor", lambda e: e.matmul(
                    ps[4][:], ones1282[:], zrhs2[:, 1, :], start=True,
                    stop=True), [zc_] + bk.dep_w(4), name=f"knb{l}_{j}")
                bk.wr(4, bc_)

                def norm_fn(e, j=j):
                    e.tensor_mul(xtmp2[:], kf4[:, j, :], ps[4][:])
                    e.tensor_single_scalar(xtmp2[:], xtmp2[:], knC[:],
                                           op=ALU.mult)
                    return e.tensor_copy(kf4[:, j, :], xtmp2[:])
                nf = sc.add("vector", norm_fn, [bc_, ld_misc[6]],
                            name=f"knorm{l}_{j}")
                bk.rd(4, nf)
                sw = sc.add("tensor", (lambda j: lambda e: e.matmul(
                    ps[2][:], swapP[:], kf4[:, j, :], start=True, stop=True)
                    )(j), [nf, s_swp] + bk.dep_w(2), name=f"ksw{l}_{j}")
                bk.wr(2, sw)

                def rope_fn(e, j=j):
                    e.tensor_mul(tcos[:], kf4[:, j, :], cosF[:])
                    e.tensor_mul(tsin[:], ps[2][:], sinN[:])
                    return e.tensor_add(kf4[:, j, :], tcos[:], tsin[:])
                rp = sc.add("vector", rope_fn, [sw, rt4], name=f"krope{l}_{j}")
                bk.rd(2, rp)
                krope.append(rp)
                sqk = sc.add("scalar", (lambda j: lambda e: e.square(
                    qsqS[:], kf4[:, j, :]))(j), [rp], name=f"ksq2_{l}_{j}")
                mmk = sc.add("tensor", (lambda j: lambda e: e.matmul(
                    ps_rb[:], ones_b2[:], qsqS[:], start=True, stop=True)
                    )(j), [sqk] + bk.dep_w(7), name=f"ksmm{l}_{j}")
                bk.wr(7, mmk)
                prev_band["mmk"] = mmk
                cpk = sc.add("vector", lambda e: e.tensor_copy(
                    rows2[0:1, 3 * BAND:4 * BAND], ps_rb[:]),
                    [mmk, prev_band.get("stksq")], name=f"kscp{l}_{j}")
                bk.rd(7, cpk)
                prev_band["stksq"] = sc.dma(
                    "sync", "st", o_ksq[j:j + 1, cols],
                    rows2[0:1, 3 * BAND:4 * BAND], [cpk],
                    name=f"stksq{l}_{j}")
            stkf_step = sc.dma(
                "sync", "st",
                o_kf.rearrange("(h p) t -> p h t", p=128)[:, :, cols],
                kf4[:].rearrange("p h t -> p (h t)"), krope,
                name=f"stkf{l}")
            prev_band["stkf"] = stkf_step
            vev = proj_head("v2", wv2_r, 4, kf4,
                            evdeps=[krope[-1], stkf_step])
            vtr = []
            for j in range(4):
                def trj(e, j=j):
                    i = None
                    for tt in range(4):
                        i = e.transpose(ps_bf[:, tt * 128:(tt + 1) * 128],
                                        kf4[:, j, tt * 128:(tt + 1) * 128],
                                        identb2[:])
                    return i
                t_ = sc.add("tensor", trj, [vev[j]] + bk.dep_w(5),
                            name=f"vtr{l}_{j}")
                bk.wr(5, t_)
                e_ = sc.add("vector", (lambda j: lambda e: e.tensor_copy(
                    vn4[:].rearrange("p h t -> p (h t)")[:, j * 512:
                                                         (j + 1) * 512],
                    ps_bf[:, 0:512]))(j),
                    [t_] + (prev_band.get("stvf") or []),
                    name=f"vne{l}_{j}")
                bk.rd(5, e_)
                vtr.append(e_)
            stvfs = []
            for j in range(4):
                stvfs.append(sc.dma(
                    "sync", "st",
                    o_vf[j, l * BAND:(l + 1) * BAND, :].rearrange(
                        "(tt p) d -> p tt d", p=128),
                    vn4[:, 4 * j:4 * j + 4, :], [vtr[j]],
                    name=f"stvf{l}_{j}"))
            prev_band["stvf"] = stvfs
            prev_band["vtr_last"] = vtr[-1]
            hswap = [vev[-1], vtr[-1]]
            prev_band["l1xn_rd"] = mmh[-1]
            prev_band["ropeg"] = krope[-1]

        sc.emit(nc, sems)
    return nc


def _prep_launch2_inputs(inputs, l1_results):
    wqkvu = np.asarray(inputs["L0_attn_W_qkvu"], np.float32)
    wq_t = np.ascontiguousarray(wqkvu[0:D].T)
    wu_t = np.ascontiguousarray(wqkvu[3 * D:4 * D].T)
    wo_t = np.ascontiguousarray(
        np.asarray(inputs["L0_attn_W_out"], np.float32).T)
    slope = np.ascontiguousarray(
        np.asarray(inputs["L0_attn_slope_rate"], np.float32).reshape(1, H0))
    wsh_t = np.ascontiguousarray(
        np.asarray(inputs["L1_attn_W_shift"], np.float32).T)
    gate_c = np.ascontiguousarray(
        np.asarray(inputs["L1_attn_gate"], np.float32).reshape(NDT, 128).T)
    lng_c = np.ascontiguousarray(
        np.asarray(inputs["L1_attn_ln_gamma"], np.float32).reshape(NDT,
                                                                   128).T)
    lnb_c = np.ascontiguousarray(
        np.asarray(inputs["L1_attn_ln_beta"], np.float32).reshape(NDT, 128).T)
    wkey_t = np.ascontiguousarray(
        np.asarray(inputs["L1_attn_W_key"], np.float32).T)
    wval_t = np.ascontiguousarray(
        np.asarray(inputs["L1_attn_W_value"], np.float32).T)
    td_c = np.ascontiguousarray(
        np.asarray(inputs["L1_attn_time_decay"], np.float32).reshape(64, 1))
    tf_c = np.ascontiguousarray(
        np.asarray(inputs["L1_attn_time_first"], np.float32).reshape(64, 1))
    anw_c = np.ascontiguousarray(
        np.asarray(inputs["L1_ffn_attn_norm_w"], np.float32).reshape(
            NDT, 128).T)
    qw_t = np.ascontiguousarray(np.asarray(inputs["L1_ffn_q_w"],
                                           np.float32).T)
    kw_t = np.ascontiguousarray(np.asarray(inputs["L1_ffn_k_w"],
                                           np.float32).T)
    vw_t = np.ascontiguousarray(np.asarray(inputs["L1_ffn_v_w"],
                                           np.float32).T)
    qn_c = np.ascontiguousarray(
        np.asarray(inputs["L1_ffn_q_norm_w"], np.float32).reshape(128, 1))
    kn_c = np.ascontiguousarray(
        np.asarray(inputs["L1_ffn_k_norm_w"], np.float32).reshape(128, 1))

    S = {}
    for c in range(NCORE):
        b, r = c // 4, c % 4
        so = np.asarray(l1_results[c]["sum_o"])
        for l in range(LBANDS):
            S[(b, 4 * l + r)] = so[:, :, l, :]
    in_maps = []
    for c in range(NCORE):
        b, r = c // 4, c % 4
        sum_in = np.zeros((LBANDS, 4, 128, H0 * 128), np.float32)
        for l in range(LBANDS):
            g_l = 4 * l + r
            for k2 in range(4):
                gi = g_l - 4 + k2
                if gi >= 0:
                    sum_in[l, k2] = S[(b, gi)].reshape(128, H0 * 128)
        bpos = np.zeros((128, LBANDS), np.float32)
        for l in range(LBANDS):
            bpos[:, l] = 512.0 * (4 * l + r)
        in_maps.append({
            "x1p_t": np.asarray(l1_results[c]["x1p_o"]),
            "xn0_t": np.asarray(l1_results[c]["xn0_o"]),
            "k0_t": np.asarray(l1_results[c]["k_o"]),
            "v0_t": np.asarray(l1_results[c]["v_o"]),
            "sum_in": sum_in,
            "wq_t": wq_t, "wu_t": wu_t, "wo_t": wo_t, "slope": slope,
            "wsh_t": wsh_t, "gate_c": gate_c, "lng_c": lng_c, "lnb_c": lnb_c,
            "wkey_t": wkey_t, "wval_t": wval_t, "td_c": td_c, "tf_c": tf_c,
            "anw_c": anw_c, "qw_t": qw_t, "kw_t": kw_t, "vw_t": vw_t,
            "qn_c": qn_c, "kn_c": kn_c, "bpos_c": bpos,
        })
    return in_maps


def run_launch2(inputs, l1_results):
    nc = build_launch2()
    in_maps = _prep_launch2_inputs(inputs, l1_results)
    res = run_bass_kernel_spmd(nc, in_maps, core_ids=list(range(NCORE)))
    return res.results



def build_launch3():
    nc = bass.Bass("TRN2", target_bir_lowering=False, debug=False)

    i_q = nc.dram_tensor("q_t", [D, LT], BF16, kind="ExternalInput").ap()
    i_kf = nc.dram_tensor("kf_all", [4, 128, TSEQ], BF16,
                          kind="ExternalInput").ap()
    i_vf = nc.dram_tensor("vf_all", [4, TSEQ, 128], BF16,
                          kind="ExternalInput").ap()
    i_ksq = nc.dram_tensor("ksq_all", [4, 128, 64], F32,
                           kind="ExternalInput").ap()
    i_sloc = nc.dram_tensor("sloc_t", [64, LT], F32,
                            kind="ExternalInput").ap()
    i_sfin = nc.dram_tensor("sfin_in", [LBANDS, 64, 4], F32,
                            kind="ExternalInput").ap()
    i_td = nc.dram_tensor("td_c", [64, 1], F32, kind="ExternalInput").ap()
    i_bpos = nc.dram_tensor("bpos_c", [128, LBANDS], F32,
                            kind="ExternalInput").ap()

    o_attn = nc.dram_tensor("attn_o", [D, LT], BF16,
                            kind="ExternalOutput").ap()
    o_states = nc.dram_tensor("states_o", [64, LT], F32,
                              kind="ExternalOutput").ap()

    q_r = i_q.rearrange("(k p) t -> p k t", p=128)
    attn_r = o_attn.rearrange("(k p) t -> p k t", p=128)

    sc = Sched()
    bk = Banks()

    with ExitStack() as stk:
        ec = stk.enter_context
        kfb = ec(nc.sbuf_tensor("kfb", [128, TSEQ], BF16))
        vnb = ec(nc.sbuf_tensor("vnb", [128, 64, 128], BF16))
        qb3 = ec(nc.sbuf_tensor("qb3", [128, 2, BAND], BF16))
        Pt = ec(nc.sbuf_tensor("Pt", [128, 2, BAND], BF16))
        qposB = ec(nc.sbuf_tensor("qposB", [128, BAND], F32))
        kposC = ec(nc.sbuf_tensor("kposC", [128, 2], F32))
        biasC = ec(nc.sbuf_tensor("biasC", [128, NBANDS], F32))
        flagB = ec(nc.sbuf_tensor("flagB", [128, NBANDS], F32))
        bcolB = ec(nc.sbuf_tensor("bcolB", [128, NBANDS], F32))
        iota16B = ec(nc.sbuf_tensor("iota16B", [128, NBANDS], F32))
        gcol3 = ec(nc.sbuf_tensor("gcol3", [128, 1], F32))
        qmaxR = ec(nc.sbuf_tensor("qmaxR", [1, H0], F32))
        kmaxR = ec(nc.sbuf_tensor("kmaxR", [1, 16], F32))
        brow = ec(nc.sbuf_tensor("brow", [1, 16], F32))
        ksqB = ec(nc.sbuf_tensor("ksqB", [128, 64], F32))
        kmx = ec(nc.sbuf_tensor("kmx", [128, 1], F32))
        kmxr = ec(nc.sbuf_tensor("kmxr", [1, 128], F32))
        sq3 = ec(nc.sbuf_tensor("sq3", [128, BAND], BF16))
        rbS3 = ec(nc.sbuf_tensor("rbS3", [128, BAND], F32))
        zrhsM3 = ec(nc.sbuf_tensor("zrhsM3", [128, BAND], F32))
        rows3 = ec(nc.sbuf_tensor("rows3", [1, 2 * BAND], F32))
        ones_b3 = ec(nc.sbuf_tensor("ones_b3", [128, 1], BF16))
        ones1283 = ec(nc.sbuf_tensor("ones1283", [128, 128], F32))
        ident_f3 = ec(nc.sbuf_tensor("ident_f3", [128, 128], F32))
        iotap3 = ec(nc.sbuf_tensor("iotap3", [128, 128], F32))
        iotaT3 = ec(nc.sbuf_tensor("iotaT3", [128, BAND], F32))
        bposC3 = ec(nc.sbuf_tensor("bposC3", [128, LBANDS], F32))
        tdC3 = ec(nc.sbuf_tensor("tdC3", [64, 1], F32))
        w512C = ec(nc.sbuf_tensor("w512C", [64, 1], F32))
        wpow = ec(nc.sbuf_tensor("wpow", [64, 4], F32))
        w2048C = ec(nc.sbuf_tensor("w2048C", [64, 1], F32))
        sfinB = ec(nc.sbuf_tensor("sfinB", [64, 4], F32))
        tmp4 = ec(nc.sbuf_tensor("tmp4", [64, 4], F32))
        csumC = ec(nc.sbuf_tensor("csumC", [64, 1], F32))
        carC = ec(nc.sbuf_tensor("carC", [64, 2], F32))
        decayP = ec(nc.sbuf_tensor("decayP", [64, BAND], F32))
        slocB3 = ec(nc.sbuf_tensor("slocB3", [64, 2, BAND], F32))
        stS = ec(nc.sbuf_tensor("stS", [64, 2, BAND], F32))

        ps = [ec(nc.psum_tensor(f"p3_{i}", [128, BAND], F32))
              for i in range(5)]
        ps_ra3 = ec(nc.psum_tensor("p3_ra", [1, BAND], F32))
        ps_rb3 = ec(nc.psum_tensor("p3_rb", [1, BAND], F32))

        sems = {}
        for n in SEMS:
            sems[n] = ec(nc.semaphore(n))

        CSC = float(1.0 / np.sqrt(128.0))
        BIGB = 1.0e9

        ld_td = sc.dma("sync", "ld", tdC3[:], i_td, name="l3td")
        ld_bp = sc.dma("sync", "ld", bposC3[:], i_bpos, name="l3bp")

        def fn_io3(e):
            e.iota(iotap3[:], pattern=[[-1, 128]], base=0,
                   channel_multiplier=1, allow_small_or_imprecise_dtypes=True)
            e.iota(iotaT3[:], pattern=[[1, BAND]], base=0,
                   channel_multiplier=0, allow_small_or_imprecise_dtypes=True)
            e.iota(iota16B[:], pattern=[[1, NBANDS]], base=0,
                   channel_multiplier=0, allow_small_or_imprecise_dtypes=True)
            e.iota(kposC[:, 0:1], pattern=[[0, 1]], base=0,
                   channel_multiplier=1, allow_small_or_imprecise_dtypes=True)
            return e.iota(decayP[:], pattern=[[1, BAND]], base=1,
                          channel_multiplier=0,
                          allow_small_or_imprecise_dtypes=True)
        s_io3 = sc.add("gpsimd", fn_io3, name="io3")

        def fn_con3(e):
            e.memset(ones_b3[:], 1.0)
            e.memset(ones1283[:], 1.0)
            e.memset(zrhsM3[:], 0.0)
            return e.tensor_single_scalar(ident_f3[:], iotap3[:], 0.0,
                                          op=ALU.is_equal)
        s_con3 = sc.add("vector", fn_con3, [s_io3], name="con3")

        s_w512 = sc.add("scalar", lambda e: e.activation(
            w512C[:], tdC3[:], AF.Exp, scale=512.0), [ld_td, s_con3],
            name="w512")
        s_dp = sc.add("scalar", lambda e: e.activation(
            decayP[:], decayP[:], AF.Exp, scale=tdC3[:]),
            [s_w512, s_io3], name="dp3")
        wp0 = sc.add("vector", lambda e: (
            e.memset(wpow[:, 0:1], 1.0),
            e.tensor_copy(wpow[:, 1:2], w512C[:]))[-1], [s_w512], name="wp0")
        wp1 = sc.add("vector", lambda e: e.tensor_mul(
            wpow[:, 2:3], wpow[:, 1:2], w512C[:]), [wp0], name="wp1")
        wp2 = sc.add("vector", lambda e: e.tensor_mul(
            wpow[:, 3:4], wpow[:, 2:3], w512C[:]), [wp1], name="wp2")
        wp3 = sc.add("vector", lambda e: e.tensor_mul(
            w2048C[:], wpow[:, 3:4], w512C[:]), [wp2], name="wp3")

        st_guard = {0: None, 1: None}
        prev_car = None
        prev_prod = wp3
        for l in range(LBANDS):
            cur = l % 2
            cols = slice(l * BAND, (l + 1) * BAND)
            ldsf = sc.dma("sync", "ld", sfinB[:], i_sfin[l],
                          [prev_prod], name=f"rsf{l}")
            ldsl = sc.dma("sync", "ld", slocB3[:, cur, :], i_sloc[:, cols],
                          [st_guard[cur]], name=f"rsl{l}")
            prod = sc.add("vector", lambda e: e.tensor_mul(
                tmp4[:], sfinB[:], wpow[:]), [ldsf, wp3, prev_car],
                name=f"rpr{l}")
            prev_prod = prod
            if l == 0:
                car = sc.add("vector", lambda e: e.reduce_sum(
                    carC[:, 0:1], tmp4[:], axis=AX.X), [prod],
                    name=f"rcar{l}")
            else:
                cs = sc.add("vector", lambda e: e.reduce_sum(
                    csumC[:], tmp4[:], axis=AX.X), [prod], name=f"rcs{l}")
                car = sc.add("vector", (lambda cur: lambda e:
                             e.scalar_tensor_tensor(
                                 carC[:, cur:cur + 1],
                                 carC[:, 1 - cur:2 - cur], w2048C[:],
                                 csumC[:], op0=ALU.mult, op1=ALU.add))(cur),
                             [cs], name=f"rcar{l}")
            prev_car = car
            stt_ = sc.add("vector", (lambda cur: lambda e:
                          e.scalar_tensor_tensor(
                              stS[:, cur, :], decayP[:],
                              carC[:, cur:cur + 1], slocB3[:, cur, :],
                              op0=ALU.mult, op1=ALU.add))(cur),
                          [car, ldsl, s_dp, st_guard[cur]],
                          name=f"rst{l}")
            st_guard[cur] = sc.dma("sync", "st", o_states[:, cols],
                                   stS[:, cur, :], [stt_], name=f"rsts{l}")

        prev_sq = s_con3
        qm_steps = []
        for qh in range(H0):
            mmq = None
            for l in range(LBANDS):
                ldq = sc.dma("sync", "ld", qb3[:, l % 2, :],
                             q_r[:, qh, l * BAND:(l + 1) * BAND],
                             [prev_sq], name=f"bq{qh}_{l}")
                sqq = sc.add("scalar", (lambda l: lambda e: e.square(
                    sq3[:], qb3[:, l % 2, :]))(l), [ldq, prev_sq],
                    name=f"bsq{qh}_{l}")
                mmq = sc.add("tensor", (lambda l: lambda e: e.matmul(
                    ps_rb3[:], ones_b3[:], sq3[:], start=(l == 0),
                    stop=(l == LBANDS - 1)))(l),
                    [sqq] + (bk.dep_w(6) if l == 0 else []),
                    name=f"bmm{qh}_{l}")
                prev_sq = sqq
            bk.wr(6, mmq)
            qm = sc.add("vector", (lambda qh: lambda e: e.reduce_max(
                qmaxR[0:1, qh:qh + 1], ps_rb3[:], axis=AX.X))(qh), [mmq],
                name=f"bqm{qh}")
            bk.rd(6, qm)
            prev_sq = qm
            qm_steps.append(qm)
        km_steps = []
        for kh in range(4):
            ldk3 = sc.dma("sync", "ld", ksqB[:], i_ksq[kh],
                          [km_steps[-1] if km_steps else None],
                          name=f"bk{kh}")
            r1 = sc.add("vector", lambda e: e.reduce_max(
                kmx[:], ksqB[:], axis=AX.X),
                [ldk3, km_steps[-1] if km_steps else None], name=f"bkr{kh}")
            tr = sc.add("tensor", lambda e: e.transpose(
                ps[4][0:1, 0:128], kmx[:], ident_f3[:]),
                [r1, s_con3] + bk.dep_w(4), name=f"bkt{kh}")
            bk.wr(4, tr)
            cp = sc.add("vector", lambda e: e.tensor_copy(
                kmxr[:], ps[4][0:1, 0:128]), [tr], name=f"bkc{kh}")
            bk.rd(4, cp)
            r2 = sc.add("vector", (lambda kh: lambda e: e.reduce_max(
                kmaxR[0:1, kh:kh + 1], kmxr[:], axis=AX.X))(kh), [cp],
                name=f"bkm{kh}")
            km_steps.append(r2)
        kx2 = sc.add("vector", lambda e: (
            [e.tensor_copy(kmaxR[0:1, 15 - i:16 - i],
                           kmaxR[0:1, (15 - i) // 4:(15 - i) // 4 + 1])
             for i in range(16)][-1]), km_steps, name="kx2")
        bm1 = sc.add("vector", lambda e: e.tensor_mul(
            brow[:], qmaxR[:], kmaxR[:]), [kx2] + qm_steps, name="bm1")
        bm2 = sc.add("scalar", lambda e: e.activation(
            brow[:], brow[:], AF.Sqrt), [bm1], name="bm2")
        bm3 = sc.add("vector", lambda e: e.tensor_single_scalar(
            brow[:], brow[:], -CSC, op=ALU.mult), [bm2], name="bm3")
        bm4 = sc.add("vector", lambda e: e.tensor_copy(
            zrhsM3[0:1, 0:16], brow[:]), [bm3], name="bm4")
        bm5 = sc.add("tensor", lambda e: e.matmul(
            ps[4][:, 0:16], ones1283[:], zrhsM3[:, 0:16], start=True,
            stop=True), [bm4] + bk.dep_w(4), name="bm5")
        bk.wr(4, bm5)
        bm6 = sc.add("vector", lambda e: (
            e.tensor_copy(bcolB[:], ps[4][:, 0:16]),
            e.memset(zrhsM3[0:1, 0:16], 0.0))[-1], [bm5], name="bm6")
        bk.rd(4, bm6)

        prev_kv = [None]
        prev_bias = [bm6]
        prev_store = [None]
        for kvh in range(4):
            ldkf = sc.dma("sync", "ld", kfb[:], i_kf[kvh], [prev_kv[0]],
                          name=f"fkf{kvh}")
            ldvf = sc.dma("sync", "ld", vnb[:],
                          i_vf[kvh].rearrange("(kt p) d -> p kt d", p=128),
                          [prev_kv[0]], name=f"fvf{kvh}")
            for qi in range(4):
                qh = 4 * kvh + qi
                for l in range(LBANDS):
                    it = qh * 4 + l
                    ldq = sc.dma("sync", "ld", qb3[:, it % 2, :],
                                 q_r[:, qh, l * BAND:(l + 1) * BAND],
                                 [prev_bias[0]], name=f"fq{qh}_{l}")
                    gv = sc.add("vector", (lambda l: lambda e:
                                e.tensor_single_scalar(
                                    gcol3[:], bposC3[:, l:l + 1],
                                    1.0 / 512.0, op=ALU.mult))(l),
                                [ld_bp, prev_bias[0]], name=f"fg1{qh}_{l}")
                    fv = sc.add("vector", lambda e: e.tensor_single_scalar(
                        flagB[:], iota16B[:], gcol3[:], op=ALU.is_gt),
                        [gv, s_io3], name=f"fg2{qh}_{l}")
                    fb = sc.add("vector", (lambda qh: lambda e:
                                e.tensor_scalar(
                                    biasC[:], flagB[:], -BIGB,
                                    bcolB[:, qh:qh + 1], op0=ALU.mult,
                                    op1=ALU.add))(qh), [fv],
                                name=f"fb{qh}_{l}")
                    qp = sc.add("vector", (lambda l: lambda e:
                                e.tensor_single_scalar(
                                    qposB[:], iotaT3[:],
                                    bposC3[:, l:l + 1], op=ALU.add))(l),
                                [ld_bp, prev_bias[0], s_io3],
                                name=f"fqp{qh}_{l}")
                    prev_bias[0] = fb
                    nT = 16 * (l + 1)
                    prev_exp = {0: None, 1: None}
                    den_mm = None
                    pv_mm = None
                    started = False
                    for ti in range(nT):
                        K = ti // 4
                        sslot = ti % 2
                        tok = ti * 128
                        smm = sc.add("tensor", (lambda tok, sslot, it:
                                     lambda e: e.matmul(
                                         ps[sslot][:],
                                         kfb[:, tok:tok + 128],
                                         qb3[:, it % 2, :],
                                         start=True, stop=True))(
                                         tok, sslot, it),
                                     [ldkf, ldq] + bk.dep_w(sslot),
                                     name=f"S{qh}_{l}_{ti}")
                        bk.wr(sslot, smm)
                        ex = sc.add("scalar", (lambda K, sslot: lambda e:
                                    e.activation(Pt[:, sslot, :],
                                                 ps[sslot][:], AF.Exp,
                                                 scale=CSC,
                                                 bias=biasC[:, K:K + 1]))(
                                        K, sslot),
                                    [smm, fb, prev_exp[sslot]],
                                    name=f"E{qh}_{l}_{ti}")
                        bk.rd(sslot, ex)
                        use = ex
                        if K >= 4 * l:
                            mk1 = sc.add("gpsimd", (lambda tok: lambda e:
                                         e.tensor_single_scalar(
                                             kposC[:, 1:2], kposC[:, 0:1],
                                             float(tok), op=ALU.add))(tok),
                                         [s_io3, prev_store[0]],
                                         name=f"M1{qh}_{l}_{ti}")
                            mk2 = sc.add("gpsimd", lambda e:
                                         e.tensor_single_scalar(
                                             sq3[:], qposB[:],
                                             kposC[:, 1:2], op=ALU.is_ge),
                                         [mk1, qp], name=f"M2{qh}_{l}_{ti}")
                            mk3 = sc.add("gpsimd", (lambda sslot: lambda e:
                                         e.tensor_mul(Pt[:, sslot, :],
                                                      Pt[:, sslot, :],
                                                      sq3[:]))(sslot),
                                         [mk2, ex], name=f"M3{qh}_{l}_{ti}")
                            use = mk3
                        last = (ti == nT - 1)
                        den_mm = sc.add("tensor",
                                        (lambda sslot, st=not started,
                                         sp=last: lambda e: e.matmul(
                                             ps_ra3[:], ones_b3[:],
                                             Pt[:, sslot, :], start=st,
                                             stop=sp))(sslot),
                                        [use] + (bk.dep_w(5) if not started
                                                 else []),
                                        name=f"D{qh}_{l}_{ti}")
                        pv_mm = sc.add("tensor",
                                       (lambda tok, sslot, st=not started,
                                        sp=last: lambda e: e.matmul(
                                            ps[2][:], vnb[:, tok // 128, :],
                                            Pt[:, sslot, :], start=st,
                                            stop=sp))(tok, sslot),
                                       [use, ldvf] + (bk.dep_w(2) if not
                                                      started else []),
                                       name=f"P{qh}_{l}_{ti}")
                        prev_exp[sslot] = pv_mm
                        started = True
                    bk.wr(5, den_mm)
                    bk.wr(2, pv_mm)
                    dn1 = sc.add("vector", lambda e: e.tensor_copy(
                        rows3[0:1, 0:BAND], ps_ra3[:]), [den_mm],
                        name=f"dn1{qh}_{l}")
                    bk.rd(5, dn1)
                    dn2 = sc.add("vector", lambda e: e.reciprocal(
                        rows3[0:1, 0:BAND], rows3[0:1, 0:BAND]), [dn1],
                        name=f"dn2{qh}_{l}")
                    dn3 = sc.add("vector", lambda e: e.tensor_copy(
                        zrhsM3[0:1, :], rows3[0:1, 0:BAND]), [dn2],
                        name=f"dn3{qh}_{l}")
                    dn4 = sc.add("tensor", lambda e: e.matmul(
                        ps[3][:], ones1283[:], zrhsM3[:], start=True,
                        stop=True), [dn3] + bk.dep_w(3), name=f"dn4{qh}_{l}")
                    bk.wr(3, dn4)
                    dn5 = sc.add("vector", lambda e: e.tensor_copy(
                        rbS3[:], ps[3][:]), [dn4], name=f"dn5{qh}_{l}")
                    bk.rd(3, dn5)
                    aev = sc.add("vector", lambda e: e.tensor_mul(
                        sq3[:], ps[2][:], rbS3[:]),
                        [dn5, pv_mm, prev_store[0]], name=f"aev{qh}_{l}")
                    bk.rd(2, aev)
                    ast = sc.dma("sync", "st",
                                 attn_r[:, qh, l * BAND:(l + 1) * BAND],
                                 sq3[:], [aev], name=f"ast{qh}_{l}")
                    prev_store[0] = ast
                    prev_bias[0] = aev
            prev_kv[0] = prev_store[0]

        fin3 = sc.add("vector", lambda e: e.engine_nop(),
                      [prev_store[0]] + [st_guard[0], st_guard[1]],
                      name="final3")
        sc.emit(nc, sems)
    return nc



def build_launch4():
    nc = bass.Bass("TRN2", target_bir_lowering=False, debug=False)

    i_x1 = nc.dram_tensor("x1_t", [D, LT], F32, kind="ExternalInput").ap()
    i_xn1 = nc.dram_tensor("xn1_t", [D, LT], BF16, kind="ExternalInput").ap()
    i_attn = nc.dram_tensor("attn_t", [D, LT], BF16,
                            kind="ExternalInput").ap()
    i_st = nc.dram_tensor("states_t", [64, LT], F32,
                          kind="ExternalInput").ap()
    i_wor = nc.dram_tensor("wor_t", [64, D], F32, kind="ExternalInput").ap()
    i_wo2 = nc.dram_tensor("wo2_t", [NDT, D, 128], F32,
                           kind="ExternalInput").ap()
    i_fnw = nc.dram_tensor("fnw_c", [128, NDT], F32,
                           kind="ExternalInput").ap()
    i_gw2 = nc.dram_tensor("gw2_t", [D, 8], F32, kind="ExternalInput").ap()
    i_wg2 = nc.dram_tensor("wg2", [8, D, 1408], F32,
                           kind="ExternalInput").ap()
    i_wu2 = nc.dram_tensor("wu2", [8, D, 1408], F32,
                           kind="ExternalInput").ap()
    i_wd2 = nc.dram_tensor("wd2", [8, 1408, D], F32,
                           kind="ExternalInput").ap()

    o_x = nc.dram_tensor("xo_t", [D, LT], F32, kind="ExternalOutput").ap()
    d_comb4 = nc.dram_tensor("comb4_d", [1, 8192], F32, kind="Internal").ap()

    x1_r = i_x1.rearrange("(k p) t -> p k t", p=128)
    xn1_r = i_xn1.rearrange("(k p) t -> p k t", p=128)
    attn_in_r = i_attn.rearrange("(k p) t -> p k t", p=128)
    gw_rr = i_gw2.rearrange("(k p) e -> p k e", p=128)
    o_x_r = o_x.rearrange("(k p) t -> p k t", p=128)

    sc = Sched()
    bk = Banks()

    with ExitStack() as stk:
        ec = stk.enter_context
        xbh = ec(nc.sbuf_tensor("xbh", [128, NDT, 1024], F32))
        hid2 = ec(nc.sbuf_tensor("hid2", [128, NDT, 1024], BF16))
        REG4 = ec(nc.sbuf_tensor("REG4", [128, 23552], BF16))
        abuf = REG4[:, 0:16384].rearrange("p (k t) -> p k t", k=NDT)
        h2h = REG4[:, 0:16384].rearrange("p (k t) -> p k t", k=NDT)
        gS = REG4[:, 16384:22528].rearrange("p (j t) -> p j t", j=6)
        sgb = REG4[:, 22528:23040]
        gtmp = REG4[:, 23040:23552]
        WBUF = ec(nc.sbuf_tensor("WBUF", [128, 8192], BF16))
        wo2p = WBUF[:, 0:4096].rearrange("p (s k c) -> p s k c", s=2, k=NDT)
        wga = WBUF[:, 0:4096].rearrange("p (s k c) -> p s k c", s=2, k=NDT)
        wub = WBUF[:, 4096:8192].rearrange("p (s k c) -> p s k c", s=2,
                                           k=NDT)
        wdb = ec(nc.sbuf_tensor("wdb", [128, 2, 6, 128], BF16))
        worb = ec(nc.sbuf_tensor("worb", [64, D], BF16))
        gwb = ec(nc.sbuf_tensor("gwb", [128, NDT, 8], BF16))
        stb = ec(nc.sbuf_tensor("stb", [64, 1024], BF16))
        fnwC = ec(nc.sbuf_tensor("fnwC", [128, NDT], F32))
        logit = ec(nc.sbuf_tensor("logit", [8, 1024], F32))
        combT = logit
        combB = ec(nc.sbuf_tensor("combB", [128, 2, BAND], F32))
        gsA = ec(nc.sbuf_tensor("gsA", [128, 8], F32))
        gsB = ec(nc.sbuf_tensor("gsB", [128, 8], F32))
        gcol = ec(nc.sbuf_tensor("gcol", [128, 8], F32))
        xtmp4 = ec(nc.sbuf_tensor("xtmp4", [128, BAND], F32))
        xtmp4b = xtmp4
        sqS = ec(nc.sbuf_tensor("sqS", [128, BAND], BF16))
        rbS4 = ec(nc.sbuf_tensor("rbS4", [128, BAND], F32))
        zrhsM4 = ec(nc.sbuf_tensor("zrhsM4", [128, BAND], F32))
        rows4 = ec(nc.sbuf_tensor("rows4", [1, 2 * BAND], F32))
        ones_b4 = ec(nc.sbuf_tensor("ones_b4", [128, 1], BF16))
        ones1284 = ec(nc.sbuf_tensor("ones1284", [128, 128], F32))
        ident_f4 = ec(nc.sbuf_tensor("ident_f4", [128, 128], F32))
        iotap4 = ec(nc.sbuf_tensor("iotap4", [128, 128], F32))
        ccol4 = ec(nc.sbuf_tensor("ccol4", [1, 2], F32))

        ps = [ec(nc.psum_tensor(f"p4_{i}", [128, BAND], F32))
              for i in range(5)]
        ps_ra4 = ec(nc.psum_tensor("p4_ra", [1, BAND], F32))

        sems = {}
        for n in SEMS:
            sems[n] = ec(nc.semaphore(n))

        s_io4 = sc.add("gpsimd", lambda e: e.iota(
            iotap4[:], pattern=[[-1, 128]], base=0, channel_multiplier=1,
            allow_small_or_imprecise_dtypes=True), name="io4")

        def fn_con4(e):
            e.memset(ones_b4[:], 1.0)
            e.memset(ones1284[:], 1.0)
            e.memset(zrhsM4[:], 0.0)
            e.memset(ccol4[0:1, 0:1], RMS_EPS)
            return e.tensor_single_scalar(ident_f4[:], iotap4[:], 0.0,
                                          op=ALU.is_equal)
        s_con4 = sc.add("vector", fn_con4, [s_io4], name="con4")

        ld_wor = sc.dma("gpsimd", "wt", worb[:], i_wor, name="l4wor")
        ld_gw = sc.dma("gpsimd", "wt", gwb[:], gw_rr, name="l4gw")
        ld_fnw = sc.dma("sync", "ld", fnwC[:], i_fnw, name="l4fnw")

        xbh_guard = [None]
        hid2_guard = [None]
        h2h_guard = [None]
        wbuf_guard = {0: None, 1: None}
        wdb_guard = {0: None, 1: None}
        zrhs_guard = [s_con4]
        comb_guard = [None]
        prev_gev = [None]
        prev_sexp = [None]
        st_steps = []

        for hh in range(2):
            hcols = slice(hh * 1024, (hh + 1) * 1024)
            ldx1 = sc.dma("sync", "ld", xbh[:], x1_r[:, :, hcols],
                          [xbh_guard[0]], name=f"ldx1_{hh}")
            ldxn1 = sc.dma("sync", "ld", hid2[:], xn1_r[:, :, hcols],
                           [hid2_guard[0]], name=f"ldxn1_{hh}")
            ldat = sc.dma("sync", "ld", abuf[:], attn_in_r[:, :, hcols],
                          [prev_gev[0]], name=f"ldat{hh}")
            ldst = sc.dma("gpsimd", "wt", stb[:], i_st[:, hcols],
                          [xbh_guard[0]], name=f"ldst{hh}")

            wor_adds = {}
            for j in range(NDT):
                for hb in range(2):
                    bsl = slice(hb * BAND, (hb + 1) * BAND)
                    wmm = sc.add("tensor", (lambda j, bsl: lambda e:
                                 e.matmul(ps[4][:],
                                          worb[:, j * 128:(j + 1) * 128],
                                          stb[:, bsl], start=True,
                                          stop=True))(j, bsl),
                                 [ld_wor, ldst] + bk.dep_w(4),
                                 name=f"wm{hh}_{j}_{hb}")
                    bk.wr(4, wmm)
                    xa = sc.add("vector", (lambda j, bsl: lambda e:
                                e.tensor_add(xbh[:, j, bsl],
                                             xbh[:, j, bsl], ps[4][:]))(
                                    j, bsl), [wmm, ldx1],
                                name=f"wa{hh}_{j}_{hb}")
                    bk.rd(4, xa)
                    wor_adds[(j, hb)] = xa

            hv3s = {}
            for j in range(NDT):
                slot = j % 2
                ldw = sc.dma("gpsimd", "wt", wo2p[:, slot],
                             i_wo2[j].rearrange("(k p) c -> p k c", p=128),
                             [wbuf_guard[slot]], name=f"lo{hh}_{j}")
                glast = None
                for hb in range(2):
                    bsl = slice(hb * BAND, (hb + 1) * BAND)
                    bank = hb

                    def om(e, slot=slot, bsl=bsl, bank=bank):
                        i = None
                        for kt in range(NDT):
                            i = e.matmul(ps[bank][:], wo2p[:, slot, kt, :],
                                         abuf[:, kt, bsl],
                                         start=(kt == 0),
                                         stop=(kt == NDT - 1))
                        return i
                    omm = sc.add("tensor", om,
                                 [ldw, ldat] + bk.dep_w(bank),
                                 name=f"om{hh}_{j}_{hb}")
                    bk.wr(bank, omm)
                    glast = omm
                    hv1 = sc.add("vector", (lambda j, bsl, bank: lambda e:
                                 e.tensor_add(xtmp4[:], ps[bank][:],
                                              hid2[:, j, bsl]))(
                                     j, bsl, bank), [omm, ldxn1],
                                 name=f"h1{hh}_{j}_{hb}")
                    bk.rd(bank, hv1)
                    hv2 = sc.add("vector", (lambda j, bsl: lambda e:
                                 e.tensor_add(xbh[:, j, bsl],
                                              xbh[:, j, bsl], xtmp4[:]))(
                                     j, bsl), [hv1, wor_adds[(j, hb)]],
                                 name=f"h2{hh}_{j}_{hb}")
                    hv3 = sc.add("vector", (lambda j, bsl: lambda e:
                                 e.tensor_copy(hid2[:, j, bsl], xtmp4[:]))(
                                     j, bsl), [hv2], name=f"h3{hh}_{j}_{hb}")
                    hv3s[(j, hb)] = hv3
                wbuf_guard[slot] = glast

            h2_steps = {}
            for hb in range(2):
                bsl = slice(hb * BAND, (hb + 1) * BAND)
                mmh = []
                for kt in range(NDT):
                    s_ = sc.add("scalar", (lambda kt, bsl: lambda e:
                                e.square(sqS[:], hid2[:, kt, bsl]))(kt, bsl),
                                [hv3s[(kt, hb)],
                                 mmh[kt - 2] if kt >= 2 else prev_sexp[0]],
                                name=f"hsq{hh}_{hb}_{kt}")
                    m_ = sc.add("tensor", (lambda kt: lambda e: e.matmul(
                        ps_ra4[:], ones_b4[:], sqS[:], start=(kt == 0),
                        stop=(kt == NDT - 1)))(kt),
                        [s_] + (bk.dep_w(5) if kt == 0 else []),
                        name=f"hmm{hh}_{hb}_{kt}")
                    mmh.append(m_)
                bk.wr(5, mmh[-1])
                rhA = sc.add("scalar", (lambda hb: lambda e:
                    e.activation(rows4[0:1, hb * BAND:(hb + 1) * BAND],
                                 ps_ra4[:], AF.Ln, scale=1.0 / D,
                                 bias=ccol4[0:1, 0:1]))(hb), [mmh[-1]],
                    name=f"rhA{hh}_{hb}")
                rh = sc.add("scalar", (lambda hb: lambda e:
                    e.activation(rows4[0:1, hb * BAND:(hb + 1) * BAND],
                                 rows4[0:1, hb * BAND:(hb + 1) * BAND],
                                 AF.Exp, scale=-0.5))(hb), [rhA],
                    name=f"rh{hh}_{hb}")
                prev_sexp[0] = rh
                bk.rd(5, rh)
                zch = sc.add("vector", (lambda hb: lambda e: e.tensor_copy(
                    zrhsM4[0:1, :],
                    rows4[0:1, hb * BAND:(hb + 1) * BAND]))(hb),
                    [rh, zrhs_guard[0]], name=f"zch{hh}_{hb}")
                bch = sc.add("tensor", lambda e: e.matmul(
                    ps[3][:], ones1284[:], zrhsM4[:], start=True,
                    stop=True), [zch] + bk.dep_w(3), name=f"bch{hh}_{hb}")
                bk.wr(3, bch)
                zrhs_guard[0] = bch
                rbh = sc.add("vector", lambda e: e.tensor_copy(
                    rbS4[:], ps[3][:]), [bch], name=f"rbh{hh}_{hb}")
                bk.rd(3, rbh)
                for kt in range(NDT):
                    def fn(e, kt=kt, bsl=bsl):
                        e.tensor_mul(xtmp4b[:], hid2[:, kt, bsl], rbS4[:])
                        return e.tensor_single_scalar(
                            h2h[:, kt, bsl], xtmp4b[:], fnwC[:, kt:kt + 1],
                            op=ALU.mult)
                    h2_steps[(kt, hb)] = sc.add(
                        "vector", fn, [rbh, ld_fnw, h2h_guard[0]],
                        name=f"h2s{hh}_{hb}_{kt}")
            h2_all = list(h2_steps.values())

            gmm = []
            for band in range(2):
                def g(e, band=band):
                    i = None
                    for kt in range(NDT):
                        i = e.matmul(ps[band][0:8, :], gwb[:, kt, :],
                                     h2h[:, kt,
                                         band * BAND:(band + 1) * BAND],
                                     start=(kt == 0), stop=(kt == NDT - 1))
                    return i
                gm = sc.add("tensor", g,
                            [h2_steps[(kt, band)] for kt in range(NDT)] +
                            [ld_gw] + bk.dep_w(band), name=f"gmm{hh}_{band}")
                bk.wr(band, gm)
                gmm.append(gm)

            def lv(e):
                e.tensor_copy(logit[0:8, 0:BAND], ps[0][0:8, :])
                return e.tensor_copy(logit[0:8, BAND:2 * BAND], ps[1][0:8, :])
            lev = sc.add("vector", lv, gmm + [comb_guard[0]],
                         name=f"lev{hh}")
            bk.rd(0, lev)
            bk.rd(1, lev)

            comb_done = []
            prev_cev = None
            prev_vt = None
            prev_btr = None
            for tt in range(8):
                c0 = tt * 128
                ftr = sc.add("tensor", (lambda c0: lambda e: e.transpose(
                    ps[2][0:128, 0:8], logit[0:8, c0:c0 + 128],
                    ident_f4[0:8, 0:8]))(c0), [lev, prev_cev] + bk.dep_w(2),
                    name=f"ftr{hh}_{tt}")
                bk.wr(2, ftr)
                smax0 = sc.add("vector", lambda e: e.reduce_max(
                    gcol[:, 0:1], ps[2][0:128, 0:8], axis=AX.X),
                    [ftr, prev_sexp[0]], name=f"vmax0{hh}_{tt}")
                smax = sc.add("vector", lambda e: e.tensor_single_scalar(
                    gcol[:, 1:2], gcol[:, 0:1], -1.0, op=ALU.mult), [smax0],
                    name=f"vmax{hh}_{tt}")
                sexp = sc.add("scalar", lambda e: e.activation(
                    gsA[:], ps[2][0:128, 0:8], AF.Exp, bias=gcol[:, 1:2],
                    accum_out=gcol[:, 2:3]), [smax, prev_vt],
                    name=f"sexp{hh}_{tt}")
                prev_sexp[0] = sexp
                bk.rd(2, sexp)
                v1 = sc.add("vector", lambda e: e.reciprocal(
                    gcol[:, 2:3], gcol[:, 2:3]), [sexp, prev_btr],
                    name=f"vt1{hh}_{tt}")
                v2 = sc.add("vector", lambda e: e.tensor_single_scalar(
                    gsA[:], gsA[:], gcol[:, 2:3], op=ALU.mult), [v1],
                    name=f"vt2{hh}_{tt}")
                v3 = sc.add("vector", lambda e: e.reduce_max(
                    gcol[:, 0:1], gsA[:], axis=AX.X), [v2],
                    name=f"vt3{hh}_{tt}")
                v4 = sc.add("vector", lambda e: e.tensor_single_scalar(
                    gsB[:], gsA[:], gcol[:, 0:1], op=ALU.is_equal), [v3],
                    name=f"vt4{hh}_{tt}")
                v5 = sc.add("vector", lambda e: e.scalar_tensor_tensor(
                    gsB[:], gsB[:], NEG_BIG, gsA[:], op0=ALU.mult,
                    op1=ALU.add), [v4], name=f"vt5{hh}_{tt}")
                v6 = sc.add("vector", lambda e: e.reduce_max(
                    gcol[:, 3:4], gsB[:], axis=AX.X), [v5],
                    name=f"vt6{hh}_{tt}")
                v7 = sc.add("vector", lambda e: e.tensor_single_scalar(
                    gsB[:], gsA[:], gcol[:, 3:4], op=ALU.is_ge), [v6],
                    name=f"vt7{hh}_{tt}")
                v8 = sc.add("vector", lambda e: e.tensor_mul(
                    gsB[:], gsB[:], gsA[:]), [v7], name=f"vt8{hh}_{tt}")
                v9 = sc.add("vector", lambda e: e.reduce_sum(
                    gcol[:, 4:5], gsB[:], axis=AX.X), [v8],
                    name=f"vt9{hh}_{tt}")
                v10 = sc.add("vector", lambda e: e.tensor_single_scalar(
                    gcol[:, 4:5], gcol[:, 4:5], 1e-9, op=ALU.add), [v9],
                    name=f"vtA{hh}_{tt}")
                v11 = sc.add("vector", lambda e: e.reciprocal(
                    gcol[:, 4:5], gcol[:, 4:5]), [v10], name=f"vtB{hh}_{tt}")
                vt = sc.add("vector", lambda e: e.tensor_single_scalar(
                    gsB[:], gsB[:], gcol[:, 4:5], op=ALU.mult), [v11],
                    name=f"vtop{hh}_{tt}")
                prev_vt = vt
                btr = sc.add("tensor", lambda e: e.transpose(
                    ps[3][0:8, 0:128], gsB[:], ident_f4[:]),
                    [vt] + bk.dep_w(3), name=f"btr{hh}_{tt}")
                bk.wr(3, btr)
                prev_btr = btr
                cev = sc.add("vector", (lambda c0: lambda e: e.tensor_copy(
                    combT[0:8, c0:c0 + 128], ps[3][0:8, 0:128]))(c0), [btr],
                    name=f"cev{hh}_{tt}")
                bk.rd(3, cev)
                prev_cev = cev
                comb_done.append(cev)
            st_c = sc.dma("sync", "st",
                          d_comb4.rearrange("o (e t) -> (o e) t", e=8),
                          combT[:], comb_done, name=f"stc{hh}")
            prev_cbm = [None]

            x_add_steps = []
            gS_guard = {0: None, 1: None}
            for e_idx in range(8):
                hfs = [list(range(0, 6)), list(range(6, 11))]
                wg_rr = i_wg2[e_idx].rearrange("(k p) h -> p k h", p=128)
                wu_rr = i_wu2[e_idx].rearrange("(k p) h -> p k h", p=128)
                wd_rr = i_wd2[e_idx].rearrange("(j p) d -> p j d", p=128)

                cb = []
                for band in range(2):
                    srcap = d_comb4[0:1, e_idx * 1024 + band * BAND:
                                    e_idx * 1024 + (band + 1) * BAND]
                    cp = sc.dma("sync", "ld", zrhsM4[0:1, :], srcap,
                                [st_c, prev_cbm[0], zrhs_guard[0]],
                                name=f"cbc{hh}_{e_idx}_{band}")
                    mm = sc.add("tensor", lambda e: e.matmul(
                        ps[4][:], ones1284[:], zrhsM4[:], start=True,
                        stop=True), [cp] + bk.dep_w(4),
                        name=f"cbm{hh}_{e_idx}_{band}")
                    prev_cbm[0] = mm
                    zrhs_guard[0] = mm
                    bk.wr(4, mm)
                    ev = sc.add("vector", (lambda band: lambda e:
                                e.tensor_copy(combB[:, band, :], ps[4][:])
                                )(band), [mm],
                                name=f"cbe{hh}_{e_idx}_{band}")
                    bk.rd(4, ev)
                    cb.append(ev)

                prev_ld = {0: wbuf_guard[0], 1: wbuf_guard[1]}
                for hf_i, jl in enumerate(hfs):
                    gS_ready = {}
                    for jj, j in enumerate(jl):
                        slot = j % 2
                        ldg = sc.dma("gpsimd", "wt", wga[:, slot],
                                     wg_rr[:, :, j * 128:(j + 1) * 128],
                                     [prev_ld[slot]],
                                     name=f"lg{hh}_{e_idx}_{j}")
                        ldu = sc.dma("gpsimd", "wt", wub[:, slot],
                                     wu_rr[:, :, j * 128:(j + 1) * 128],
                                     [prev_ld[slot]],
                                     name=f"lu{hh}_{e_idx}_{j}")
                        glast = None
                        for band in range(2):
                            bg = band * 2
                            bu = band * 2 + 1

                            def gg(e, slot=slot, bg=bg, band=band):
                                i = None
                                for kt in range(NDT):
                                    i = e.matmul(
                                        ps[bg][:], wga[:, slot, kt, :],
                                        h2h[:, kt,
                                            band * BAND:(band + 1) * BAND],
                                        start=(kt == 0), stop=(kt == NDT - 1))
                                return i

                            def gu(e, slot=slot, bu=bu, band=band):
                                i = None
                                for kt in range(NDT):
                                    i = e.matmul(
                                        ps[bu][:], wub[:, slot, kt, :],
                                        h2h[:, kt,
                                            band * BAND:(band + 1) * BAND],
                                        start=(kt == 0), stop=(kt == NDT - 1))
                                return i
                            depg = [ldg] + h2_all + bk.dep_w(bg)
                            depu = [ldu] + bk.dep_w(bu)
                            g1 = sc.add("tensor", gg, depg,
                                        name=f"g{hh}_{e_idx}_{j}_{band}")
                            bk.wr(bg, g1)
                            g2 = sc.add("tensor", gu, depu,
                                        name=f"u{hh}_{e_idx}_{j}_{band}")
                            bk.wr(bu, g2)
                            glast = g2
                            sil = sc.add("scalar", (lambda bg: lambda e:
                                         e.activation(sgb[:], ps[bg][:],
                                                      AF.Silu))(bg),
                                         [g1, prev_gev[0], ldat],
                                         name=f"sil{hh}_{e_idx}_{j}_{band}")
                            bk.rd(bg, sil)

                            def gev(e, jj=jj, band=band, bu=bu):
                                e.tensor_mul(gtmp[:], sgb[:], ps[bu][:])
                                return e.tensor_mul(
                                    gS[:, jj, band * BAND:(band + 1) * BAND],
                                    gtmp[:], combB[:, band, :])
                            gevs = sc.add("vector", gev,
                                          [sil, g2, gS_guard[band]] + cb,
                                          name=f"ge{hh}_{e_idx}_{j}_{band}")
                            bk.rd(bu, gevs)
                            prev_gev[0] = gevs
                            gS_ready[(jj, band)] = gevs
                        prev_ld[slot] = glast
                        wbuf_guard[slot] = glast

                    jn = len(jl)
                    prev_wd = {0: wdb_guard[0], 1: wdb_guard[1]}
                    for dt in range(NDT):
                        slot = dt % 2
                        lw = sc.dma(
                            "gpsimd", "wt", wdb[:, slot, 0:jn, :],
                            wd_rr[:, hf_i * 6:hf_i * 6 + jn,
                                  dt * 128:(dt + 1) * 128],
                            [prev_wd[slot]],
                            name=f"lw{hh}_{e_idx}_{hf_i}_{dt}")
                        for band in range(2):
                            bank = band

                            def dn(e, slot=slot, jn=jn, band=band,
                                   bank=bank):
                                i = None
                                for jj in range(jn):
                                    i = e.matmul(
                                        ps[bank][:], wdb[:, slot, jj, :],
                                        gS[:, jj,
                                           band * BAND:(band + 1) * BAND],
                                        start=(jj == 0), stop=(jj == jn - 1))
                                return i
                            ddeps = [lw] + bk.dep_w(bank) + [
                                gS_ready[(jj, band)] for jj in range(jn)]
                            dg = sc.add("tensor", dn, ddeps,
                                        name=f"d{hh}_{e_idx}_{hf_i}_{dt}_{band}")
                            bk.wr(bank, dg)
                            if dt >= NDT - 2:
                                gS_guard[band] = dg

                            def xadd(e, dt=dt, band=band, bank=bank):
                                return e.tensor_add(
                                    xbh[:, dt,
                                        band * BAND:(band + 1) * BAND],
                                    xbh[:, dt,
                                        band * BAND:(band + 1) * BAND],
                                    ps[bank][:])
                            xa = sc.add("vector", xadd, [dg],
                                        name=f"xa{hh}_{e_idx}_{hf_i}_{dt}_{band}")
                            bk.rd(bank, xa)
                            x_add_steps.append(xa)
                        prev_wd[slot] = dg
                        wdb_guard[slot] = dg

            stx = sc.dma("sync", "st", o_x_r[:, :, hcols], xbh[:],
                         x_add_steps, name=f"stx{hh}")
            st_steps.append(stx)
            xbh_guard[0] = stx
            hid2_guard[0] = h2_all[-1]
            h2h_guard[0] = x_add_steps[-1]
            comb_guard[0] = prev_cbm[0]

        fin4 = sc.add("vector", lambda e: e.engine_nop(), st_steps,
                      name="final4")
        sc.emit(nc, sems)
    return nc



def _prep_launch3_inputs(inputs, l2_results):
    td_c = np.ascontiguousarray(
        np.asarray(inputs["L1_attn_time_decay"], np.float32).reshape(64, 1))
    kf = {c: np.asarray(l2_results[c]["kf_o"]) for c in range(NCORE)}
    vf = {c: np.asarray(l2_results[c]["vf_o"]) for c in range(NCORE)}
    ksq = {c: np.asarray(l2_results[c]["ksq_o"]) for c in range(NCORE)}
    sloc = {c: np.asarray(l2_results[c]["sloc_o"]) for c in range(NCORE)}
    bf = kf[0].dtype
    in_maps = []
    for c in range(NCORE):
        b, r = c // 4, c % 4
        kf_all = np.zeros((4, 128, TSEQ), bf)
        vf_all = np.zeros((4, TSEQ, 128), bf)
        ksq_all = np.zeros((4, TSEQ), np.float32)
        for g in range(NBANDS):
            oc = 4 * b + g % 4
            lb = g // 4
            gs = slice(g * BAND, (g + 1) * BAND)
            ls = slice(lb * BAND, (lb + 1) * BAND)
            for kvh in range(4):
                kf_all[kvh, :, gs] = kf[oc][kvh * 128:(kvh + 1) * 128, ls]
                vf_all[kvh, gs, :] = vf[oc][kvh, ls, :]
                ksq_all[kvh, gs] = ksq[oc][kvh, ls]
        sfin = np.zeros((LBANDS, 64, 4), np.float32)
        for l in range(LBANDS):
            g = 4 * l + r
            for s in range(4):
                gi = g - 1 - s
                if gi >= 0:
                    oc = 4 * b + gi % 4
                    lb = gi // 4
                    sfin[l, :, s] = sloc[oc][:, lb * BAND + BAND - 1]
        bpos = np.zeros((128, LBANDS), np.float32)
        for l in range(LBANDS):
            bpos[:, l] = 512.0 * (4 * l + r)
        in_maps.append({
            "q_t": np.asarray(l2_results[c]["q_o"]),
            "kf_all": kf_all, "vf_all": vf_all,
            "ksq_all": np.ascontiguousarray(ksq_all.reshape(4, 128, 64)),
            "sloc_t": sloc[c], "sfin_in": sfin, "td_c": td_c,
            "bpos_c": bpos,
        })
    return in_maps


def run_launch3(inputs, l2_results):
    nc = build_launch3()
    in_maps = _prep_launch3_inputs(inputs, l2_results)
    res = run_bass_kernel_spmd(nc, in_maps, core_ids=list(range(NCORE)))
    return res.results


def _prep_launch4_inputs(inputs, l3_results, l2_results):
    wor_t = np.ascontiguousarray(
        np.asarray(inputs["L1_attn_W_output"], np.float32).T)
    ow = np.asarray(inputs["L1_ffn_o_w"], np.float32)
    wo2_t = np.ascontiguousarray(
        ow.T.reshape(D, NDT, 128).transpose(1, 0, 2))
    fnw_c = np.ascontiguousarray(
        np.asarray(inputs["L1_ffn_ffn_norm_w"], np.float32).reshape(
            NDT, 128).T)
    gw2_t = np.ascontiguousarray(
        np.asarray(inputs["L1_ffn_gate_w"], np.float32).T)
    wg2 = np.ascontiguousarray(
        np.asarray(inputs["L1_ffn_wg"], np.float32).transpose(0, 2, 1))
    wu2 = np.ascontiguousarray(
        np.asarray(inputs["L1_ffn_wu"], np.float32).transpose(0, 2, 1))
    wd2 = np.ascontiguousarray(
        np.asarray(inputs["L1_ffn_wd"], np.float32).transpose(0, 2, 1))
    in_maps = []
    for c in range(NCORE):
        in_maps.append({
            "x1_t": np.asarray(l2_results[c]["x1_o"]),
            "xn1_t": np.asarray(l2_results[c]["xn1_o"]),
            "attn_t": np.asarray(l3_results[c]["attn_o"]),
            "states_t": np.asarray(l3_results[c]["states_o"]),
            "wor_t": wor_t, "wo2_t": wo2_t, "fnw_c": fnw_c,
            "gw2_t": gw2_t, "wg2": wg2, "wu2": wu2, "wd2": wd2,
        })
    return in_maps


def run_launch4(inputs, l3_results, l2_results):
    nc = build_launch4()
    in_maps = _prep_launch4_inputs(inputs, l3_results, l2_results)
    res = run_bass_kernel_spmd(nc, in_maps, core_ids=list(range(NCORE)))
    return res.results


def kernel(**inputs):
    l1 = run_launch1(inputs)
    l2 = run_launch2(inputs, l1)
    l3 = run_launch3(inputs, l2)
    l4 = run_launch4(inputs, l3, l2)
    out = np.zeros((BATCH, TSEQ, D), np.float32)
    for c in range(NCORE):
        b, r = c // 4, c % 4
        xo = np.asarray(l4[c]["xo_t"], np.float32)
        for l in range(LBANDS):
            g = 4 * l + r
            out[b, g * BAND:(g + 1) * BAND, :] = \
                xo[:, l * BAND:(l + 1) * BAND].T
    return out


# version 03 (final)
# speedup vs baseline: 1.2618x; 1.2618x over previous
import numpy as np
from contextlib import ExitStack

import concourse.bass as bass
import concourse.mybir as mybir
from concourse.bass_utils import run_bass_kernel_spmd

F32 = mybir.dt.float32
BF16 = mybir.dt.bfloat16
I32 = mybir.dt.int32
AF = mybir.ActivationFunctionType
ALU = mybir.AluOpType
AX = mybir.AxisListType

NCORE = 8
D = 2048
TSEQ = 8192
BATCH = 2
BAND = 512
NBANDS = 16
LBANDS = 4
LT = LBANDS * BAND
NDT = D // 128
H0 = 16
BLOCK = 256
NEG_BIG = -1.0e30
LN_EPS = 1e-5
RMS_EPS = 1e-6

ENGS = ("sync", "gpsimd", "scalar", "vector", "tensor")
SEMS = ("ld", "wt", "st", "t", "v", "s", "g")
DMA_SEMS = {"ld", "wt", "st"}
DEFAULT_SEM = {"sync": "ld", "gpsimd": "g", "scalar": "s", "vector": "v",
               "tensor": "t"}


class Step:
    __slots__ = ("eng", "fn", "deps", "sem", "inc", "val", "name")

    def __init__(self, eng, fn, deps, sem, inc, name):
        self.eng = eng
        self.fn = fn
        self.deps = deps
        self.sem = sem
        self.inc = inc
        self.val = None
        self.name = name


class Sched:
    def __init__(self):
        self.steps = []
        self.counts = {s: 0 for s in SEMS}

    def add(self, eng, fn, deps=(), sem=None, name=""):
        if sem is None:
            sem = DEFAULT_SEM[eng]
        inc = 16 if sem in DMA_SEMS else 1
        dd = []
        for d in deps:
            if d is None:
                continue
            if isinstance(d, (list, tuple)):
                dd.extend(x for x in d if x is not None)
            else:
                dd.append(d)
        st = Step(eng, fn, dd, sem, inc, name)
        self.counts[sem] += inc
        st.val = self.counts[sem]
        self.steps.append(st)
        return st

    def dma(self, eng, sem, out, in_, deps=(), name=""):
        return self.add(eng, lambda e: e.dma_start(out=out, in_=in_), deps,
                        sem=sem, name=name)

    def emit(self, nc, sems):
        per_eng = {en: [s for s in self.steps if s.eng == en] for en in ENGS}
        with nc.Block() as block:
            for en in ENGS:
                def make(en):
                    def body(raw):
                        guaranteed = {s: 0 for s in SEMS}
                        for st in per_eng[en]:
                            for d in st.deps:
                                if (d.eng == en == "tensor"
                                        and d.sem not in DMA_SEMS):
                                    continue
                                if d.val <= guaranteed[d.sem]:
                                    continue
                                raw.wait_ge(sems[d.sem], d.val)
                                guaranteed[d.sem] = d.val
                            inst = st.fn(raw)
                            assert inst is not None, f"step {st.name}"
                            inst.then_inc(sems[st.sem], st.inc)
                    return body
                getattr(block, en)(make(en))


class Banks:

    def __init__(self, n=16):
        self.last_reader = {}
        self.last_writer = {}

    def rd(self, bank, step):
        self.last_reader[bank] = step

    def wr(self, bank, step):
        self.last_writer[bank] = step

    def dep_w(self, bank):
        return [self.last_reader.get(bank), self.last_writer.get(bank)]

    def dep_r(self, bank):
        return [self.last_writer.get(bank)]



def build_launch1():
    nc = bass.Bass("TRN2", target_bir_lowering=False, debug=False)

    i_x = nc.dram_tensor("x_t", [D, LT], F32, kind="ExternalInput").ap()
    i_wk = nc.dram_tensor("wk_t", [D, D], F32, kind="ExternalInput").ap()
    i_wv = nc.dram_tensor("wv_t", [D, D], F32, kind="ExternalInput").ap()
    i_slope = nc.dram_tensor("slope", [1, H0], F32, kind="ExternalInput").ap()
    i_gatew = nc.dram_tensor("gate_w", [D, 8], F32, kind="ExternalInput").ap()
    i_wg = nc.dram_tensor("wg", [8, D, 1408], F32, kind="ExternalInput").ap()
    i_wu = nc.dram_tensor("wu", [8, D, 1408], F32, kind="ExternalInput").ap()
    i_wd = nc.dram_tensor("wd", [8, 1408, D], F32, kind="ExternalInput").ap()
    i_swg = nc.dram_tensor("swg", [D, 2816], F32, kind="ExternalInput").ap()
    i_swu = nc.dram_tensor("swu", [D, 2816], F32, kind="ExternalInput").ap()
    i_swd = nc.dram_tensor("swd", [2816, D], F32, kind="ExternalInput").ap()

    o_xn = nc.dram_tensor("xn0_o", [D, LT], BF16, kind="ExternalOutput").ap()
    o_k = nc.dram_tensor("k_o", [D, LT], BF16, kind="ExternalOutput").ap()
    o_v = nc.dram_tensor("v_o", [D, LT], BF16, kind="ExternalOutput").ap()
    o_x1p = nc.dram_tensor("x1p_o", [D, LT], F32, kind="ExternalOutput").ap()
    o_sum = nc.dram_tensor("sum_o", [128, H0, LBANDS, 128], F32,
                           kind="ExternalOutput").ap()
    d_comb = nc.dram_tensor("comb_d", [1, 8192], F32, kind="Internal").ap()

    x_r = i_x.rearrange("(k p) t -> p k t", p=128)
    wk_r = i_wk.rearrange("(k p) e -> p k e", p=128)
    wv_r = i_wv.rearrange("(k p) e -> p k e", p=128)
    gw_r = i_gatew.rearrange("(k p) e -> p k e", p=128)
    o_xn_r = o_xn.rearrange("(k p) t -> p k t", p=128)
    o_k_r = o_k.rearrange("(k p) t -> p k t", p=128)
    o_v_r = o_v.rearrange("(k p) t -> p k t", p=128)
    o_x1p_r = o_x1p.rearrange("(k p) t -> p k t", p=128)

    sc = Sched()
    bk = Banks()

    with ExitStack() as stk:
        ec = stk.enter_context
        xb = ec(nc.sbuf_tensor("xb", [128, NDT, 1024], F32))
        xnb = ec(nc.sbuf_tensor("xnb", [128, NDT, 1024], BF16))
        regA = ec(nc.sbuf_tensor("regA", [128, 16384], BF16))
        kT = regA[:, 0:8192].rearrange("p (h t) -> p h t", h=H0)
        vT = regA[:, 8192:16384].rearrange("p (h t) -> p h t", h=H0)
        wdb = regA[:, 0:1536].rearrange("p (s j t) -> p s j t", s=2, j=6)
        gS = regA[:, 4096:16384].rearrange("p (j t) -> p j t", j=12)
        wga = ec(nc.sbuf_tensor("wga", [128, 2, NDT, 128], BF16))
        wub = ec(nc.sbuf_tensor("wub", [128, 2, NDT, 128], BF16))
        gwb = ec(nc.sbuf_tensor("gwb", [128, NDT, 8], BF16))
        xsq = ec(nc.sbuf_tensor("xsq", [128, 2, BAND], F32))
        nat8 = ec(nc.sbuf_tensor("nat8", [128, 8, 128], BF16))
        sums = ec(nc.sbuf_tensor("sums", [128, H0, 128], F32))
        sumt = ec(nc.sbuf_tensor("sumt", [128, 128], F32))
        combB = ec(nc.sbuf_tensor("combB", [128, 2, BAND], F32))
        sgb = ec(nc.sbuf_tensor("sgb", [128, BAND], BF16))
        gtmp = ec(nc.sbuf_tensor("gtmp", [128, BAND], BF16))
        ktmp = ec(nc.sbuf_tensor("ktmp", [128, 2, BLOCK], BF16))
        kdec = ec(nc.sbuf_tensor("kdec", [128, H0, BLOCK], BF16))
        ones_f = ec(nc.sbuf_tensor("ones_f", [128, 1], F32))
        ones128 = ec(nc.sbuf_tensor("ones128", [128, 128], F32))
        ident_bf = ec(nc.sbuf_tensor("ident_bf", [128, 128], BF16))
        ident_f = ec(nc.sbuf_tensor("ident_f", [128, 128], F32))
        iota_pf = ec(nc.sbuf_tensor("iota_pf", [128, 128], F32))
        iota255 = ec(nc.sbuf_tensor("iota255", [128, BLOCK], F32))
        scols = ec(nc.sbuf_tensor("scols", [128, H0], F32))
        bdcol = ec(nc.sbuf_tensor("bdcol", [128, H0], F32))
        zrhs = ec(nc.sbuf_tensor("zrhs", [128, 2, BAND], F32))
        zrhsM = ec(nc.sbuf_tensor("zrhsM", [128, BAND], F32))
        ccol = ec(nc.sbuf_tensor("ccol", [128, 4], F32))
        slp = ec(nc.sbuf_tensor("slp", [1, H0], F32))
        rows = ec(nc.sbuf_tensor("rows", [1, 4 * BAND], F32))
        gsA = ec(nc.sbuf_tensor("gsA", [128, 8], F32))
        gsB = ec(nc.sbuf_tensor("gsB", [128, 8], F32))
        gcol = ec(nc.sbuf_tensor("gcol", [128, 4], F32))
        logit = ec(nc.sbuf_tensor("logit", [8, 1024], F32))
        combT = ec(nc.sbuf_tensor("combT", [8, 1024], F32))
        xtmp = ec(nc.sbuf_tensor("xtmp", [128, BAND], F32))

        ps = [ec(nc.psum_tensor(f"ps{i}", [128, BAND], F32)) for i in range(5)]
        ps_bf = ec(nc.psum_tensor("ps_bf", [128, 1024], BF16))
        ps_ra = ec(nc.psum_tensor("ps_ra", [1, BAND], F32))
        ps_rb = ec(nc.psum_tensor("ps_rb", [1, BAND], F32))

        sems = {}
        for n in SEMS:
            sems[n] = ec(nc.semaphore(n))

        ld_slope = sc.dma("sync", "ld", slp[:], i_slope, name="ld_slope")
        ld_gw = sc.dma("gpsimd", "wt", gwb[:], gw_r, name="ld_gw")

        def fn_iota(e):
            e.iota(iota_pf[:], pattern=[[-1, 128]], base=0,
                   channel_multiplier=1, allow_small_or_imprecise_dtypes=True)
            return e.iota(iota255[:], pattern=[[1, BLOCK]], base=-255,
                          channel_multiplier=0,
                          allow_small_or_imprecise_dtypes=True)
        s_iota = sc.add("gpsimd", fn_iota, name="iota")

        def fn_consts(e):
            e.memset(ones_f[:], 1.0)
            e.memset(ones128[:], 1.0)
            e.memset(zrhs[:], 0.0)
            e.memset(zrhsM[:], 0.0)
            e.memset(ccol[:, 0:1], LN_EPS)
            e.memset(ccol[:, 1:2], RMS_EPS)
            e.tensor_single_scalar(ident_bf[:], iota_pf[:], 0.0,
                                   op=ALU.is_equal)
            e.tensor_single_scalar(ident_f[:], iota_pf[:], 0.0,
                                   op=ALU.is_equal)
            return e.tensor_copy(zrhs[0:1, 0, 0:H0], slp[:])
        s_consts = sc.add("vector", fn_consts, [s_iota, ld_slope],
                          name="consts")

        s_scol_mm = sc.add(
            "tensor", lambda e: e.matmul(ps[0][:, 0:H0], ones128[:],
                                         zrhs[:, 0, 0:H0], start=True,
                                         stop=True), [s_consts], name="scolmm")
        bk.wr(0, s_scol_mm)

        def fn_scols(e):
            e.tensor_copy(scols[:], ps[0][:, 0:H0])
            return e.memset(zrhs[0:1, 0, 0:H0], 0.0)
        s_scols = sc.add("vector", fn_scols, [s_scol_mm], name="scols")
        bk.rd(0, s_scols)

        def fn_tables(e):
            e.activation(bdcol[:], scols[:], AF.Exp, scale=-256.0)
            i = None
            for h in range(H0):
                i = e.activation(kdec[:, h, :], iota255[:], AF.Exp,
                                 scale=scols[:, h:h + 1])
            return i
        s_tables = sc.add("scalar", fn_tables, [s_scols], name="tables")

        regA_readers = [s_tables]
        wbuf_guard = {0: None, 1: None}
        xb_guard = {0: [], 1: []}
        prev_sum_store = [None]
        prev_ln = {"sq": [None, None], "rstd": None, "bc": None}

        def band_phase(l):
            off = (l % 2) * BAND
            cols = slice(l * BAND, (l + 1) * BAND)
            ldx = sc.dma("sync", "ld", xb[:, :, off:off + BAND],
                         x_r[:, :, cols], xb_guard[l % 2], name=f"ldx{l}")
            xb_guard[l % 2] = []

            xbt = lambda kt: xb[:, kt, off:off + BAND]
            xnt = lambda kt: xnb[:, kt, off:off + BAND]
            sq = []
            mm2 = []
            for kt in range(NDT):
                deps = [ldx]
                if kt >= 2:
                    deps.append(mm2[kt - 2])
                elif prev_ln["sq"][kt] is not None:
                    deps.append(prev_ln["sq"][kt])
                s = sc.add("scalar", (lambda kt: lambda e: e.square(
                    xsq[:, kt % 2, :], xbt(kt)))(kt), deps, name=f"sq{l}_{kt}")
                sq.append(s)
                m = sc.add("tensor", (lambda kt: lambda e: e.matmul(
                    ps_rb[:], ones_f[:], xsq[:, kt % 2, :], start=(kt == 0),
                    stop=(kt == NDT - 1)))(kt),
                    [s] + (bk.dep_w(7) if kt == 0 else []),
                    name=f"sx2_{l}_{kt}")
                mm2.append(m)
            prev_ln["sq"] = [mm2[-2], mm2[-1]]
            bk.wr(7, mm2[-1])

            def mk_sx(e):
                i = None
                for kt in range(NDT):
                    i = e.matmul(ps_ra[:], ones_f[:], xbt(kt),
                                 start=(kt == 0), stop=(kt == NDT - 1))
                return i
            mm1 = sc.add("tensor", mk_sx, [ldx] + bk.dep_w(6), name=f"sx_{l}")
            bk.wr(6, mm1)

            mrow = rows[0:1, 0:BAND]
            vrow = rows[0:1, BAND:2 * BAND]
            rrow = rows[0:1, 2 * BAND:3 * BAND]
            mrr = rows[0:1, 3 * BAND:4 * BAND]

            def st_stats(e):
                e.tensor_single_scalar(mrow, ps_ra[:], 1.0 / D, op=ALU.mult)
                e.tensor_single_scalar(vrow, ps_rb[:], 1.0 / D, op=ALU.mult)
                e.tensor_mul(mrr, mrow, mrow)
                return e.tensor_sub(vrow, vrow, mrr)
            s_stats = sc.add("vector", st_stats,
                             [mm1, mm2[-1], prev_ln["rstd"]],
                             name=f"stats{l}")
            bk.rd(6, s_stats)
            bk.rd(7, s_stats)

            s_rstdA = sc.add("scalar", lambda e: e.activation(
                rrow, vrow, AF.Ln, bias=ccol[0:1, 0:1]), [s_stats],
                name=f"rstdA{l}")
            s_rstd = sc.add("scalar", lambda e: e.activation(
                rrow, rrow, AF.Exp, scale=-0.5), [s_rstdA], name=f"rstd{l}")
            prev_ln["rstd"] = s_rstd

            def st_rows(e):
                e.scalar_tensor_tensor(mrr, mrow, -1.0, rrow, op0=ALU.mult,
                                       op1=ALU.mult)
                e.tensor_copy(zrhs[0:1, 0, :], rrow)
                return e.tensor_copy(zrhs[0:1, 1, :], mrr)
            s_rows = sc.add("vector", st_rows, [s_rstd, prev_ln["bc"]],
                            name=f"lnrows{l}")

            def st_bc(e):
                e.matmul(ps[3][:], ones128[:], zrhs[:, 0, :], start=True,
                         stop=True)
                return e.matmul(ps[4][:], ones128[:], zrhs[:, 1, :],
                                start=True, stop=True)
            s_bc = sc.add("tensor", st_bc,
                          [s_rows] + bk.dep_w(3) + bk.dep_w(4),
                          name=f"lnbc{l}")
            bk.wr(3, s_bc)
            bk.wr(4, s_bc)
            prev_ln["bc"] = s_bc

            xn_steps = []
            for kt in range(NDT):
                def fn(e, kt=kt):
                    e.tensor_mul(xtmp[:], xbt(kt), ps[3][:])
                    return e.tensor_add(xnt(kt), xtmp[:], ps[4][:])
                xn_steps.append(sc.add("vector", fn, [s_bc],
                                       name=f"xn{l}_{kt}"))
            bk.rd(3, xn_steps[-1])
            bk.rd(4, xn_steps[-1])
            sc.dma("sync", "st", o_xn_r[:, :, cols],
                   xnb[:, :, off:off + BAND], xn_steps, name=f"st_xn{l}")

            kv_evac = []
            prev_group = {0: wbuf_guard[0], 1: wbuf_guard[1]}
            for j in range(2 * NDT):
                src = wk_r if j < NDT else wv_r
                jj = j % NDT
                slot = j % 2
                ldeps = [prev_group[slot]]
                if j < 2:
                    ldeps += regA_readers
                ldw = sc.dma("gpsimd", "wt", wga[:, slot],
                             src[:, :, jj * 128:(jj + 1) * 128], ldeps,
                             name=f"ldw{l}_{j}")

                def grp(e, slot=slot):
                    i = None
                    for kt in range(NDT):
                        i = e.matmul(ps[slot][:], wga[:, slot, kt, :],
                                     xnb[:, kt, off:off + BAND],
                                     start=(kt == 0), stop=(kt == NDT - 1))
                    return i
                g = sc.add("tensor", grp,
                           [ldw] + xn_steps + bk.dep_w(slot),
                           name=f"kvp{l}_{j}")
                prev_group[slot] = g
                bk.wr(slot, g)
                if j < NDT:
                    ev = sc.add("scalar", (lambda j, slot: lambda e:
                                e.activation(kT[:, j, :], ps[slot][:],
                                             AF.Silu))(j, slot), [g],
                                name=f"kev{l}_{j}")
                else:
                    ev = sc.add("scalar", (lambda j, slot: lambda e:
                                e.copy(vT[:, j - NDT, :], ps[slot][:])
                                )(j, slot), [g], name=f"vev{l}_{j}")
                bk.rd(slot, ev)
                kv_evac.append(ev)
                wbuf_guard[slot] = g
            st_k = sc.dma("sync", "st", o_k_r[:, :, cols], regA[:, 0:8192],
                          kv_evac[:NDT], name=f"st_k{l}")
            st_v = sc.dma("sync", "st", o_v_r[:, :, cols],
                          regA[:, 8192:16384], kv_evac[NDT:], name=f"st_v{l}")

            prev_cp1 = None
            sum_steps = []
            for h in range(H0):
                km0 = sc.add("vector", (lambda h: lambda e: e.tensor_mul(
                    ktmp[:, 0, :], kT[:, h, 0:BLOCK], kdec[:, h, :]))(h),
                    [kv_evac[h], s_tables], name=f"km0_{l}_{h}")
                km1 = sc.add("vector", (lambda h: lambda e: e.tensor_mul(
                    ktmp[:, 1, :], kT[:, h, BLOCK:2 * BLOCK], kdec[:, h, :])
                    )(h), [kv_evac[h], s_tables], name=f"km1_{l}_{h}")

                def tr0(e, h=h):
                    e.transpose(ps_bf[:, 0:128], ktmp[:, 0, 0:128],
                                ident_bf[:])
                    e.transpose(ps_bf[:, 128:256], ktmp[:, 0, 128:256],
                                ident_bf[:])
                    e.transpose(ps_bf[:, 256:384], vT[:, h, 0:128],
                                ident_bf[:])
                    return e.transpose(ps_bf[:, 384:512], vT[:, h, 128:256],
                                       ident_bf[:])
                t0s = sc.add("tensor", tr0,
                             [km0, kv_evac[NDT + h]] + bk.dep_w(5),
                             name=f"tr0_{l}_{h}")
                cp0 = sc.add("vector", lambda e: e.tensor_copy(
                    regA_nat(0), ps_bf[:, 0:512]), [t0s], name=f"cp0_{l}_{h}")

                def tr1(e, h=h):
                    e.transpose(ps_bf[:, 512:640], ktmp[:, 1, 0:128],
                                ident_bf[:])
                    e.transpose(ps_bf[:, 640:768], ktmp[:, 1, 128:256],
                                ident_bf[:])
                    e.transpose(ps_bf[:, 768:896],
                                vT[:, h, BLOCK:BLOCK + 128], ident_bf[:])
                    return e.transpose(ps_bf[:, 896:1024],
                                       vT[:, h, BLOCK + 128:2 * BLOCK],
                                       ident_bf[:])
                t1s = sc.add("tensor", tr1, [km1, cp0], name=f"tr1_{l}_{h}")
                cp1 = sc.add("vector", lambda e: e.tensor_copy(
                    regA_nat(1), ps_bf[:, 512:1024]), [t1s],
                    name=f"cp1_{l}_{h}")
                bk.wr(5, t1s)
                bk.rd(5, cp1)
                prev_cp1 = cp1

                def mmA(e):
                    e.matmul(ps[2][:, 0:128], nat8[:, 0, :], nat8[:, 2, :],
                             start=True, stop=False)
                    return e.matmul(ps[2][:, 0:128], nat8[:, 1, :],
                                    nat8[:, 3, :], start=False, stop=True)

                def mmB(e):
                    e.matmul(ps[3][:, 0:128], nat8[:, 4, :], nat8[:, 6, :],
                             start=True, stop=False)
                    return e.matmul(ps[3][:, 0:128], nat8[:, 5, :],
                                    nat8[:, 7, :], start=False, stop=True)
                a1 = sc.add("tensor", mmA, [cp0, cp1] + bk.dep_w(2),
                            name=f"A1_{l}_{h}")
                a2 = sc.add("tensor", mmB, [cp1] + bk.dep_w(3),
                            name=f"A2_{l}_{h}")
                bk.wr(2, a1)
                bk.wr(3, a2)

                def ev(e, h=h):
                    e.tensor_copy(sumt[:], ps[2][:, 0:128])
                    return e.scalar_tensor_tensor(
                        sums[:, h, :], sumt[:], bdcol[:, h:h + 1],
                        ps[3][:, 0:128], op0=ALU.mult, op1=ALU.add)
                sdeps = [a1, a2]
                if h == 0 and prev_sum_store[0] is not None:
                    sdeps.append(prev_sum_store[0])
                sev = sc.add("vector", ev, sdeps, name=f"sum_{l}_{h}")
                bk.rd(2, sev)
                bk.rd(3, sev)
                sum_steps.append(sev)
            stsum = sc.dma("sync", "st", o_sum[:, :, l, :],
                           sums.rearrange("p h t -> p (h t)"), sum_steps,
                           name=f"st_sum{l}")
            prev_sum_store[0] = stsum
            regA_readers[:] = sum_steps + [st_k, st_v]
            return xn_steps, ldx

        def regA_nat(half):
            v = nat8.rearrange("p j t -> p (j t)")
            return v[:, half * 512:(half + 1) * 512]

        def moe_phase(hh, xn_all):
            gmm = []
            for band in range(2):
                def g(e, band=band):
                    i = None
                    for kt in range(NDT):
                        i = e.matmul(ps[band][0:8, :], gwb[:, kt, :],
                                     xnb[:, kt, band * BAND:(band + 1) * BAND],
                                     start=(kt == 0), stop=(kt == NDT - 1))
                    return i
                gm = sc.add("tensor", g, xn_all + [ld_gw] + bk.dep_w(band),
                            name=f"gmm{hh}_{band}")
                bk.wr(band, gm)
                gmm.append(gm)

            def lv(e):
                e.tensor_copy(logit[0:8, 0:BAND], ps[0][0:8, :])
                return e.tensor_copy(logit[0:8, BAND:2 * BAND], ps[1][0:8, :])
            lev = sc.add("vector", lv, gmm, name=f"lev{hh}")
            bk.rd(0, lev)
            bk.rd(1, lev)

            comb_done = []
            prev_cev = None
            prev_vt = None
            prev_btr = None
            prev_sexp = None
            for tt in range(8):
                c0 = tt * 128
                ftr = sc.add("tensor", (lambda c0: lambda e: e.transpose(
                    ps[2][0:128, 0:8], logit[0:8, c0:c0 + 128],
                    ident_f[0:8, 0:8]))(c0), [lev, prev_cev] + bk.dep_w(2),
                    name=f"ftr{hh}_{tt}")
                bk.wr(2, ftr)

                smax0 = sc.add("vector", lambda e: e.reduce_max(
                    gcol[:, 0:1], ps[2][0:128, 0:8], axis=AX.X),
                    [ftr, prev_sexp], name=f"vmax0{hh}_{tt}")
                smax = sc.add("vector", lambda e: e.tensor_single_scalar(
                    gcol[:, 1:2], gcol[:, 0:1], -1.0, op=ALU.mult), [smax0],
                    name=f"vmax{hh}_{tt}")
                sexp = sc.add("scalar", lambda e: e.activation(
                    gsA[:], ps[2][0:128, 0:8], AF.Exp, bias=gcol[:, 1:2],
                    accum_out=gcol[:, 2:3]), [smax, prev_vt],
                    name=f"sexp{hh}_{tt}")
                prev_sexp = sexp
                bk.rd(2, sexp)

                v1 = sc.add("vector", lambda e: e.reciprocal(
                    gcol[:, 2:3], gcol[:, 2:3]), [sexp, prev_btr],
                    name=f"vt1{hh}_{tt}")
                v2 = sc.add("vector", lambda e: e.tensor_single_scalar(
                    gsA[:], gsA[:], gcol[:, 2:3], op=ALU.mult), [v1],
                    name=f"vt2{hh}_{tt}")
                v3 = sc.add("vector", lambda e: e.reduce_max(
                    gcol[:, 0:1], gsA[:], axis=AX.X), [v2],
                    name=f"vt3{hh}_{tt}")
                v4 = sc.add("vector", lambda e: e.tensor_single_scalar(
                    gsB[:], gsA[:], gcol[:, 0:1], op=ALU.is_equal), [v3],
                    name=f"vt4{hh}_{tt}")
                v5 = sc.add("vector", lambda e: e.scalar_tensor_tensor(
                    gsB[:], gsB[:], NEG_BIG, gsA[:], op0=ALU.mult,
                    op1=ALU.add), [v4], name=f"vt5{hh}_{tt}")
                v6 = sc.add("vector", lambda e: e.reduce_max(
                    gcol[:, 3:4], gsB[:], axis=AX.X), [v5],
                    name=f"vt6{hh}_{tt}")
                v7 = sc.add("vector", lambda e: e.tensor_single_scalar(
                    gsB[:], gsA[:], gcol[:, 3:4], op=ALU.is_ge), [v6],
                    name=f"vt7{hh}_{tt}")
                vt = sc.add("vector", lambda e: e.tensor_mul(
                    gsB[:], gsB[:], gsA[:]), [v7], name=f"vtop{hh}_{tt}")
                prev_vt = vt
                btr = sc.add("tensor", lambda e: e.transpose(
                    ps[3][0:8, 0:128], gsB[:], ident_f[:]),
                    [vt] + bk.dep_w(3), name=f"btr{hh}_{tt}")
                bk.wr(3, btr)
                prev_btr = btr
                cev = sc.add("vector", (lambda c0: lambda e: e.tensor_copy(
                    combT[0:8, c0:c0 + 128], ps[3][0:8, 0:128]))(c0), [btr],
                    name=f"cev{hh}_{tt}")
                bk.rd(3, cev)
                prev_cev = cev
                comb_done.append(cev)
            st_c = sc.dma("sync", "st",
                          d_comb.rearrange("o (e t) -> (o e) t", e=8),
                          combT[:], comb_done, name=f"stc{hh}")
            prev_cbm = [None]

            x_add_steps = []
            gS_guard = {0: None, 1: None}
            last_dn = [None]
            prev_gev = [None]
            for e_idx in range(9):
                shared = (e_idx == 8)
                if shared:
                    hfs = [list(range(0, 6)), list(range(6, 12)),
                           list(range(12, 18)), list(range(18, 22))]
                    wg_rr = i_swg.rearrange("(k p) h -> p k h", p=128)
                    wu_rr = i_swu.rearrange("(k p) h -> p k h", p=128)
                    wd_rr = i_swd.rearrange("(j p) d -> p j d", p=128)
                else:
                    hfs = [list(range(0, 6)), list(range(6, 11))]
                    wg_rr = i_wg[e_idx].rearrange("(k p) h -> p k h", p=128)
                    wu_rr = i_wu[e_idx].rearrange("(k p) h -> p k h", p=128)
                    wd_rr = i_wd[e_idx].rearrange("(j p) d -> p j d", p=128)

                cb = []
                if not shared:
                    for band in range(2):
                        srcap = d_comb[0:1, e_idx * 1024 + band * BAND:
                                       e_idx * 1024 + (band + 1) * BAND]
                        cp = sc.dma("sync", "ld", zrhsM[0:1, :], srcap,
                                    [st_c, prev_cbm[0]],
                                    name=f"cbc{hh}_{e_idx}_{band}")
                        mm = sc.add("tensor", lambda e: e.matmul(
                            ps[4][:], ones128[:], zrhsM[:], start=True,
                            stop=True), [cp] + bk.dep_w(4),
                            name=f"cbm{hh}_{e_idx}_{band}")
                        prev_cbm[0] = mm
                        bk.wr(4, mm)
                        ev = sc.add("vector", (lambda band: lambda e:
                                    e.tensor_copy(combB[:, band, :], ps[4][:])
                                    )(band), [mm],
                                    name=f"cbe{hh}_{e_idx}_{band}")
                        bk.rd(4, ev)
                        cb.append(ev)

                gS_ready = {}
                prev_ld = {0: wbuf_guard[0], 1: wbuf_guard[1]}
                for hf_i, jl in enumerate(hfs):
                    for jj, j in enumerate(jl):
                        slot = j % 2
                        xdep = regA_readers if (e_idx == 0 and j < 2) else []
                        ldg = sc.dma("gpsimd", "wt", wga[:, slot],
                                     wg_rr[:, :, j * 128:(j + 1) * 128],
                                     [prev_ld[slot]] + xdep,
                                     name=f"lg{hh}_{e_idx}_{j}")
                        ldu = sc.dma("gpsimd", "wt", wub[:, slot],
                                     wu_rr[:, :, j * 128:(j + 1) * 128],
                                     [prev_ld[slot]] + xdep,
                                     name=f"lu{hh}_{e_idx}_{j}")
                        glast = None
                        for band in range(2):
                            bg = band * 2
                            bu = band * 2 + 1

                            def gg(e, slot=slot, bg=bg, band=band):
                                i = None
                                for kt in range(NDT):
                                    i = e.matmul(
                                        ps[bg][:], wga[:, slot, kt, :],
                                        xnb[:, kt,
                                            band * BAND:(band + 1) * BAND],
                                        start=(kt == 0), stop=(kt == NDT - 1))
                                return i

                            def gu(e, slot=slot, bu=bu, band=band):
                                i = None
                                for kt in range(NDT):
                                    i = e.matmul(
                                        ps[bu][:], wub[:, slot, kt, :],
                                        xnb[:, kt,
                                            band * BAND:(band + 1) * BAND],
                                        start=(kt == 0), stop=(kt == NDT - 1))
                                return i
                            depg = [ldg] + xn_all + bk.dep_w(bg)
                            depu = [ldu] + bk.dep_w(bu)
                            g1 = sc.add("tensor", gg, depg,
                                        name=f"g{hh}_{e_idx}_{j}_{band}")
                            bk.wr(bg, g1)
                            g2 = sc.add("tensor", gu, depu,
                                        name=f"u{hh}_{e_idx}_{j}_{band}")
                            bk.wr(bu, g2)
                            glast = g2
                            sil = sc.add("scalar", (lambda bg: lambda e:
                                         e.activation(sgb[:], ps[bg][:],
                                                      AF.Silu))(bg),
                                         [g1, prev_gev[0]],
                                         name=f"sil{hh}_{e_idx}_{j}_{band}")
                            bk.rd(bg, sil)

                            def gev(e, jj=jj, band=band, bu=bu,
                                    shared=shared):
                                e.tensor_mul(gtmp[:], sgb[:], ps[bu][:])
                                if shared:
                                    return e.tensor_copy(
                                        gS[:, jj, band * BAND:
                                           (band + 1) * BAND], gtmp[:])
                                return e.tensor_mul(
                                    gS[:, jj, band * BAND:(band + 1) * BAND],
                                    gtmp[:], combB[:, band, :])
                            gevs = sc.add("vector", gev,
                                          [sil, g2, gS_guard[band]] + cb,
                                          name=f"ge{hh}_{e_idx}_{j}_{band}")
                            bk.rd(bu, gevs)
                            prev_gev[0] = gevs
                            gS_ready[(hf_i, jj, band)] = gevs
                        prev_ld[slot] = glast
                        wbuf_guard[slot] = glast

                    jn = len(jl)
                    prev_wd = {0: None, 1: None}
                    for dt in range(NDT):
                        slot = dt % 2
                        lw = sc.dma(
                            "gpsimd", "wt", wdb[:, slot, 0:jn, :],
                            wd_rr[:, hf_i * 6:hf_i * 6 + jn,
                                  dt * 128:(dt + 1) * 128],
                            [prev_wd[slot]] +
                            (regA_readers if
                             (e_idx == 0 and hf_i == 0 and dt < 2) else []),
                            name=f"lw{hh}_{e_idx}_{hf_i}_{dt}")
                        for band in range(2):
                            bank = band

                            def dn(e, slot=slot, jn=jn, band=band, bank=bank,
                                   hf_i=hf_i):
                                i = None
                                for jj in range(jn):
                                    i = e.matmul(
                                        ps[bank][:], wdb[:, slot, jj, :],
                                        gS[:, jj,
                                           band * BAND:(band + 1) * BAND],
                                        start=(jj == 0), stop=(jj == jn - 1))
                                return i
                            ddeps = [lw] + bk.dep_w(bank) + [
                                gS_ready[(hf_i, jj, band)] for jj in range(jn)]
                            dg = sc.add("tensor", dn, ddeps,
                                        name=f"d{hh}_{e_idx}_{hf_i}_{dt}_{band}")
                            bk.wr(bank, dg)
                            if dt == NDT - 1:
                                gS_guard[band] = dg
                            last_dn[0] = dg

                            def xadd(e, dt=dt, band=band, bank=bank):
                                return e.tensor_add(
                                    xb[:, dt, band * BAND:(band + 1) * BAND],
                                    xb[:, dt, band * BAND:(band + 1) * BAND],
                                    ps[bank][:])
                            xa = sc.add("vector", xadd, [dg],
                                        name=f"xa{hh}_{e_idx}_{hf_i}_{dt}_{band}")
                            bk.rd(bank, xa)
                            x_add_steps.append(xa)
                        prev_wd[slot] = dg
            return x_add_steps

        xn0, ld0 = band_phase(0)
        xn1, ld1 = band_phase(1)
        xa_h0 = moe_phase(0, xn0 + xn1)
        stx0 = sc.dma("sync", "st", o_x1p_r[:, :, 0:1024], xb[:, :, :],
                      xa_h0, name="st_x1p0")
        xb_guard[0] = [stx0]
        xb_guard[1] = [stx0]
        regA_readers[:] = [stx0]
        xn2, ld2 = band_phase(2)
        xn3, ld3 = band_phase(3)
        xa_h1 = moe_phase(1, xn2 + xn3)
        stx1 = sc.dma("sync", "st", o_x1p_r[:, :, 1024:2048], xb[:, :, :],
                      xa_h1, name="st_x1p1")

        fin = sc.add("vector", lambda e: e.engine_nop(),
                     [stx0, stx1, prev_sum_store[0]], name="final")

        sc.emit(nc, sems)

    return nc



def _prep_launch1_inputs(inputs):
    x = np.asarray(inputs["L0_attn_x"], np.float32)
    wqkvu = np.asarray(inputs["L0_attn_W_qkvu"], np.float32)
    wk_t = np.ascontiguousarray(wqkvu[D:2 * D].T)
    wv_t = np.ascontiguousarray(wqkvu[2 * D:3 * D].T)
    slope = np.ascontiguousarray(
        np.asarray(inputs["L0_attn_slope_rate"], np.float32).reshape(1, H0))
    gate_w = np.ascontiguousarray(np.asarray(inputs["L0_ffn_gate_w"],
                                             np.float32))
    wg = np.ascontiguousarray(np.asarray(inputs["L0_ffn_w_gate"], np.float32))
    wu = np.ascontiguousarray(np.asarray(inputs["L0_ffn_w_up"], np.float32))
    wd = np.ascontiguousarray(np.asarray(inputs["L0_ffn_w_down"], np.float32))
    swg = np.ascontiguousarray(np.asarray(inputs["L0_ffn_sw_gate"],
                                          np.float32))
    swu = np.ascontiguousarray(np.asarray(inputs["L0_ffn_sw_up"], np.float32))
    swd = np.ascontiguousarray(np.asarray(inputs["L0_ffn_sw_down"],
                                          np.float32))
    in_maps = []
    for c in range(NCORE):
        b, r = c // 4, c % 4
        bands = [r + 4 * i for i in range(LBANDS)]
        xc = np.concatenate([x[b, g * BAND:(g + 1) * BAND] for g in bands], 0)
        in_maps.append({
            "x_t": np.ascontiguousarray(xc.T),
            "wk_t": wk_t, "wv_t": wv_t, "slope": slope, "gate_w": gate_w,
            "wg": wg, "wu": wu, "wd": wd, "swg": swg, "swu": swu, "swd": swd,
        })
    return in_maps


def run_launch1(inputs):
    nc = build_launch1()
    in_maps = _prep_launch1_inputs(inputs)
    res = run_bass_kernel_spmd(nc, in_maps, core_ids=list(range(NCORE)))
    return res.results


def kernel(**inputs):
    raise NotImplementedError("launches 2/3 under construction")



L2_PARTS = "all"


def build_launch2():
    nc = bass.Bass("TRN2", target_bir_lowering=False, debug=False)

    i_x1p = nc.dram_tensor("x1p_t", [D, LT], F32, kind="ExternalInput").ap()
    i_xn0 = nc.dram_tensor("xn0_t", [D, LT], BF16, kind="ExternalInput").ap()
    i_k0 = nc.dram_tensor("k0_t", [D, LT], BF16, kind="ExternalInput").ap()
    i_v0 = nc.dram_tensor("v0_t", [D, LT], BF16, kind="ExternalInput").ap()
    i_sum = nc.dram_tensor("sum_in", [LBANDS, 4, 128, H0 * 128], F32,
                           kind="ExternalInput").ap()
    i_wq = nc.dram_tensor("wq_t", [D, D], F32, kind="ExternalInput").ap()
    i_wu = nc.dram_tensor("wu_t", [D, D], F32, kind="ExternalInput").ap()
    i_wo = nc.dram_tensor("wo_t", [D, D], F32, kind="ExternalInput").ap()
    i_slope = nc.dram_tensor("slope", [1, H0], F32, kind="ExternalInput").ap()
    i_wsh = nc.dram_tensor("wsh_t", [D, D], F32, kind="ExternalInput").ap()
    i_gate = nc.dram_tensor("gate_c", [128, NDT], F32,
                            kind="ExternalInput").ap()
    i_lng = nc.dram_tensor("lng_c", [128, NDT], F32,
                           kind="ExternalInput").ap()
    i_lnb = nc.dram_tensor("lnb_c", [128, NDT], F32,
                           kind="ExternalInput").ap()
    i_wkey = nc.dram_tensor("wkey_t", [D, 64], F32, kind="ExternalInput").ap()
    i_wval = nc.dram_tensor("wval_t", [D, 64], F32, kind="ExternalInput").ap()
    i_td = nc.dram_tensor("td_c", [64, 1], F32, kind="ExternalInput").ap()
    i_tf = nc.dram_tensor("tf_c", [64, 1], F32, kind="ExternalInput").ap()
    i_anw = nc.dram_tensor("anw_c", [128, NDT], F32,
                           kind="ExternalInput").ap()
    i_qw = nc.dram_tensor("qw_t", [D, D], F32, kind="ExternalInput").ap()
    i_kw = nc.dram_tensor("kw_t", [D, 512], F32, kind="ExternalInput").ap()
    i_vw = nc.dram_tensor("vw_t", [D, 512], F32, kind="ExternalInput").ap()
    i_qn = nc.dram_tensor("qn_c", [128, 1], F32, kind="ExternalInput").ap()
    i_kn = nc.dram_tensor("kn_c", [128, 1], F32, kind="ExternalInput").ap()
    i_bpos = nc.dram_tensor("bpos_c", [128, LBANDS], F32,
                            kind="ExternalInput").ap()

    o_x1 = nc.dram_tensor("x1_o", [D, LT], F32, kind="ExternalOutput").ap()
    o_xn1 = nc.dram_tensor("xn1_o", [D, LT], BF16, kind="ExternalOutput").ap()
    o_q = nc.dram_tensor("q_o", [D, LT], BF16, kind="ExternalOutput").ap()
    o_kf = nc.dram_tensor("kf_o", [512, LT], BF16, kind="ExternalOutput").ap()
    o_vf = nc.dram_tensor("vf_o", [4, LT, 128], BF16,
                          kind="ExternalOutput").ap()
    o_ksq = nc.dram_tensor("ksq_o", [4, LT], F32, kind="ExternalOutput").ap()
    o_sloc = nc.dram_tensor("sloc_o", [64, LT], F32,
                            kind="ExternalOutput").ap()
    o_dbg2 = nc.dram_tensor("dbg2_o", [128, 2048], BF16,
                            kind="ExternalOutput").ap()

    x1p_r = i_x1p.rearrange("(k p) t -> p k t", p=128)
    xn0_r = i_xn0.rearrange("(k p) t -> p k t", p=128)
    k0_r = i_k0.rearrange("(k p) t -> p k t", p=128)
    v0_r = i_v0.rearrange("(k p) t -> p k t", p=128)
    wq_r = i_wq.rearrange("(k p) e -> p k e", p=128)
    wu_r = i_wu.rearrange("(k p) e -> p k e", p=128)
    wo_r = i_wo.rearrange("(k p) e -> p k e", p=128)
    wsh_r = i_wsh.rearrange("(k p) e -> p k e", p=128)
    wq2_r = i_qw.rearrange("(k p) e -> p k e", p=128)
    wk2_r = i_kw.rearrange("(k p) e -> p k e", p=128)
    wv2_r = i_vw.rearrange("(k p) e -> p k e", p=128)
    wkey_r = i_wkey.rearrange("(k p) e -> p k e", p=128)
    wval_r = i_wval.rearrange("(k p) e -> p k e", p=128)
    o_x1_r = o_x1.rearrange("(k p) t -> p k t", p=128)
    o_xn1_r = o_xn1.rearrange("(k p) t -> p k t", p=128)
    o_q_r = o_q.rearrange("(k p) t -> p k t", p=128)
    o_kf_r = o_kf.rearrange("(k p) t -> p k t", p=128)

    sc = Sched()
    bk = Banks()

    with ExitStack() as stk:
        ec = stk.enter_context
        REG = ec(nc.sbuf_tensor("REG", [128, 32768], BF16))
        REGF = ec(nc.sbuf_tensor("REGF", [128, 8192], F32))
        xn1s = ec(nc.sbuf_tensor("xn1s", [128, NDT, BAND], BF16))
        kTb = REG[:, 0:8192].rearrange("p (h t) -> p h t", h=H0)
        vTb = REG[:, 8192:16384].rearrange("p (h t) -> p h t", h=H0)
        qTb = REG[:, 16384:24576].rearrange("p (h t) -> p h t", h=H0)
        oTb = REG[:, 24576:32768].rearrange("p (h t) -> p h t", h=H0)
        xn1c = REG[:, 0:8192].rearrange("p (h t) -> p h t", h=NDT)
        hTb = REG[:, 0:8192].rearrange("p (h t) -> p h t", h=NDT)
        q2b = REG[:, 8192:16384].rearrange("p (h t) -> p h t", h=H0)
        kf4 = REG[:, 16384:18432].rearrange("p (h t) -> p h t", h=4)
        vn4 = REG[:, 18432:20480].rearrange("p (h t) -> p h t", h=H0)
        xb = REGF[:, 0:8192].rearrange("p (k t) -> p k t", k=NDT)
        xsB = REGF[:, 0:8192].rearrange("p (k t) -> p k t", k=NDT)
        KVb = ec(nc.sbuf_tensor("KVb", [128, H0, 128], F32))
        sumS = ec(nc.sbuf_tensor("sumS", [128, 1, H0 * 128], F32))
        wga2 = ec(nc.sbuf_tensor("wga2", [128, 2, NDT, 128], BF16))
        wkv2 = ec(nc.sbuf_tensor("wkv2", [128, 2, NDT, 64], BF16))
        qdecS = ec(nc.sbuf_tensor("qdecS", [128, BLOCK], BF16))
        kdecS = ec(nc.sbuf_tensor("kdecS", [128, BLOCK], BF16))
        diagS = ec(nc.sbuf_tensor("diagS", [128, 2, BLOCK], BF16))
        emask = ec(nc.sbuf_tensor("emask", [128, 2, BLOCK], F32))
        iotaQ = ec(nc.sbuf_tensor("iotaQ", [128, BLOCK], F32))
        qtmp = ec(nc.sbuf_tensor("qtmp", [128, BLOCK], BF16))
        ktmp2 = ec(nc.sbuf_tensor("ktmp2", [128, BLOCK], BF16))
        qkt = ec(nc.sbuf_tensor("qkt", [128, 2, BLOCK], BF16))
        natv = ec(nc.sbuf_tensor("natv", [128, 2, 128], BF16))
        natk = ec(nc.sbuf_tensor("natk", [128, 2, 128], BF16))
        kvbf = ec(nc.sbuf_tensor("kvbf", [128, 128], BF16))
        kvstep = ec(nc.sbuf_tensor("kvstep", [128, 128], F32))
        kvstep_bf = ec(nc.sbuf_tensor("kvstep_bf", [128, 128], BF16))
        osq = ec(nc.sbuf_tensor("osq", [128, BAND], BF16))
        rbS = ec(nc.sbuf_tensor("rbS", [128, BAND], F32))
        xtmp2 = ec(nc.sbuf_tensor("xtmp2", [128, BAND], F32))
        xsq2 = ec(nc.sbuf_tensor("xsq2", [128, 2, BAND], F32))
        zrhs2 = ec(nc.sbuf_tensor("zrhs2", [128, 2, BAND], F32))
        rows2 = ec(nc.sbuf_tensor("rows2", [1, 4 * BAND], F32))
        ones_f2 = ec(nc.sbuf_tensor("ones_f2", [128, 1], F32))
        ones_b2 = ec(nc.sbuf_tensor("ones_b2", [128, 1], BF16))
        ones1282 = ec(nc.sbuf_tensor("ones1282", [128, 128], F32))
        identb2 = ec(nc.sbuf_tensor("identb2", [128, 128], BF16))
        iotapf2 = ec(nc.sbuf_tensor("iotapf2", [128, 128], F32))
        scols2 = ec(nc.sbuf_tensor("scols2", [128, H0], F32))
        nscols2 = ec(nc.sbuf_tensor("nscols2", [128, H0], F32))
        s254 = ec(nc.sbuf_tensor("s254", [128, H0], F32))
        intS = ec(nc.sbuf_tensor("intS", [128, BAND], I32))
        bdcol2 = ec(nc.sbuf_tensor("bdcol2", [128, H0], F32))
        bd2col = ec(nc.sbuf_tensor("bd2col", [128, H0], F32))
        slp2 = ec(nc.sbuf_tensor("slp2", [1, H0], F32))
        ccol2 = ec(nc.sbuf_tensor("ccol2", [128, 8], F32))
        gateC = ec(nc.sbuf_tensor("gateC", [128, NDT], F32))
        gm1C = ec(nc.sbuf_tensor("gm1C", [128, NDT], F32))
        lngC = ec(nc.sbuf_tensor("lngC", [128, NDT], F32))
        lnbC = ec(nc.sbuf_tensor("lnbC", [128, NDT], F32))
        anwC = ec(nc.sbuf_tensor("anwC", [128, NDT], F32))
        qnC = ec(nc.sbuf_tensor("qnC", [128, 1], F32))
        knC = ec(nc.sbuf_tensor("knC", [128, 1], F32))
        tdC = ec(nc.sbuf_tensor("tdC", [64, 1], F32))
        tfC = ec(nc.sbuf_tensor("tfC", [64, 1], F32))
        wcolC = ec(nc.sbuf_tensor("wcolC", [64, 1], F32))
        nuC = ec(nc.sbuf_tensor("nuC", [64, 1], F32))
        bposC = ec(nc.sbuf_tensor("bposC", [128, LBANDS], F32))
        freqC = ec(nc.sbuf_tensor("freqC", [128, 1], F32))
        angT = ec(nc.sbuf_tensor("angT", [128, BAND], F32))
        cosF = ec(nc.sbuf_tensor("cosF", [128, BAND], F32))
        sinF = ec(nc.sbuf_tensor("sinF", [128, BAND], F32))
        tcos = angT
        tsin = ec(nc.sbuf_tensor("tsin", [128, BAND], F32))
        swapP = ec(nc.sbuf_tensor("swapP", [128, 128], BF16))
        sgnC = ec(nc.sbuf_tensor("sgnC", [128, 1], F32))
        sinN = ec(nc.sbuf_tensor("sinN", [128, BAND], F32))
        iotaT = ec(nc.sbuf_tensor("iotaT", [128, BAND], F32))
        xn0bb = ec(nc.sbuf_tensor("xn0bb", [128, NDT, BAND], BF16))
        sigS = ec(nc.sbuf_tensor("sigS", [64, BAND], F32))
        wtile = cosF[0:64, :]
        wkvS = tsin[0:64, :]
        slocB = sigS
        qsqS = osq

        ps = [ec(nc.psum_tensor(f"p2_{i}", [128, BAND], F32)) for i in range(5)]
        ps_bf = ec(nc.psum_tensor("p2_bf", [128, 1024], BF16))
        ps_ra = ec(nc.psum_tensor("p2_ra", [1, BAND], F32))
        ps_rb = ec(nc.psum_tensor("p2_rb", [1, BAND], F32))

        sems = {}
        for n in SEMS:
            sems[n] = ec(nc.semaphore(n))

        ld_misc = []
        ld_misc.append(sc.dma("sync", "ld", slp2[:], i_slope, name="lslp"))
        ld_misc.append(sc.dma("sync", "ld", gateC[:], i_gate, name="lgate"))
        ld_misc.append(sc.dma("sync", "ld", lngC[:], i_lng, name="llng"))
        ld_misc.append(sc.dma("sync", "ld", lnbC[:], i_lnb, name="llnb"))
        ld_misc.append(sc.dma("sync", "ld", anwC[:], i_anw, name="lanw"))
        ld_misc.append(sc.dma("sync", "ld", qnC[:], i_qn, name="lqn"))
        ld_misc.append(sc.dma("sync", "ld", knC[:], i_kn, name="lkn"))
        ld_misc.append(sc.dma("sync", "ld", tdC[:], i_td, name="ltd"))
        ld_misc.append(sc.dma("sync", "ld", tfC[:], i_tf, name="ltf"))
        ld_misc.append(sc.dma("sync", "ld", bposC[:], i_bpos, name="lbp"))
        ld_wkey = sc.dma("gpsimd", "wt", wkv2[:, 0], wkey_r, name="lwkey")
        ld_wval = sc.dma("gpsimd", "wt", wkv2[:, 1], wval_r, name="lwval")

        def fn_iota2(e):
            e.iota(iotapf2[:], pattern=[[-1, 128]], base=0,
                   channel_multiplier=1, allow_small_or_imprecise_dtypes=True)
            e.iota(iotaQ[:], pattern=[[-1, BLOCK]], base=-1,
                   channel_multiplier=0, allow_small_or_imprecise_dtypes=True)
            e.iota(emask[:, 0, :], pattern=[[1, BLOCK]], base=0,
                   channel_multiplier=-1, allow_small_or_imprecise_dtypes=True)
            e.iota(emask[:, 1, :], pattern=[[1, BLOCK]], base=-128,
                   channel_multiplier=-1, allow_small_or_imprecise_dtypes=True)
            e.iota(iotaT[:], pattern=[[1, BAND]], base=0,
                   channel_multiplier=0, allow_small_or_imprecise_dtypes=True)
            return e.iota(freqC[:], pattern=[[0, 1]], base=0,
                          channel_multiplier=1,
                          allow_small_or_imprecise_dtypes=True)
        s_iota2 = sc.add("gpsimd", fn_iota2, name="iota2")

        def fn_con2(e):
            e.memset(ones_f2[:], 1.0)
            e.memset(ones_b2[:], 1.0)
            e.memset(ones1282[:], 1.0)
            e.memset(wtile[:], 1.0)
            e.memset(zrhs2[:], 0.0)
            e.memset(ccol2[:, 0:1], LN_EPS)
            e.memset(ccol2[:, 1:2], RMS_EPS)
            e.memset(ccol2[:, 2:3], -np.pi)
            return e.tensor_single_scalar(identb2[:], iotapf2[:], 0.0,
                                          op=ALU.is_equal)
        s_con2 = sc.add("vector", fn_con2, [s_iota2], name="con2")

        sw1 = sc.add("vector", lambda e: e.tensor_mul(
            angT[:, 0:128], iotapf2[:], iotapf2[:]), [s_con2], name="sw1")
        sw2 = sc.add("vector", lambda e: e.tensor_single_scalar(
            swapP[:], angT[:, 0:128], 4096.0, op=ALU.is_equal), [sw1],
            name="sw2")
        sw3 = sc.add("vector", lambda e: e.tensor_single_scalar(
            sgnC[:], freqC[:], 63.5, op=ALU.is_gt), [s_con2, s_iota2],
            name="sw3")
        s_swp = sc.add("vector", lambda e: e.tensor_scalar(
            sgnC[:], sgnC[:], 2.0, -1.0, op0=ALU.mult, op1=ALU.add),
            [sw3, sw2], name="swp")

        em1 = sc.add("vector", lambda e: e.tensor_single_scalar(
            angT[:, 0:BLOCK], emask[:, 0, :], 0.0, op=ALU.is_lt),
            [s_con2, s_swp], name="em1")
        em2 = sc.add("vector", lambda e: e.scalar_tensor_tensor(
            emask[:, 0, :], angT[:, 0:BLOCK], 1.0e38, emask[:, 0, :],
            op0=ALU.mult, op1=ALU.add), [em1], name="em2")
        em3 = sc.add("vector", lambda e: e.tensor_single_scalar(
            angT[:, 0:BLOCK], emask[:, 1, :], 0.0, op=ALU.is_lt), [em2],
            name="em3")
        s_em = sc.add("vector", lambda e: e.scalar_tensor_tensor(
            emask[:, 1, :], angT[:, 0:BLOCK], 1.0e38, emask[:, 1, :],
            op0=ALU.mult, op1=ALU.add), [em3], name="emask")

        cp_s = sc.add("vector", lambda e: e.tensor_copy(
            zrhs2[0:1, 0, 0:H0], slp2[:]), [s_con2, ld_misc[0]], name="cps")
        mm_s = sc.add("tensor", lambda e: e.matmul(
            ps[0][:, 0:H0], ones1282[:], zrhs2[:, 0, 0:H0], start=True,
            stop=True), [cp_s], name="mms")
        bk.wr(0, mm_s)
        ev_s = sc.add("vector", lambda e: (e.tensor_copy(
            scols2[:], ps[0][:, 0:H0])), [mm_s], name="evs")
        bk.rd(0, ev_s)
        ng_s = sc.add("vector", lambda e: (
            e.tensor_single_scalar(nscols2[:], scols2[:], -1.0, op=ALU.mult),
            e.tensor_single_scalar(s254[:], scols2[:], -256.0, op=ALU.mult)
            )[-1], [ev_s], name="ngs")
        z_s = sc.add("vector", lambda e: e.memset(zrhs2[0:1, 0, 0:H0], 0.0),
                     [mm_s], name="zs")

        def fn_tab2(e):
            e.activation(bdcol2[:], scols2[:], AF.Exp, scale=-256.0)
            e.activation(bd2col[:], scols2[:], AF.Exp, scale=-512.0)
            e.activation(wcolC[:], tdC[:], AF.Exp)
            return e.activation(nuC[:], tfC[:], AF.Exp)
        s_tab2a = sc.add("scalar", fn_tab2, [ev_s, ld_misc[7], ld_misc[8],
                                             s_con2], name="tab2a")
        s_tab2 = sc.add("scalar", lambda e: e.activation(
            wtile[:], wtile[:], AF.Copy, scale=wcolC[:]), [s_tab2a],
            name="tab2")
        nu_s = sc.add("vector", lambda e: e.tensor_single_scalar(
            nuC[:], nuC[:], -1.0, op=ALU.mult), [s_tab2], name="nus")
        sgate = sc.add("scalar", lambda e: e.activation(
            gateC[:], gateC[:], AF.Sigmoid), [ld_misc[1], s_con2],
            name="sgate")
        gm1_s = sc.add("vector", lambda e: e.scalar_tensor_tensor(
            gm1C[:], gateC[:], -1.0, ones1282[:, 0:NDT], op0=ALU.mult,
            op1=ALU.add), [sgate], name="gm1")
        fr0 = sc.add("vector", lambda e: e.tensor_single_scalar(
            xtmp2[:, 0:1], freqC[:], 63.5, op=ALU.is_gt),
            [s_iota2, s_con2, s_swp], name="fr0")
        fr1 = sc.add("vector", lambda e: e.scalar_tensor_tensor(
            freqC[:], xtmp2[:, 0:1], -64.0, freqC[:], op0=ALU.mult,
            op1=ALU.add), [fr0], name="fr1")
        fr2 = sc.add("scalar", lambda e: e.activation(
            freqC[:], freqC[:], AF.Exp,
            scale=float(-2.0 * np.log(1.0e6) / 128.0)), [fr1], name="fr2")

        prev_band = {}
        x1_stores = []
        xn1_steps_all = []
        prefix_last = [s_con2]
        kv_zero = sc.add("vector", lambda e: e.memset(
            KVb.rearrange("p h t -> p (h t)"), 0.0), [], name="kvz")
        ldxn0 = {}

        def xn0b_t(kt, l):
            return xn0bb[:, kt, :]

        for l in range(LBANDS):
            cols = slice(l * BAND, (l + 1) * BAND)
            g0 = [prev_band.get("st_x1")]
            ldk = sc.dma("sync", "ld", kTb[:].rearrange("p h t -> p (h t)"),
                         k0_r[:, :, cols], [prev_band.get("okq")],
                         name=f"ldk{l}")
            ldv = sc.dma("sync", "ld", vTb[:].rearrange("p h t -> p (h t)"),
                         v0_r[:, :, cols], [prev_band.get("okq")],
                         name=f"ldv{l}")
            ldx = sc.dma("sync", "ld", xb[:], x1p_r[:, :, cols], g0,
                         name=f"ldx2_{l}")
            ldxn0[l] = sc.dma("sync", "ld", xn0bb[:], xn0_r[:, :, cols],
                              [prev_band.get("xn0_guard")], name=f"ldxn0_{l}")
            ldsum = []
            pref_steps = []
            pref_by_h = {}
            for k2 in range(4):
                sdeps = [prefix_last[0]]
                if len(ldsum) >= 1:
                    sdeps = [pref_steps[-1]]
                s_ = sc.dma("sync", "ld", sumS[:, 0],
                            i_sum[l, k2].rearrange("p e -> p e"),
                            sdeps, name=f"ldsum{l}_{k2}")
                ldsum.append(s_)
                for h in range(H0):
                    def pf(e, k2=k2, h=h):
                        return e.scalar_tensor_tensor(
                            KVb[:, h, :], KVb[:, h, :], bd2col[:, h:h + 1],
                            sumS[:, 0, h * 128:(h + 1) * 128],
                            op0=ALU.mult, op1=ALU.add)
                    deps = [s_, s_tab2, pref_by_h.get(h)]
                    if l == 0 and k2 == 0:
                        deps.append(kv_zero)
                    stp = sc.add("vector", pf, deps, name=f"pf{l}_{k2}_{h}")
                    pref_steps.append(stp)
                    pref_by_h[h] = stp
            prefix_last[0] = pref_steps[-1]
            proj_ev = {}
            prev_ldw = {0: prev_band.get("wga0"), 1: prev_band.get("wga1")}
            for j in range(NDT):
                slot = j % 2
                ldw = sc.dma("gpsimd", "wt", wga2[:, slot],
                             wq_r[:, :, j * 128:(j + 1) * 128],
                             [prev_ldw[slot]], name=f"lw2_{l}_{j}")

                def grp(e, slot=slot):
                    i = None
                    for kt in range(NDT):
                        i = e.matmul(ps[slot][:], wga2[:, slot, kt, :],
                                     xn0b_t(kt, l), start=(kt == 0),
                                     stop=(kt == NDT - 1))
                    return i
                gdeps = [ldw, ldxn0[l]] + bk.dep_w(slot)
                g = sc.add("tensor", grp, gdeps, name=f"qup{l}_{j}")
                bk.wr(slot, g)
                prev_ldw[slot] = g
                ev = sc.add("scalar", (lambda j, slot: lambda e:
                            e.activation(qTb[:, j, :], ps[slot][:],
                                         AF.Silu))(j, slot), [g],
                            name=f"qev{l}_{j}")
                bk.rd(slot, ev)
                proj_ev[j] = ev
            prev_band["wga0"] = prev_ldw[0]
            prev_band["wga1"] = prev_ldw[1]
            prev_band["xn0_guard"] = prev_ldw[1]

            o_ev = []
            for h in range(H0):
                hd = [proj_ev[h], ldk, ldv]
                tb = sc.add("scalar", (lambda h: lambda e: (
                    e.activation(qdecS[:], iotaQ[:], AF.Exp,
                                 scale=scols2[:, h:h + 1]),
                    e.activation(kdecS[:], iotaQ[:], AF.Exp,
                                 scale=nscols2[:, h:h + 1],
                                 bias=s254[:, h:h + 1]),
                    e.activation(diagS[:, 0, :], emask[:, 0, :], AF.Exp,
                                 scale=nscols2[:, h:h + 1]),
                    e.activation(diagS[:, 1, :], emask[:, 1, :], AF.Exp,
                                 scale=nscols2[:, h:h + 1]))[-1])(h),
                    [s_em, ng_s] + ([o_ev[-1]] if o_ev else
                                    [prev_band.get("okq")]),
                    name=f"tb{l}_{h}")
                kvc = sc.add("vector", (lambda h: lambda e: e.tensor_copy(
                    kvbf[:], KVb[:, h, :]))(h),
                    [pref_steps[3 * H0 + h]], name=f"kvc{l}_{h}")
                blk_o = []
                for blk in range(2):
                    bc = slice(blk * BLOCK, (blk + 1) * BLOCK)
                    lhs_kv = kvbf if blk == 0 else kvstep_bf
                    kv_dep = kvc if blk == 0 else blk_o[-1]
                    qm = sc.add("vector", (lambda h, bc: lambda e:
                                e.tensor_mul(qtmp[:], qTb[:, h, bc],
                                             qdecS[:]))(h, bc), [tb, kv_dep],
                                name=f"qm{l}_{h}_{blk}")
                    ond = sc.add("tensor", (lambda lhs_kv, h, bc: lambda e:
                                 e.matmul(ps[2][:, 0:BLOCK], lhs_kv[:],
                                          qtmp[:], start=True, stop=False))(
                                     lhs_kv, h, bc),
                                 [qm, kv_dep] + bk.dep_w(2),
                                 name=f"ond{l}_{h}_{blk}")
                    bk.wr(2, ond)
                    qka = sc.add("tensor", (lambda h, bc: lambda e: e.matmul(
                        ps[3][:, 0:BLOCK], kTb[:, h, bc][:, 0:128],
                        qTb[:, h, bc], start=True, stop=True))(h, bc),
                        [hd[0], ldk] + bk.dep_w(3), name=f"qka{l}_{h}_{blk}")
                    bk.wr(3, qka)
                    qkb = sc.add("tensor", (lambda h, bc: lambda e: e.matmul(
                        ps[4][:, 0:BLOCK], kTb[:, h, bc][:, 128:256],
                        qTb[:, h, bc], start=True, stop=True))(h, bc),
                        [hd[0], ldk] + bk.dep_w(4), name=f"qkb{l}_{h}_{blk}")
                    bk.wr(4, qkb)
                    dm0 = sc.add("vector", lambda e: e.tensor_mul(
                        qkt[:, 0, :], ps[3][:, 0:BLOCK], diagS[:, 0, :]),
                        [qka, tb], name=f"dm0_{l}_{h}_{blk}")
                    bk.rd(3, dm0)
                    dm1 = sc.add("vector", lambda e: e.tensor_mul(
                        qkt[:, 1, :], ps[4][:, 0:BLOCK], diagS[:, 1, :]),
                        [qkb, tb], name=f"dm1_{l}_{h}_{blk}")
                    bk.rd(4, dm1)
                    tv = sc.add("tensor", (lambda h, bc: lambda e: (
                        e.transpose(ps_bf[:, 0:128],
                                    vTb[:, h, bc][:, 0:128], identb2[:]),
                        e.transpose(ps_bf[:, 128:256],
                                    vTb[:, h, bc][:, 128:256], identb2[:])
                        )[-1])(h, bc), [ldv] + bk.dep_w(5),
                        name=f"tv{l}_{h}_{blk}")
                    bk.wr(5, tv)
                    ev_v = sc.add("vector", lambda e: e.tensor_copy(
                        natv.rearrange("p j t -> p (j t)"),
                        ps_bf[:, 0:256]), [tv], name=f"evv{l}_{h}_{blk}")
                    bk.rd(5, ev_v)
                    od = sc.add("tensor", lambda e: (
                        e.matmul(ps[2][:, 0:BLOCK], natv[:, 0, :],
                                 qkt[:, 0, :], start=False, stop=False),
                        e.matmul(ps[2][:, 0:BLOCK], natv[:, 1, :],
                                 qkt[:, 1, :], start=False, stop=True))[-1],
                        [dm0, dm1, ev_v, ond], name=f"od{l}_{h}_{blk}")
                    bk.wr(2, od)
                    oev = sc.add("vector", (lambda h, bc: lambda e:
                                 e.tensor_copy(oTb[:, h, bc],
                                               ps[2][:, 0:BLOCK]))(h, bc),
                                 [od], name=f"oev{l}_{h}_{blk}")
                    bk.rd(2, oev)
                    o_ev.append(oev)
                    if blk == 0:
                        km = sc.add("vector", (lambda h, bc: lambda e:
                                    e.tensor_mul(ktmp2[:], kTb[:, h, bc],
                                                 kdecS[:]))(h, bc),
                                    [tb, ldk], name=f"km2_{l}_{h}")
                        tk = sc.add("tensor", lambda e: (
                            e.transpose(ps_bf[:, 256:384], ktmp2[:, 0:128],
                                        identb2[:]),
                            e.transpose(ps_bf[:, 384:512], ktmp2[:, 128:256],
                                        identb2[:]))[-1],
                            [km, ev_v] + bk.dep_w(5), name=f"tk{l}_{h}")
                        bk.wr(5, tk)
                        ev_k = sc.add("vector", lambda e: e.tensor_copy(
                            natk.rearrange("p j t -> p (j t)"),
                            ps_bf[:, 256:512]), [tk], name=f"evk{l}_{h}")
                        bk.rd(5, ev_k)
                        am = sc.add("tensor", lambda e: (
                            e.matmul(ps[0][:, 0:128], natk[:, 0, :],
                                     natv[:, 0, :], start=True, stop=False),
                            e.matmul(ps[0][:, 0:128], natk[:, 1, :],
                                     natv[:, 1, :], start=False, stop=True)
                            )[-1], [ev_k, ev_v] + bk.dep_w(0),
                            name=f"am{l}_{h}")
                        bk.wr(0, am)
                        ks1 = sc.add("vector", (lambda h: lambda e:
                                     e.scalar_tensor_tensor(
                                         kvstep[:], KVb[:, h, :],
                                         bdcol2[:, h:h + 1], ps[0][:, 0:128],
                                         op0=ALU.mult, op1=ALU.add))(h),
                                     [am, kvc], name=f"ks1_{l}_{h}")
                        bk.rd(0, ks1)
                        ks2 = sc.add("vector", lambda e: e.tensor_copy(
                            kvstep_bf[:], kvstep[:]), [ks1],
                            name=f"ks2_{l}_{h}")
                        blk_o.append(ks2)
                    else:
                        blk_o.append(oev)

            osum_mm = []
            for j in range(H0):
                sq = sc.add("scalar", (lambda j: lambda e: e.square(
                    osq[:], oTb[:, j, :]))(j),
                    [o_ev[2 * j + 1], osum_mm[-1] if osum_mm else None],
                    name=f"osq{l}_{j}")
                m = sc.add("tensor", (lambda j: lambda e: e.matmul(
                    ps_ra[:], ones_b2[:], osq[:], start=(j == 0),
                    stop=(j == H0 - 1)))(j), [sq] + (bk.dep_w(6) if j == 0
                                                     else []),
                    name=f"osmm{l}_{j}")
                osum_mm.append(m)
            bk.wr(6, osum_mm[-1])
            roA = sc.add("scalar", lambda e: e.activation(
                rows2[0:1, 0:BAND], ps_ra[:], AF.Ln,
                scale=1.0 / D, bias=ccol2[0:1, 1:2]), [osum_mm[-1]],
                name=f"roA{l}")
            ro = sc.add("scalar", lambda e: e.activation(
                rows2[0:1, 0:BAND], rows2[0:1, 0:BAND],
                AF.Exp, scale=-0.5), [roA], name=f"ro{l}")
            bk.rd(6, ro)
            zc = sc.add("vector", lambda e: e.tensor_copy(
                zrhs2[0:1, 0, :], rows2[0:1, 0:BAND]), [ro], name=f"zc{l}")
            rb_mm = sc.add("tensor", lambda e: e.matmul(
                ps[3][:], ones1282[:], zrhs2[:, 0, :], start=True,
                stop=True), [zc] + bk.dep_w(3), name=f"rbm{l}")
            bk.wr(3, rb_mm)
            rb_ev = sc.add("vector", lambda e: e.tensor_copy(rbS[:], ps[3][:]),
                           [rb_mm], name=f"rbe{l}")
            bk.rd(3, rb_ev)
            z_steps = []
            for j in range(H0):
                slot = j % 2
                ldu = sc.dma("gpsimd", "wt", wga2[:, slot],
                             wu_r[:, :, j * 128:(j + 1) * 128],
                             [prev_band["wga0"] if slot == 0 else
                              prev_band["wga1"]] if j < 2 else
                             [z_steps[j - 2]], name=f"lu2_{l}_{j}")

                def ugrp(e, slot=slot):
                    i = None
                    for kt in range(NDT):
                        i = e.matmul(ps[slot][:], wga2[:, slot, kt, :],
                                     xn0b_t(kt, l), start=(kt == 0),
                                     stop=(kt == NDT - 1))
                    return i
                ug = sc.add("tensor", ugrp,
                            [ldu, ldxn0[l]] + bk.dep_w(slot),
                            name=f"ug{l}_{j}")
                bk.wr(slot, ug)
                z_ = sc.add("vector", (lambda j, slot: lambda e:
                            e.tensor_mul(oTb[:, j, :], oTb[:, j, :],
                                         ps[slot][:]))(j, slot),
                            [osum_mm[j], ug], name=f"zmul{l}_{j}")
                bk.rd(slot, z_)
                z_steps.append(z_)
            prev_band["wga0"] = ug if H0 % 2 == 1 else z_steps[-2]
            prev_band["wga1"] = z_steps[-1]
            prev_band["xn0_guard"] = ug
            xadds = []
            wog = []
            for j in range(NDT):
                slot = j % 2
                ldw = sc.dma("gpsimd", "wt", wga2[:, slot],
                             wo_r[:, :, j * 128:(j + 1) * 128],
                             [prev_band["wga0"] if slot == 0 else
                              prev_band["wga1"]] if j < 2 else
                             [wog[j - 2]],
                             name=f"lwo{l}_{j}")

                def grp(e, slot=slot):
                    i = None
                    for kt in range(NDT):
                        i = e.matmul(ps[slot][:], wga2[:, slot, kt, :],
                                     oTb[:, kt, :], start=(kt == 0),
                                     stop=(kt == NDT - 1))
                    return i
                g = sc.add("tensor", grp,
                           [ldw] + z_steps + bk.dep_w(slot),
                           name=f"wog{l}_{j}")
                bk.wr(slot, g)
                wog.append(g)
                xa1 = sc.add("vector", (lambda j, slot: lambda e:
                             e.tensor_mul(xtmp2[:], ps[slot][:], rbS[:])
                             )(j, slot), [g, rb_ev, ldx], name=f"xm{l}_{j}")
                bk.rd(slot, xa1)
                xa2 = sc.add("vector", (lambda j: lambda e: e.tensor_add(
                    xb[:, j, :], xb[:, j, :], xtmp2[:]))(j), [xa1],
                    name=f"xa2_{l}_{j}")
                xadds.append(xa1)
                xadds.append(xa2)
            prev_band["wga0"] = wog[-2]
            prev_band["wga1"] = wog[-1]
            st_x1 = sc.dma("sync", "st", o_x1_r[:, :, cols], xb[:],
                           xadds, name=f"stx1_{l}")
            prev_band["st_x1"] = st_x1
            x1_stores.append(st_x1)

            sq2 = []
            mm2b = []
            for kt in range(NDT):
                deps = [xadds[2 * kt + 1]]
                if kt >= 2:
                    deps.append(mm2b[kt - 2])
                elif prev_band.get("lnsq") is not None:
                    deps.append(prev_band["lnsq"][kt])
                s_ = sc.add("scalar", (lambda kt: lambda e: e.square(
                    xsq2[:, kt % 2, :], xb[:, kt, :]))(kt), deps,
                    name=f"sq2_{l}_{kt}")
                sq2.append(s_)
                m_ = sc.add("tensor", (lambda kt: lambda e: e.matmul(
                    ps_rb[:], ones_f2[:], xsq2[:, kt % 2, :],
                    start=(kt == 0), stop=(kt == NDT - 1)))(kt),
                    [s_] + (bk.dep_w(7) if kt == 0 else []),
                    name=f"smm2_{l}_{kt}")
                mm2b.append(m_)
            prev_band["lnsq"] = [mm2b[-2], mm2b[-1]]
            bk.wr(7, mm2b[-1])

            def mk_sx2(e):
                i = None
                for kt in range(NDT):
                    i = e.matmul(ps_ra[:], ones_f2[:], xb[:, kt, :],
                                 start=(kt == 0), stop=(kt == NDT - 1))
                return i
            mm1b = sc.add("tensor", mk_sx2, xadds + bk.dep_w(6),
                          name=f"sx2_{l}")
            bk.wr(6, mm1b)
            mrow = rows2[0:1, 0:BAND]
            vrow = rows2[0:1, BAND:2 * BAND]
            rrow = rows2[0:1, 2 * BAND:3 * BAND]
            mrr = rows2[0:1, 3 * BAND:4 * BAND]

            def st_stats2(e):
                e.tensor_single_scalar(mrow, ps_ra[:], 1.0 / D, op=ALU.mult)
                e.tensor_single_scalar(vrow, ps_rb[:], 1.0 / D, op=ALU.mult)
                e.tensor_mul(mrr, mrow, mrow)
                return e.tensor_sub(vrow, vrow, mrr)
            s_st2 = sc.add("vector", st_stats2,
                           [mm1b, mm2b[-1], prev_band.get("rstd")],
                           name=f"st2_{l}")
            bk.rd(6, s_st2)
            bk.rd(7, s_st2)
            s_rstd2A = sc.add("scalar", lambda e: e.activation(
                rrow, vrow, AF.Ln, bias=ccol2[0:1, 0:1]), [s_st2],
                name=f"rstd2A_{l}")
            s_rstd2 = sc.add("scalar", lambda e: e.activation(
                rrow, rrow, AF.Exp, scale=-0.5), [s_rstd2A],
                name=f"rstd2_{l}")
            prev_band["rstd"] = s_rstd2
            s_rows2 = sc.add("vector", lambda e: (
                e.scalar_tensor_tensor(mrr, mrow, -1.0, rrow, op0=ALU.mult,
                                       op1=ALU.mult),
                e.tensor_copy(zrhs2[0:1, 0, :], rrow),
                e.tensor_copy(zrhs2[0:1, 1, :], mrr))[-1],
                [s_rstd2, prev_band.get("lnbc")], name=f"rows2_{l}")
            s_bc2 = sc.add("tensor", lambda e: (
                e.matmul(ps[3][:], ones1282[:], zrhs2[:, 0, :], start=True,
                         stop=True),
                e.matmul(ps[4][:], ones1282[:], zrhs2[:, 1, :], start=True,
                         stop=True))[-1],
                [s_rows2] + bk.dep_w(3) + bk.dep_w(4), name=f"bc2_{l}")
            bk.wr(3, s_bc2)
            bk.wr(4, s_bc2)
            prev_band["lnbc"] = s_bc2
            xn1_steps = []
            for kt in range(NDT):
                def fn(e, kt=kt):
                    e.tensor_mul(xtmp2[:], xb[:, kt, :], ps[3][:])
                    return e.tensor_add(xn1s[:, kt, :], xtmp2[:],
                                        ps[4][:])
                xn1_steps.append(sc.add("vector", fn,
                                        [s_bc2, xadds[2 * kt + 1],
                                         prev_band.get("xn1s_rd")],
                                        name=f"xn1_{l}_{kt}"))
            bk.rd(3, xn1_steps[-1])
            bk.rd(4, xn1_steps[-1])
            stxn1 = sc.dma("sync", "st", o_xn1_r[:, :, cols],
                           xn1s[:], xn1_steps, name=f"stxn1_{l}")
            prev_band["xn1s_rd"] = stxn1
            prev_band["stxn1"] = stxn1
            xn1_steps_all.append(stxn1)
            prev_band["okq"] = o_ev[-1]

        if L2_PARTS == "bands":
            fin0 = sc.add("vector", lambda e: e.engine_nop(),
                          x1_stores + [prev_band["stxn1"], prev_band["okq"]],
                          name="fin0")
            sc.emit(nc, sems)
            return nc
        xnS_guard = [None]
        prev_sh = {0: prev_band["wga0"], 1: prev_band["wga1"]}
        rw_done = []
        rw_regA_guard = [prev_band["okq"], prev_band.get("stxn1")]
        for l in range(LBANDS):
            cols = slice(l * BAND, (l + 1) * BAND)
            catg = (l + 2) % 4
            ldc = sc.dma("sync", "ld", REG[:, 0:8192],
                         o_xn1_r[:, :, catg * BAND:(catg + 1) * BAND],
                         rw_regA_guard + [xn1_steps_all[catg]],
                         name=f"ldxc{l}")
            lds = sc.dma("sync", "ld", xn1s[:],
                         o_xn1_r[:, :, cols],
                         [xn1_steps_all[l], prev_band.get("xn1s_rd")],
                         name=f"ldxs{l}")
            prev_band["xn1s_rd"] = None
            xs_steps = []
            shg = []
            for j in range(NDT):
                slot = j % 2
                ldw = sc.dma("gpsimd", "wt", wga2[:, slot],
                             wsh_r[:, :, j * 128:(j + 1) * 128],
                             [prev_sh[slot]], name=f"lsh{l}_{j}")

                def grp(e, slot=slot):
                    i = None
                    for kt in range(NDT):
                        i = e.matmul(ps[slot][:], wga2[:, slot, kt, :],
                                     xn1c[:, kt, :], start=(kt == 0),
                                     stop=(kt == NDT - 1))
                    return i
                g = sc.add("tensor", grp,
                           [ldw, ldc, lds] + bk.dep_w(slot),
                           name=f"shg{l}_{j}")
                bk.wr(slot, g)
                shg.append(g)
                prev_sh[slot] = g

                def ev1(e, j=j, slot=slot):
                    return e.tensor_single_scalar(
                        xtmp2[:], ps[slot][:], gateC[:, j:j + 1], op=ALU.mult)
                e1 = sc.add("vector", ev1, [g, sgate,
                                            xs_steps[-1] if xs_steps else
                                            xnS_guard[0],
                                            prev_band.get("st_x1")],
                            name=f"xse1_{l}_{j}")
                bk.rd(slot, e1)

                def ev2(e, j=j):
                    return e.scalar_tensor_tensor(
                        xsB[:, j, :], xn1s[:, j, :], gm1C[:, j:j + 1],
                        xtmp2[:], op0=ALU.mult, op1=ALU.add)
                e2 = sc.add("vector", ev2, [e1, gm1_s], name=f"xse2_{l}_{j}")
                xs_steps.append(e2)
            sqr = []
            mmr = []
            for kt in range(NDT):
                deps = [xs_steps[kt]]
                if kt >= 2:
                    deps.append(mmr[kt - 2])
                elif prev_band.get("rwsq") is not None:
                    deps.append(prev_band["rwsq"][kt])
                s_ = sc.add("scalar", (lambda kt: lambda e: e.square(
                    xsq2[:, kt % 2, :], xsB[:, kt, :]))(kt), deps,
                    name=f"rsq{l}_{kt}")
                sqr.append(s_)
                m_ = sc.add("tensor", (lambda kt: lambda e: e.matmul(
                    ps_rb[:], ones_f2[:], xsq2[:, kt % 2, :],
                    start=(kt == 0), stop=(kt == NDT - 1)))(kt),
                    [s_] + (bk.dep_w(7) if kt == 0 else []),
                    name=f"rsmm{l}_{kt}")
                mmr.append(m_)
            prev_band["rwsq"] = [mmr[-2], mmr[-1]]
            bk.wr(7, mmr[-1])

            def mk_sxr(e):
                i = None
                for kt in range(NDT):
                    i = e.matmul(ps_ra[:], ones_f2[:], xsB[:, kt, :],
                                 start=(kt == 0), stop=(kt == NDT - 1))
                return i
            mm1r = sc.add("tensor", mk_sxr, xs_steps + bk.dep_w(6),
                          name=f"rsx{l}")
            bk.wr(6, mm1r)
            mrow = rows2[0:1, 0:BAND]
            vrow = rows2[0:1, BAND:2 * BAND]
            rrow = rows2[0:1, 2 * BAND:3 * BAND]
            mrr = rows2[0:1, 3 * BAND:4 * BAND]
            s_str = sc.add("vector", lambda e: (
                e.tensor_single_scalar(mrow, ps_ra[:], 1.0 / D, op=ALU.mult),
                e.tensor_single_scalar(vrow, ps_rb[:], 1.0 / D, op=ALU.mult),
                e.tensor_mul(mrr, mrow, mrow),
                e.tensor_sub(vrow, vrow, mrr))[-1],
                [mm1r, mmr[-1], prev_band.get("rstd")], name=f"rst{l}")
            bk.rd(6, s_str)
            bk.rd(7, s_str)
            s_rstdrA = sc.add("scalar", lambda e: e.activation(
                rrow, vrow, AF.Ln, bias=ccol2[0:1, 0:1]), [s_str],
                name=f"rrstdA{l}")
            s_rstdr = sc.add("scalar", lambda e: e.activation(
                rrow, rrow, AF.Exp, scale=-0.5), [s_rstdrA],
                name=f"rrstd{l}")
            prev_band["rstd"] = s_rstdr
            s_rowsr = sc.add("vector", lambda e: (
                e.scalar_tensor_tensor(mrr, mrow, -1.0, rrow, op0=ALU.mult,
                                       op1=ALU.mult),
                e.tensor_copy(zrhs2[0:1, 0, :], rrow),
                e.tensor_copy(zrhs2[0:1, 1, :], mrr))[-1],
                [s_rstdr, prev_band.get("lnbc")], name=f"rrows{l}")
            s_bcr = sc.add("tensor", lambda e: (
                e.matmul(ps[3][:], ones1282[:], zrhs2[:, 0, :], start=True,
                         stop=True),
                e.matmul(ps[4][:], ones1282[:], zrhs2[:, 1, :], start=True,
                         stop=True))[-1],
                [s_rowsr] + bk.dep_w(3) + bk.dep_w(4), name=f"rbc{l}")
            bk.wr(3, s_bcr)
            bk.wr(4, s_bcr)
            prev_band["lnbc"] = s_bcr
            xns_steps = []
            for kt in range(NDT):
                def fn(e, kt=kt):
                    e.tensor_mul(xtmp2[:], xsB[:, kt, :], ps[3][:])
                    e.tensor_add(xtmp2[:], xtmp2[:], ps[4][:])
                    return e.tensor_scalar(xn0bb[:, kt, :], xtmp2[:],
                                           lngC[:, kt:kt + 1],
                                           lnbC[:, kt:kt + 1],
                                           op0=ALU.mult, op1=ALU.add)
                xns_steps.append(sc.add("vector", fn,
                                        [s_bcr, xs_steps[kt],
                                         ldxn0[3] if l == 0 else None,
                                         ld_misc[2], ld_misc[3]],
                                        name=f"xns{l}_{kt}"))
            bk.rd(3, xns_steps[-1])
            bk.rd(4, xns_steps[-1])
            xnS_guard[0] = xns_steps[-1]
            def kgrp(e):
                i = None
                for kt in range(NDT):
                    i = e.matmul(ps[0][0:64, :], wkv2[:, 0, kt, :],
                                 xn0bb[:, kt, :], start=(kt == 0),
                                 stop=(kt == NDT - 1))
                return i
            kg = sc.add("tensor", kgrp,
                        [ld_wkey] + xns_steps + bk.dep_w(0), name=f"rkg{l}")
            bk.wr(0, kg)

            def vgrp(e):
                i = None
                for kt in range(NDT):
                    i = e.matmul(ps[1][0:64, :], wkv2[:, 1, kt, :],
                                 xn0bb[:, kt, :], start=(kt == 0),
                                 stop=(kt == NDT - 1))
                return i
            vg = sc.add("tensor", vgrp,
                        [ld_wval] + xns_steps + bk.dep_w(1), name=f"rvg{l}")
            bk.wr(1, vg)
            sg1 = sc.add("scalar", lambda e: e.activation(
                sigS[:], ps[0][0:64, :], AF.Sigmoid), [kg, rw_done[-1] if
                                                       rw_done else None],
                name=f"rsig{l}")
            bk.rd(0, sg1)
            sg2 = sc.add("scalar", lambda e: e.activation(
                sigS[:], sigS[:], AF.Exp, scale=nuC[:]), [sg1, nu_s],
                name=f"rexp{l}")
            wk1 = sc.add("vector", lambda e: e.tensor_mul(
                wkvS[:], sigS[:], ps[1][0:64, :]), [sg2, vg], name=f"rwkv{l}")
            bk.rd(1, wk1)
            scn = sc.add("vector", lambda e: e.tensor_tensor_scan(
                slocB[:], wtile[:], wkvS[:], 0.0, op0=ALU.mult, op1=ALU.add),
                [wk1, s_tab2], name=f"rscan{l}")
            std = sc.dma("sync", "st", o_sloc[:, cols], slocB[:], [scn],
                         name=f"stsloc{l}")
            rw_done.append(std)
            rw_regA_guard = [shg[-1], shg[-2]]

        if L2_PARTS == "bands+rwkv":
            fin0 = sc.add("vector", lambda e: e.engine_nop(),
                          x1_stores + rw_done, name="fin0")
            sc.emit(nc, sems)
            return nc
        prev_l1 = {0: prev_sh[0], 1: prev_sh[1]}
        hswap = [rw_regA_guard[0], kg, vg, wk1]
        for l in range(LBANDS):
            cols = slice(l * BAND, (l + 1) * BAND)
            lds1 = sc.dma("sync", "ld", xn1s[:], o_xn1_r[:, :, cols],
                          [xn1_steps_all[l], prev_band.get("l1xn_rd")],
                          name=f"l1xs{l}")
            sqh = []
            mmh = []
            for kt in range(NDT):
                deps = [lds1]
                if kt >= 2:
                    deps.append(mmh[kt - 2])
                elif prev_band.get("l1sq") is not None:
                    deps.append(prev_band["l1sq"][kt])
                s_ = sc.add("scalar", (lambda kt: lambda e: e.square(
                    qsqS[:], xn1s[:, kt, :]))(kt),
                    deps + [mmh[-1] if mmh else prev_band.get("mmk")],
                    name=f"hsq{l}_{kt}")
                sqh.append(s_)
                m_ = sc.add("tensor", (lambda kt: lambda e: e.matmul(
                    ps_ra[:], ones_b2[:], qsqS[:], start=(kt == 0),
                    stop=(kt == NDT - 1)))(kt),
                    [s_] + (bk.dep_w(6) if kt == 0 else []),
                    name=f"hmm{l}_{kt}")
                mmh.append(m_)
            prev_band["l1sq"] = [mmh[-2], mmh[-1]]
            bk.wr(6, mmh[-1])
            rhA = sc.add("scalar", lambda e: e.activation(
                rows2[0:1, 0:BAND], ps_ra[:], AF.Ln,
                scale=1.0 / D, bias=ccol2[0:1, 1:2]), [mmh[-1]],
                name=f"rhA{l}")
            rh = sc.add("scalar", lambda e: e.activation(
                rows2[0:1, 0:BAND], rows2[0:1, 0:BAND],
                AF.Exp, scale=-0.5), [rhA], name=f"rh{l}")
            bk.rd(6, rh)
            zch = sc.add("vector", lambda e: e.tensor_copy(
                zrhs2[0:1, 0, :], rows2[0:1, 0:BAND]),
                [rh, prev_band.get("lnbc")], name=f"zch{l}")
            bch = sc.add("tensor", lambda e: e.matmul(
                ps[3][:], ones1282[:], zrhs2[:, 0, :], start=True,
                stop=True), [zch] + bk.dep_w(3), name=f"bch{l}")
            bk.wr(3, bch)
            prev_band["lnbc"] = bch
            rbh = sc.add("vector", lambda e: e.tensor_copy(rbS[:], ps[3][:]),
                         [bch], name=f"rbh{l}")
            bk.rd(3, rbh)
            h_steps = []
            for kt in range(NDT):
                def fn(e, kt=kt):
                    e.tensor_mul(xtmp2[:], xn1s[:, kt, :], rbS[:])
                    return e.tensor_single_scalar(hTb[:, kt, :], xtmp2[:],
                                                  anwC[:, kt:kt + 1],
                                                  op=ALU.mult)
                h_steps.append(sc.add("vector", fn,
                                      [rbh, ld_misc[4]] + hswap,
                                      name=f"hst{l}_{kt}"))
            rt1a = sc.add("vector", (lambda l: lambda e:
                e.tensor_single_scalar(angT[:], iotaT[:],
                                       bposC[:, l:l + 1], op=ALU.add))(l),
                [fr2, ld_misc[9], prev_band.get("ropeg"), rw_done[-1]],
                name=f"rt1a_{l}")
            rt1 = sc.add("vector", lambda e: e.tensor_single_scalar(
                angT[:], angT[:], freqC[:], op=ALU.mult), [rt1a],
                name=f"rt1_{l}")
            inv2pi = float(1.0 / (2.0 * np.pi))
            twopi = float(2.0 * np.pi)

            r2a = sc.add("vector", lambda e: (
                e.tensor_single_scalar(sinF[:], angT[:], inv2pi,
                                       op=ALU.mult)), [rt1],
                name=f"r2a_{l}")
            r2b = sc.add("vector", lambda e: e.tensor_copy(intS[:], sinF[:]),
                         [r2a], name=f"r2b_{l}")
            r2c = sc.add("vector", lambda e: e.tensor_copy(sinF[:], intS[:]),
                         [r2b], name=f"r2c_{l}")
            r2d = sc.add("vector", lambda e: e.scalar_tensor_tensor(
                sinF[:], sinF[:], -twopi, angT[:], op0=ALU.mult,
                op1=ALU.add), [r2c], name=f"r2d_{l}")
            r2e = sc.add("vector", lambda e: e.tensor_single_scalar(
                cosF[:], angT[:], float(0.5 * np.pi), op=ALU.add),
                [rt1, r2d], name=f"r2e_{l}")
            r2f = sc.add("vector", lambda e: e.tensor_single_scalar(
                tsin[:], cosF[:], inv2pi, op=ALU.mult), [r2e],
                name=f"r2f_{l}")
            r2g = sc.add("vector", lambda e: e.tensor_copy(intS[:], tsin[:]),
                         [r2f], name=f"r2g_{l}")
            r2h = sc.add("vector", lambda e: e.tensor_copy(tsin[:], intS[:]),
                         [r2g], name=f"r2h_{l}")
            rt2 = sc.add("vector", lambda e: e.scalar_tensor_tensor(
                cosF[:], tsin[:], -twopi, cosF[:], op0=ALU.mult,
                op1=ALU.add), [r2h], name=f"rt2_{l}")
            rt3 = sc.add("scalar", lambda e: (
                e.activation(cosF[:], cosF[:], AF.Sin),
                e.activation(sinF[:], sinF[:], AF.Sin))[-1], [rt2],
                name=f"rt3_{l}")
            rt4 = sc.add("vector", lambda e: e.tensor_single_scalar(
                sinN[:], sinF[:], sgnC[:], op=ALU.mult), [rt3, s_swp],
                name=f"rt4_{l}")

            def proj_head(tag, wsrc, n_heads, dest, evdeps=()):
                outs = []
                for j in range(n_heads):
                    slot = j % 2
                    ldw = sc.dma("gpsimd", "wt", wga2[:, slot],
                                 wsrc[:, :, j * 128:(j + 1) * 128],
                                 [prev_l1[slot]], name=f"{tag}w{l}_{j}")

                    def grp(e, slot=slot):
                        i = None
                        for kt in range(NDT):
                            i = e.matmul(ps[slot][:], wga2[:, slot, kt, :],
                                         hTb[:, kt, :], start=(kt == 0),
                                         stop=(kt == NDT - 1))
                        return i
                    g = sc.add("tensor", grp,
                               [ldw] + h_steps + bk.dep_w(slot),
                               name=f"{tag}g{l}_{j}")
                    bk.wr(slot, g)
                    prev_l1[slot] = g
                    ev = sc.add("scalar", (lambda j, slot: lambda e: e.copy(
                        dest[:, j, :], ps[slot][:]))(j, slot),
                        [g] + list(evdeps), name=f"{tag}e{l}_{j}")
                    bk.rd(slot, ev)
                    outs.append(ev)
                return outs

            qev = proj_head("q", wq2_r, H0, q2b,
                            evdeps=[prev_band.get("stq")])
            qrope = []
            d1 = d2 = None
            for j in range(H0):
                sq_ = sc.add("scalar", (lambda j: lambda e: e.square(
                    qsqS[:], q2b[:, j, :]))(j),
                    [qev[j], qrope[-1] if qrope else (mmh[-1]),
                     prev_band.get("qn_mm")],
                    name=f"qsq{l}_{j}")
                mm_ = sc.add("tensor", (lambda j: lambda e: e.matmul(
                    ps_rb[:], ones_b2[:], qsqS[:], start=True, stop=True)
                    )(j), [sq_] + bk.dep_w(7), name=f"qn_mm{l}_{j}")
                bk.wr(7, mm_)
                prev_band["qn_mm"] = mm_
                rrA_ = sc.add("scalar", lambda e: e.activation(
                    rows2[0:1, BAND:2 * BAND], ps_rb[:], AF.Ln,
                    scale=1.0 / 128.0, bias=ccol2[0:1, 1:2]), [mm_],
                    name="A")
                rr_ = sc.add("scalar", lambda e: e.activation(
                    rows2[0:1, BAND:2 * BAND],
                    rows2[0:1, BAND:2 * BAND], AF.Exp,
                    scale=-0.5), [rrA_], name=f"qnr{l}_{j}")
                bk.rd(7, rr_)
                zc_ = sc.add("vector", lambda e: e.tensor_copy(
                    zrhs2[0:1, 1, :], rows2[0:1, BAND:2 * BAND]), [rr_],
                    name=f"qnz{l}_{j}")
                bc_ = sc.add("tensor", lambda e: e.matmul(
                    ps[4][:], ones1282[:], zrhs2[:, 1, :], start=True,
                    stop=True), [zc_] + bk.dep_w(4), name=f"qnb{l}_{j}")
                bk.wr(4, bc_)

                def norm_fn(e, j=j):
                    e.tensor_mul(xtmp2[:], q2b[:, j, :], ps[4][:])
                    e.tensor_single_scalar(xtmp2[:], xtmp2[:], qnC[:],
                                           op=ALU.mult)
                    return e.tensor_copy(q2b[:, j, :], xtmp2[:])
                nf = sc.add("vector", norm_fn, [bc_, ld_misc[5]],
                            name=f"qnorm{l}_{j}")
                bk.rd(4, nf)
                sw = sc.add("tensor", (lambda j: lambda e: e.matmul(
                    ps[2][:], swapP[:], q2b[:, j, :], start=True, stop=True)
                    )(j), [nf, s_swp] + bk.dep_w(2), name=f"qsw{l}_{j}")
                bk.wr(2, sw)

                def rope_fn(e, j=j):
                    e.tensor_mul(tcos[:], q2b[:, j, :], cosF[:])
                    e.tensor_mul(tsin[:], ps[2][:], sinN[:])
                    return e.tensor_add(q2b[:, j, :], tcos[:], tsin[:])
                rp = sc.add("vector", rope_fn,
                            [sw, rt4] + ([d2] if l == 0 and j == 0 else []),
                            name=f"qrope{l}_{j}")
                bk.rd(2, rp)
                if l == 0 and j == 0:
                    sc.dma("sync", "st", o_dbg2[:, 1024:1536],
                           q2b[:, 0, :], [rp], name="dbg_rope")
                    dcps = sc.add("vector", lambda e: e.tensor_copy(
                        qtmp[:, 0:BAND].rearrange("p (a b) -> p (a b)") if False else osq[:], sinF[:]), [rt3, rp], name="dbgcp")
                    sc.dma("sync", "st", o_dbg2[:, 1536:2048],
                           osq[:], [dcps], name="dbgsin")
                qrope.append(rp)
            stq_step = sc.dma("sync", "st", o_q_r[:, :, cols],
                              q2b[:].rearrange("p h t -> p (h t)"), qrope,
                              name=f"stq{l}")
            prev_band["stq"] = stq_step
            kev = proj_head("k2", wk2_r, 4, kf4,
                            evdeps=[prev_band.get("stkf"),
                                    prev_band.get("vtr_last")])
            krope = []
            for j in range(4):
                sq_ = sc.add("scalar", (lambda j: lambda e: e.square(
                    qsqS[:], kf4[:, j, :]))(j),
                    [kev[j], krope[-1] if krope else qrope[-1],
                     prev_band.get("mmk")],
                    name=f"ksq{l}_{j}")
                mm_ = sc.add("tensor", (lambda j: lambda e: e.matmul(
                    ps_rb[:], ones_b2[:], qsqS[:], start=True, stop=True)
                    )(j), [sq_] + bk.dep_w(7), name=f"kn_mm{l}_{j}")
                bk.wr(7, mm_)
                prev_band["qn_mm"] = mm_
                rrA_ = sc.add("scalar", lambda e: e.activation(
                    rows2[0:1, BAND:2 * BAND], ps_rb[:], AF.Ln,
                    scale=1.0 / 128.0, bias=ccol2[0:1, 1:2]), [mm_],
                    name="A")
                rr_ = sc.add("scalar", lambda e: e.activation(
                    rows2[0:1, BAND:2 * BAND],
                    rows2[0:1, BAND:2 * BAND], AF.Exp,
                    scale=-0.5), [rrA_], name=f"knr{l}_{j}")
                bk.rd(7, rr_)
                zc_ = sc.add("vector", lambda e: e.tensor_copy(
                    zrhs2[0:1, 1, :], rows2[0:1, BAND:2 * BAND]), [rr_],
                    name=f"knz{l}_{j}")
                bc_ = sc.add("tensor", lambda e: e.matmul(
                    ps[4][:], ones1282[:], zrhs2[:, 1, :], start=True,
                    stop=True), [zc_] + bk.dep_w(4), name=f"knb{l}_{j}")
                bk.wr(4, bc_)

                def norm_fn(e, j=j):
                    e.tensor_mul(xtmp2[:], kf4[:, j, :], ps[4][:])
                    e.tensor_single_scalar(xtmp2[:], xtmp2[:], knC[:],
                                           op=ALU.mult)
                    return e.tensor_copy(kf4[:, j, :], xtmp2[:])
                nf = sc.add("vector", norm_fn, [bc_, ld_misc[6]],
                            name=f"knorm{l}_{j}")
                bk.rd(4, nf)
                sw = sc.add("tensor", (lambda j: lambda e: e.matmul(
                    ps[2][:], swapP[:], kf4[:, j, :], start=True, stop=True)
                    )(j), [nf, s_swp] + bk.dep_w(2), name=f"ksw{l}_{j}")
                bk.wr(2, sw)

                def rope_fn(e, j=j):
                    e.tensor_mul(tcos[:], kf4[:, j, :], cosF[:])
                    e.tensor_mul(tsin[:], ps[2][:], sinN[:])
                    return e.tensor_add(kf4[:, j, :], tcos[:], tsin[:])
                rp = sc.add("vector", rope_fn, [sw, rt4], name=f"krope{l}_{j}")
                bk.rd(2, rp)
                krope.append(rp)
                sqk = sc.add("scalar", (lambda j: lambda e: e.square(
                    qsqS[:], kf4[:, j, :]))(j), [rp], name=f"ksq2_{l}_{j}")
                mmk = sc.add("tensor", (lambda j: lambda e: e.matmul(
                    ps_rb[:], ones_b2[:], qsqS[:], start=True, stop=True)
                    )(j), [sqk] + bk.dep_w(7), name=f"ksmm{l}_{j}")
                bk.wr(7, mmk)
                prev_band["mmk"] = mmk
                cpk = sc.add("vector", lambda e: e.tensor_copy(
                    rows2[0:1, 3 * BAND:4 * BAND], ps_rb[:]),
                    [mmk, prev_band.get("stksq")], name=f"kscp{l}_{j}")
                bk.rd(7, cpk)
                prev_band["stksq"] = sc.dma(
                    "sync", "st", o_ksq[j:j + 1, cols],
                    rows2[0:1, 3 * BAND:4 * BAND], [cpk],
                    name=f"stksq{l}_{j}")
            stkf_step = sc.dma(
                "sync", "st",
                o_kf.rearrange("(h p) t -> p h t", p=128)[:, :, cols],
                kf4[:].rearrange("p h t -> p (h t)"), krope,
                name=f"stkf{l}")
            prev_band["stkf"] = stkf_step
            vev = proj_head("v2", wv2_r, 4, kf4,
                            evdeps=[krope[-1], stkf_step])
            vtr = []
            for j in range(4):
                def trj(e, j=j):
                    i = None
                    for tt in range(4):
                        i = e.transpose(ps_bf[:, tt * 128:(tt + 1) * 128],
                                        kf4[:, j, tt * 128:(tt + 1) * 128],
                                        identb2[:])
                    return i
                t_ = sc.add("tensor", trj, [vev[j]] + bk.dep_w(5),
                            name=f"vtr{l}_{j}")
                bk.wr(5, t_)
                e_ = sc.add("vector", (lambda j: lambda e: e.tensor_copy(
                    vn4[:].rearrange("p h t -> p (h t)")[:, j * 512:
                                                         (j + 1) * 512],
                    ps_bf[:, 0:512]))(j),
                    [t_] + (prev_band.get("stvf") or []),
                    name=f"vne{l}_{j}")
                bk.rd(5, e_)
                vtr.append(e_)
            stvfs = []
            for j in range(4):
                stvfs.append(sc.dma(
                    "sync", "st",
                    o_vf[j, l * BAND:(l + 1) * BAND, :].rearrange(
                        "(tt p) d -> p tt d", p=128),
                    vn4[:, 4 * j:4 * j + 4, :], [vtr[j]],
                    name=f"stvf{l}_{j}"))
            prev_band["stvf"] = stvfs
            prev_band["vtr_last"] = vtr[-1]
            hswap = [vev[-1], vtr[-1]]
            prev_band["l1xn_rd"] = mmh[-1]
            prev_band["ropeg"] = krope[-1]

        sc.emit(nc, sems)
    return nc


def _prep_launch2_inputs(inputs, l1_results):
    wqkvu = np.asarray(inputs["L0_attn_W_qkvu"], np.float32)
    wq_t = np.ascontiguousarray(wqkvu[0:D].T)
    wu_t = np.ascontiguousarray(wqkvu[3 * D:4 * D].T)
    wo_t = np.ascontiguousarray(
        np.asarray(inputs["L0_attn_W_out"], np.float32).T)
    slope = np.ascontiguousarray(
        np.asarray(inputs["L0_attn_slope_rate"], np.float32).reshape(1, H0))
    wsh_t = np.ascontiguousarray(
        np.asarray(inputs["L1_attn_W_shift"], np.float32).T)
    gate_c = np.ascontiguousarray(
        np.asarray(inputs["L1_attn_gate"], np.float32).reshape(NDT, 128).T)
    lng_c = np.ascontiguousarray(
        np.asarray(inputs["L1_attn_ln_gamma"], np.float32).reshape(NDT,
                                                                   128).T)
    lnb_c = np.ascontiguousarray(
        np.asarray(inputs["L1_attn_ln_beta"], np.float32).reshape(NDT, 128).T)
    wkey_t = np.ascontiguousarray(
        np.asarray(inputs["L1_attn_W_key"], np.float32).T)
    wval_t = np.ascontiguousarray(
        np.asarray(inputs["L1_attn_W_value"], np.float32).T)
    td_c = np.ascontiguousarray(
        np.asarray(inputs["L1_attn_time_decay"], np.float32).reshape(64, 1))
    tf_c = np.ascontiguousarray(
        np.asarray(inputs["L1_attn_time_first"], np.float32).reshape(64, 1))
    anw_c = np.ascontiguousarray(
        np.asarray(inputs["L1_ffn_attn_norm_w"], np.float32).reshape(
            NDT, 128).T)
    qw_t = np.ascontiguousarray(np.asarray(inputs["L1_ffn_q_w"],
                                           np.float32).T)
    kw_t = np.ascontiguousarray(np.asarray(inputs["L1_ffn_k_w"],
                                           np.float32).T)
    vw_t = np.ascontiguousarray(np.asarray(inputs["L1_ffn_v_w"],
                                           np.float32).T)
    qn_c = np.ascontiguousarray(
        np.asarray(inputs["L1_ffn_q_norm_w"], np.float32).reshape(128, 1))
    kn_c = np.ascontiguousarray(
        np.asarray(inputs["L1_ffn_k_norm_w"], np.float32).reshape(128, 1))

    S = {}
    for c in range(NCORE):
        b, r = c // 4, c % 4
        so = np.asarray(l1_results[c]["sum_o"])
        for l in range(LBANDS):
            S[(b, 4 * l + r)] = so[:, :, l, :]
    in_maps = []
    for c in range(NCORE):
        b, r = c // 4, c % 4
        sum_in = np.zeros((LBANDS, 4, 128, H0 * 128), np.float32)
        for l in range(LBANDS):
            g_l = 4 * l + r
            for k2 in range(4):
                gi = g_l - 4 + k2
                if gi >= 0:
                    sum_in[l, k2] = S[(b, gi)].reshape(128, H0 * 128)
        bpos = np.zeros((128, LBANDS), np.float32)
        for l in range(LBANDS):
            bpos[:, l] = 512.0 * (4 * l + r)
        in_maps.append({
            "x1p_t": np.asarray(l1_results[c]["x1p_o"]),
            "xn0_t": np.asarray(l1_results[c]["xn0_o"]),
            "k0_t": np.asarray(l1_results[c]["k_o"]),
            "v0_t": np.asarray(l1_results[c]["v_o"]),
            "sum_in": sum_in,
            "wq_t": wq_t, "wu_t": wu_t, "wo_t": wo_t, "slope": slope,
            "wsh_t": wsh_t, "gate_c": gate_c, "lng_c": lng_c, "lnb_c": lnb_c,
            "wkey_t": wkey_t, "wval_t": wval_t, "td_c": td_c, "tf_c": tf_c,
            "anw_c": anw_c, "qw_t": qw_t, "kw_t": kw_t, "vw_t": vw_t,
            "qn_c": qn_c, "kn_c": kn_c, "bpos_c": bpos,
        })
    return in_maps


def run_launch2(inputs, l1_results):
    nc = build_launch2()
    in_maps = _prep_launch2_inputs(inputs, l1_results)
    res = run_bass_kernel_spmd(nc, in_maps, core_ids=list(range(NCORE)))
    return res.results



def build_launch3():
    nc = bass.Bass("TRN2", target_bir_lowering=False, debug=False)

    i_q = nc.dram_tensor("q_t", [D, LT], BF16, kind="ExternalInput").ap()
    i_kf = nc.dram_tensor("kf_all", [4, 128, TSEQ], BF16,
                          kind="ExternalInput").ap()
    i_vf = nc.dram_tensor("vf_all", [4, TSEQ, 128], BF16,
                          kind="ExternalInput").ap()
    i_ksq = nc.dram_tensor("ksq_all", [4, 128, 64], F32,
                           kind="ExternalInput").ap()
    i_sloc = nc.dram_tensor("sloc_t", [64, LT], F32,
                            kind="ExternalInput").ap()
    i_sfin = nc.dram_tensor("sfin_in", [LBANDS, 64, 4], F32,
                            kind="ExternalInput").ap()
    i_td = nc.dram_tensor("td_c", [64, 1], F32, kind="ExternalInput").ap()
    i_bpos = nc.dram_tensor("bpos_c", [128, LBANDS], F32,
                            kind="ExternalInput").ap()

    o_attn = nc.dram_tensor("attn_o", [D, LT], BF16,
                            kind="ExternalOutput").ap()
    o_states = nc.dram_tensor("states_o", [64, LT], F32,
                              kind="ExternalOutput").ap()

    q_r = i_q.rearrange("(k p) t -> p k t", p=128)
    attn_r = o_attn.rearrange("(k p) t -> p k t", p=128)

    sc = Sched()
    bk = Banks()

    with ExitStack() as stk:
        ec = stk.enter_context
        kfb = ec(nc.sbuf_tensor("kfb", [128, TSEQ], BF16))
        vnb = ec(nc.sbuf_tensor("vnb", [128, 64, 128], BF16))
        qb3 = ec(nc.sbuf_tensor("qb3", [128, 2, BAND], BF16))
        Pt = ec(nc.sbuf_tensor("Pt", [128, 4, BAND], BF16))
        kposC = ec(nc.sbuf_tensor("kposC", [128, 2], F32))
        kpm = ec(nc.sbuf_tensor("kpm", [128, 4], F32))
        dmaskT = ec(nc.sbuf_tensor("dmaskT", [128, 4, BAND], BF16))
        meff = ec(nc.sbuf_tensor("meff", [128, 16, BAND], BF16))
        skC = ec(nc.sbuf_tensor("skC", [128, 4], F32))
        biasC = ec(nc.sbuf_tensor("biasC", [128, NBANDS], F32))
        flagB = ec(nc.sbuf_tensor("flagB", [128, NBANDS], F32))
        bcolB = ec(nc.sbuf_tensor("bcolB", [128, NBANDS], F32))
        iota16B = ec(nc.sbuf_tensor("iota16B", [128, NBANDS], F32))
        gcol3 = ec(nc.sbuf_tensor("gcol3", [128, 1], F32))
        qmaxR = ec(nc.sbuf_tensor("qmaxR", [1, H0], F32))
        kmaxR = ec(nc.sbuf_tensor("kmaxR", [1, 16], F32))
        brow = ec(nc.sbuf_tensor("brow", [1, 16], F32))
        ksqB = ec(nc.sbuf_tensor("ksqB", [128, 64], F32))
        kmx = ec(nc.sbuf_tensor("kmx", [128, 1], F32))
        kmxr = ec(nc.sbuf_tensor("kmxr", [1, 128], F32))
        sq3 = ec(nc.sbuf_tensor("sq3", [128, BAND], BF16))
        rbS3 = ec(nc.sbuf_tensor("rbS3", [128, BAND], F32))
        zrhsM3 = ec(nc.sbuf_tensor("zrhsM3", [128, BAND], F32))
        rows3 = ec(nc.sbuf_tensor("rows3", [1, 2 * BAND], F32))
        ones_b3 = ec(nc.sbuf_tensor("ones_b3", [128, 1], BF16))
        ones1283 = ec(nc.sbuf_tensor("ones1283", [128, 128], F32))
        ident_f3 = ec(nc.sbuf_tensor("ident_f3", [128, 128], F32))
        iotap3 = ec(nc.sbuf_tensor("iotap3", [128, 128], F32))
        iotaT3 = ec(nc.sbuf_tensor("iotaT3", [128, BAND], F32))
        bposC3 = ec(nc.sbuf_tensor("bposC3", [128, LBANDS], F32))
        tdC3 = ec(nc.sbuf_tensor("tdC3", [64, 1], F32))
        w512C = ec(nc.sbuf_tensor("w512C", [64, 1], F32))
        wpow = ec(nc.sbuf_tensor("wpow", [64, 4], F32))
        w2048C = ec(nc.sbuf_tensor("w2048C", [64, 1], F32))
        sfinB = ec(nc.sbuf_tensor("sfinB", [64, 4], F32))
        tmp4 = ec(nc.sbuf_tensor("tmp4", [64, 4], F32))
        csumC = ec(nc.sbuf_tensor("csumC", [64, 1], F32))
        carC = ec(nc.sbuf_tensor("carC", [64, 2], F32))
        decayP = ec(nc.sbuf_tensor("decayP", [64, BAND], F32))
        slocB3 = ec(nc.sbuf_tensor("slocB3", [64, 2, BAND], F32))
        stS = ec(nc.sbuf_tensor("stS", [64, 2, BAND], F32))

        ps = [ec(nc.psum_tensor(f"p3_{i}", [128, BAND], F32))
              for i in range(5)]
        ps_ra3 = ec(nc.psum_tensor("p3_ra", [1, BAND], F32))
        ps_rb3 = ec(nc.psum_tensor("p3_rb", [1, BAND], F32))

        sems = {}
        for n in SEMS:
            sems[n] = ec(nc.semaphore(n))

        CSC = float(1.0 / np.sqrt(128.0))
        BIGB = 1.0e9

        ld_td = sc.dma("sync", "ld", tdC3[:], i_td, name="l3td")
        ld_bp = sc.dma("sync", "ld", bposC3[:], i_bpos, name="l3bp")

        def fn_io3(e):
            e.iota(iotap3[:], pattern=[[-1, 128]], base=0,
                   channel_multiplier=1, allow_small_or_imprecise_dtypes=True)
            e.iota(iotaT3[:], pattern=[[1, BAND]], base=0,
                   channel_multiplier=0, allow_small_or_imprecise_dtypes=True)
            e.iota(iota16B[:], pattern=[[1, NBANDS]], base=0,
                   channel_multiplier=0, allow_small_or_imprecise_dtypes=True)
            e.iota(kposC[:, 0:1], pattern=[[0, 1]], base=0,
                   channel_multiplier=1, allow_small_or_imprecise_dtypes=True)
            return e.iota(decayP[:], pattern=[[1, BAND]], base=1,
                          channel_multiplier=0,
                          allow_small_or_imprecise_dtypes=True)
        s_io3 = sc.add("gpsimd", fn_io3, name="io3")

        def fn_con3(e):
            e.memset(ones_b3[:], 1.0)
            e.memset(ones1283[:], 1.0)
            e.memset(zrhsM3[:], 0.0)
            return e.tensor_single_scalar(ident_f3[:], iotap3[:], 0.0,
                                          op=ALU.is_equal)
        s_con3 = sc.add("vector", fn_con3, [s_io3], name="con3")

        s_w512 = sc.add("scalar", lambda e: e.activation(
            w512C[:], tdC3[:], AF.Exp, scale=512.0), [ld_td, s_con3],
            name="w512")
        s_dp = sc.add("scalar", lambda e: e.activation(
            decayP[:], decayP[:], AF.Exp, scale=tdC3[:]),
            [s_w512, s_io3], name="dp3")
        wp0 = sc.add("vector", lambda e: (
            e.memset(wpow[:, 0:1], 1.0),
            e.tensor_copy(wpow[:, 1:2], w512C[:]))[-1], [s_w512], name="wp0")
        wp1 = sc.add("vector", lambda e: e.tensor_mul(
            wpow[:, 2:3], wpow[:, 1:2], w512C[:]), [wp0], name="wp1")
        wp2 = sc.add("vector", lambda e: e.tensor_mul(
            wpow[:, 3:4], wpow[:, 2:3], w512C[:]), [wp1], name="wp2")
        wp3 = sc.add("vector", lambda e: e.tensor_mul(
            w2048C[:], wpow[:, 3:4], w512C[:]), [wp2], name="wp3")

        st_guard = {0: None, 1: None}
        prev_car = None
        prev_prod = wp3
        for l in range(LBANDS):
            cur = l % 2
            cols = slice(l * BAND, (l + 1) * BAND)
            ldsf = sc.dma("sync", "ld", sfinB[:], i_sfin[l],
                          [prev_prod], name=f"rsf{l}")
            ldsl = sc.dma("sync", "ld", slocB3[:, cur, :], i_sloc[:, cols],
                          [st_guard[cur]], name=f"rsl{l}")
            prod = sc.add("vector", lambda e: e.tensor_mul(
                tmp4[:], sfinB[:], wpow[:]), [ldsf, wp3, prev_car],
                name=f"rpr{l}")
            prev_prod = prod
            if l == 0:
                car = sc.add("vector", lambda e: e.reduce_sum(
                    carC[:, 0:1], tmp4[:], axis=AX.X), [prod],
                    name=f"rcar{l}")
            else:
                cs = sc.add("vector", lambda e: e.reduce_sum(
                    csumC[:], tmp4[:], axis=AX.X), [prod], name=f"rcs{l}")
                car = sc.add("vector", (lambda cur: lambda e:
                             e.scalar_tensor_tensor(
                                 carC[:, cur:cur + 1],
                                 carC[:, 1 - cur:2 - cur], w2048C[:],
                                 csumC[:], op0=ALU.mult, op1=ALU.add))(cur),
                             [cs], name=f"rcar{l}")
            prev_car = car
            stt_ = sc.add("vector", (lambda cur: lambda e:
                          e.scalar_tensor_tensor(
                              stS[:, cur, :], decayP[:],
                              carC[:, cur:cur + 1], slocB3[:, cur, :],
                              op0=ALU.mult, op1=ALU.add))(cur),
                          [car, ldsl, s_dp, st_guard[cur]],
                          name=f"rst{l}")
            st_guard[cur] = sc.dma("sync", "st", o_states[:, cols],
                                   stS[:, cur, :], [stt_], name=f"rsts{l}")

        prev_sq = s_con3
        qm_steps = []
        for qh in range(H0):
            mmq = None
            for l in range(LBANDS):
                ldq = sc.dma("sync", "ld", qb3[:, l % 2, :],
                             q_r[:, qh, l * BAND:(l + 1) * BAND],
                             [prev_sq], name=f"bq{qh}_{l}")
                sqq = sc.add("scalar", (lambda l: lambda e: e.square(
                    sq3[:], qb3[:, l % 2, :]))(l), [ldq, prev_sq],
                    name=f"bsq{qh}_{l}")
                mmq = sc.add("tensor", (lambda l: lambda e: e.matmul(
                    ps_rb3[:], ones_b3[:], sq3[:], start=(l == 0),
                    stop=(l == LBANDS - 1)))(l),
                    [sqq] + (bk.dep_w(6) if l == 0 else []),
                    name=f"bmm{qh}_{l}")
                prev_sq = sqq
            bk.wr(6, mmq)
            qm = sc.add("vector", (lambda qh: lambda e: e.reduce_max(
                qmaxR[0:1, qh:qh + 1], ps_rb3[:], axis=AX.X))(qh), [mmq],
                name=f"bqm{qh}")
            bk.rd(6, qm)
            prev_sq = qm
            qm_steps.append(qm)
        km_steps = []
        for kh in range(4):
            ldk3 = sc.dma("sync", "ld", ksqB[:], i_ksq[kh],
                          [km_steps[-1] if km_steps else None],
                          name=f"bk{kh}")
            r1 = sc.add("vector", lambda e: e.reduce_max(
                kmx[:], ksqB[:], axis=AX.X),
                [ldk3, km_steps[-1] if km_steps else None], name=f"bkr{kh}")
            tr = sc.add("tensor", lambda e: e.transpose(
                ps[4][0:1, 0:128], kmx[:], ident_f3[:]),
                [r1, s_con3] + bk.dep_w(4), name=f"bkt{kh}")
            bk.wr(4, tr)
            cp = sc.add("vector", lambda e: e.tensor_copy(
                kmxr[:], ps[4][0:1, 0:128]), [tr], name=f"bkc{kh}")
            bk.rd(4, cp)
            r2 = sc.add("vector", (lambda kh: lambda e: e.reduce_max(
                kmaxR[0:1, kh:kh + 1], kmxr[:], axis=AX.X))(kh), [cp],
                name=f"bkm{kh}")
            km_steps.append(r2)
        kx2 = sc.add("vector", lambda e: (
            [e.tensor_copy(kmaxR[0:1, 15 - i:16 - i],
                           kmaxR[0:1, (15 - i) // 4:(15 - i) // 4 + 1])
             for i in range(16)][-1]), km_steps, name="kx2")
        bm1 = sc.add("vector", lambda e: e.tensor_mul(
            brow[:], qmaxR[:], kmaxR[:]), [kx2] + qm_steps, name="bm1")
        bm2 = sc.add("scalar", lambda e: e.activation(
            brow[:], brow[:], AF.Sqrt), [bm1], name="bm2")
        bm3 = sc.add("vector", lambda e: e.tensor_single_scalar(
            brow[:], brow[:], -CSC, op=ALU.mult), [bm2], name="bm3")
        bm4 = sc.add("vector", lambda e: e.tensor_copy(
            zrhsM3[0:1, 0:16], brow[:]), [bm3], name="bm4")
        bm5 = sc.add("tensor", lambda e: e.matmul(
            ps[4][:, 0:16], ones1283[:], zrhsM3[:, 0:16], start=True,
            stop=True), [bm4] + bk.dep_w(4), name="bm5")
        bk.wr(4, bm5)
        bm6 = sc.add("vector", lambda e: (
            e.tensor_copy(bcolB[:], ps[4][:, 0:16]),
            e.memset(zrhsM3[0:1, 0:16], 0.0))[-1], [bm5], name="bm6")
        bk.rd(4, bm6)

        kpm_s = sc.add("gpsimd", lambda e: (
            [e.tensor_single_scalar(kpm[:, m:m + 1], kposC[:, 0:1],
                                    float(128 * m), op=ALU.add)
             for m in range(4)][-1]), [s_io3], name="kpm")
        dm_s = sc.add("gpsimd", lambda e: (
            [e.tensor_single_scalar(dmaskT[:, m, :], iotaT3[:],
                                    kpm[:, m:m + 1], op=ALU.is_ge)
             for m in range(4)][-1]), [kpm_s], name="dms")

        SB = [0, 1, 3, 4]
        prev_kv = [None]
        prev_bias = [bm6]
        prev_store = [None]
        it = 0
        for kvh in range(4):
            ldkf = sc.dma("sync", "ld", kfb[:], i_kf[kvh], [prev_kv[0]],
                          name=f"fkf{kvh}")
            ldvf = sc.dma("sync", "ld", vnb[:],
                          i_vf[kvh].rearrange("(kt p) d -> p kt d", p=128),
                          [prev_kv[0]], name=f"fvf{kvh}")
            for l in range(LBANDS):
                gv = sc.add("vector", (lambda l: lambda e:
                            e.tensor_single_scalar(
                                gcol3[:], bposC3[:, l:l + 1],
                                1.0 / 512.0, op=ALU.mult))(l),
                            [ld_bp, prev_bias[0]], name=f"fg1{kvh}_{l}")
                fv = sc.add("vector", lambda e: e.tensor_single_scalar(
                    flagB[:], iota16B[:], gcol3[:], op=ALU.is_gt),
                    [gv, s_io3], name=f"fg2{kvh}_{l}")
                sk = sc.add("vector", (lambda l: lambda e: (
                    [e.tensor_single_scalar(skC[:, m:m + 1], gcol3[:],
                                            float(4 * l + m), op=ALU.is_gt)
                     for m in range(4)][-1]))(l), [fv], name=f"fsk{kvh}_{l}")
                me = sc.add("vector", (lambda l: lambda e: (
                    [e.tensor_single_scalar(meff[:, kr * 4 + m, :],
                                            dmaskT[:, m, :],
                                            skC[:, kr:kr + 1], op=ALU.max)
                     for kr in range(4) for m in range(4)][-1]))(l),
                    [sk, dm_s], name=f"fme{kvh}_{l}")
                for qi in range(4):
                    qh = 4 * kvh + qi
                    ldq = sc.dma("sync", "ld", qb3[:, it % 2, :],
                                 q_r[:, qh, l * BAND:(l + 1) * BAND],
                                 [prev_bias[0]], name=f"fq{qh}_{l}")
                    fb = sc.add("vector", (lambda qh: lambda e:
                                e.tensor_scalar(
                                    biasC[:], flagB[:], -BIGB,
                                    bcolB[:, qh:qh + 1], op0=ALU.mult,
                                    op1=ALU.add))(qh), [fv, prev_bias[0]],
                                name=f"fb{qh}_{l}")
                    prev_bias[0] = fb
                    nT = 16 * (l + 1)
                    prev_exp = {0: None, 1: None, 2: None, 3: None}
                    dp = None
                    started = False
                    for ti in range(nT):
                        K = ti // 4
                        sl = ti % 4
                        sbk = SB[sl]
                        tok = ti * 128
                        smm = sc.add("tensor", (lambda tok, sbk, it:
                                     lambda e: e.matmul(
                                         ps[sbk][:],
                                         kfb[:, tok:tok + 128],
                                         qb3[:, it % 2, :],
                                         start=True, stop=True))(
                                         tok, sbk, it),
                                     [ldkf, ldq] + bk.dep_w(sbk),
                                     name=f"S{qh}_{l}_{ti}")
                        bk.wr(sbk, smm)
                        ex = sc.add("scalar", (lambda K, sl, sbk: lambda e:
                                    e.activation(Pt[:, sl, :],
                                                 ps[sbk][:], AF.Exp,
                                                 scale=CSC,
                                                 bias=biasC[:, K:K + 1]))(
                                        K, sl, sbk),
                                    [smm, fb, prev_exp[sl]],
                                    name=f"E{qh}_{l}_{ti}")
                        bk.rd(sbk, ex)
                        use = ex
                        if K >= 4 * l:
                            kk = (K - 4 * l) * 4 + ti % 4
                            use = sc.add("vector", (lambda sl, kk: lambda e:
                                         e.tensor_mul(Pt[:, sl, :],
                                                      Pt[:, sl, :],
                                                      meff[:, kk, :]))(
                                             sl, kk), [ex, me],
                                         name=f"M{qh}_{l}_{ti}")
                        last = (ti == nT - 1)
                        dp = sc.add("tensor",
                                    (lambda tok, sl, st=not started,
                                     sp=last: lambda e: (
                                         e.matmul(ps_ra3[:], ones_b3[:],
                                                  Pt[:, sl, :], start=st,
                                                  stop=sp),
                                         e.matmul(ps[2][:],
                                                  vnb[:, tok // 128, :],
                                                  Pt[:, sl, :], start=st,
                                                  stop=sp))[-1])(tok, sl),
                                    [use, ldvf] + ((bk.dep_w(5) +
                                                    bk.dep_w(2)) if not
                                                   started else []),
                                    name=f"P{qh}_{l}_{ti}")
                        prev_exp[sl] = dp
                        started = True
                    bk.wr(5, dp)
                    bk.wr(2, dp)
                    dn1 = sc.add("vector", lambda e: e.tensor_copy(
                        rows3[0:1, 0:BAND], ps_ra3[:]), [dp],
                        name=f"dn1{qh}_{l}")
                    bk.rd(5, dn1)
                    dn2 = sc.add("vector", lambda e: e.reciprocal(
                        rows3[0:1, 0:BAND], rows3[0:1, 0:BAND]), [dn1],
                        name=f"dn2{qh}_{l}")
                    dn3 = sc.add("vector", lambda e: e.tensor_copy(
                        zrhsM3[0:1, :], rows3[0:1, 0:BAND]), [dn2],
                        name=f"dn3{qh}_{l}")
                    dn4 = sc.add("tensor", lambda e: e.matmul(
                        ps[3][:], ones1283[:], zrhsM3[:], start=True,
                        stop=True), [dn3] + bk.dep_w(3), name=f"dn4{qh}_{l}")
                    bk.wr(3, dn4)
                    dn5 = sc.add("vector", lambda e: e.tensor_copy(
                        rbS3[:], ps[3][:]), [dn4], name=f"dn5{qh}_{l}")
                    bk.rd(3, dn5)
                    aev = sc.add("vector", lambda e: e.tensor_mul(
                        sq3[:], ps[2][:], rbS3[:]),
                        [dn5, dp, prev_store[0]], name=f"aev{qh}_{l}")
                    bk.rd(2, aev)
                    ast = sc.dma("sync", "st",
                                 attn_r[:, qh, l * BAND:(l + 1) * BAND],
                                 sq3[:], [aev], name=f"ast{qh}_{l}")
                    prev_store[0] = ast
                    prev_bias[0] = aev
                    it += 1
            prev_kv[0] = prev_store[0]

        fin3 = sc.add("vector", lambda e: e.engine_nop(),
                      [prev_store[0]] + [st_guard[0], st_guard[1]],
                      name="final3")
        sc.emit(nc, sems)
    return nc



def build_launch4():
    nc = bass.Bass("TRN2", target_bir_lowering=False, debug=False)

    i_x1 = nc.dram_tensor("x1_t", [D, LT], F32, kind="ExternalInput").ap()
    i_xn1 = nc.dram_tensor("xn1_t", [D, LT], BF16, kind="ExternalInput").ap()
    i_attn = nc.dram_tensor("attn_t", [D, LT], BF16,
                            kind="ExternalInput").ap()
    i_st = nc.dram_tensor("states_t", [64, LT], F32,
                          kind="ExternalInput").ap()
    i_wor = nc.dram_tensor("wor_t", [64, D], F32, kind="ExternalInput").ap()
    i_wo2 = nc.dram_tensor("wo2_t", [NDT, D, 128], F32,
                           kind="ExternalInput").ap()
    i_fnw = nc.dram_tensor("fnw_c", [128, NDT], F32,
                           kind="ExternalInput").ap()
    i_gw2 = nc.dram_tensor("gw2_t", [D, 8], F32, kind="ExternalInput").ap()
    i_wg2 = nc.dram_tensor("wg2", [8, D, 1408], F32,
                           kind="ExternalInput").ap()
    i_wu2 = nc.dram_tensor("wu2", [8, D, 1408], F32,
                           kind="ExternalInput").ap()
    i_wd2 = nc.dram_tensor("wd2", [8, 1408, D], F32,
                           kind="ExternalInput").ap()

    o_x = nc.dram_tensor("xo_t", [D, LT], F32, kind="ExternalOutput").ap()
    d_comb4 = nc.dram_tensor("comb4_d", [1, 8192], F32, kind="Internal").ap()

    x1_r = i_x1.rearrange("(k p) t -> p k t", p=128)
    xn1_r = i_xn1.rearrange("(k p) t -> p k t", p=128)
    attn_in_r = i_attn.rearrange("(k p) t -> p k t", p=128)
    gw_rr = i_gw2.rearrange("(k p) e -> p k e", p=128)
    o_x_r = o_x.rearrange("(k p) t -> p k t", p=128)

    sc = Sched()
    bk = Banks()

    with ExitStack() as stk:
        ec = stk.enter_context
        xbh = ec(nc.sbuf_tensor("xbh", [128, NDT, 1024], F32))
        hid2 = ec(nc.sbuf_tensor("hid2", [128, NDT, 1024], BF16))
        REG4 = ec(nc.sbuf_tensor("REG4", [128, 23552], BF16))
        abuf = REG4[:, 0:16384].rearrange("p (k t) -> p k t", k=NDT)
        h2h = REG4[:, 0:16384].rearrange("p (k t) -> p k t", k=NDT)
        gS = REG4[:, 16384:22528].rearrange("p (j t) -> p j t", j=6)
        sgb = REG4[:, 22528:23040]
        gtmp = REG4[:, 23040:23552]
        WBUF = ec(nc.sbuf_tensor("WBUF", [128, 8192], BF16))
        wo2p = WBUF[:, 0:4096].rearrange("p (s k c) -> p s k c", s=2, k=NDT)
        wga = WBUF[:, 0:4096].rearrange("p (s k c) -> p s k c", s=2, k=NDT)
        wub = WBUF[:, 4096:8192].rearrange("p (s k c) -> p s k c", s=2,
                                           k=NDT)
        wdb = ec(nc.sbuf_tensor("wdb", [128, 2, 6, 128], BF16))
        worb = ec(nc.sbuf_tensor("worb", [64, D], BF16))
        gwb = ec(nc.sbuf_tensor("gwb", [128, NDT, 8], BF16))
        stb = ec(nc.sbuf_tensor("stb", [64, 1024], BF16))
        fnwC = ec(nc.sbuf_tensor("fnwC", [128, NDT], F32))
        logit = ec(nc.sbuf_tensor("logit", [8, 1024], F32))
        combT = logit
        combB = ec(nc.sbuf_tensor("combB", [128, 2, BAND], F32))
        gsA = ec(nc.sbuf_tensor("gsA", [128, 8], F32))
        gsB = ec(nc.sbuf_tensor("gsB", [128, 8], F32))
        gcol = ec(nc.sbuf_tensor("gcol", [128, 8], F32))
        xtmp4 = ec(nc.sbuf_tensor("xtmp4", [128, BAND], F32))
        xtmp4b = xtmp4
        sqS = ec(nc.sbuf_tensor("sqS", [128, BAND], BF16))
        rbS4 = ec(nc.sbuf_tensor("rbS4", [128, BAND], F32))
        zrhsM4 = ec(nc.sbuf_tensor("zrhsM4", [128, BAND], F32))
        rows4 = ec(nc.sbuf_tensor("rows4", [1, 2 * BAND], F32))
        ones_b4 = ec(nc.sbuf_tensor("ones_b4", [128, 1], BF16))
        ones1284 = ec(nc.sbuf_tensor("ones1284", [128, 128], F32))
        ident_f4 = ec(nc.sbuf_tensor("ident_f4", [128, 128], F32))
        iotap4 = ec(nc.sbuf_tensor("iotap4", [128, 128], F32))
        ccol4 = ec(nc.sbuf_tensor("ccol4", [1, 2], F32))

        ps = [ec(nc.psum_tensor(f"p4_{i}", [128, BAND], F32))
              for i in range(5)]
        ps_ra4 = ec(nc.psum_tensor("p4_ra", [1, BAND], F32))

        sems = {}
        for n in SEMS:
            sems[n] = ec(nc.semaphore(n))

        s_io4 = sc.add("gpsimd", lambda e: e.iota(
            iotap4[:], pattern=[[-1, 128]], base=0, channel_multiplier=1,
            allow_small_or_imprecise_dtypes=True), name="io4")

        def fn_con4(e):
            e.memset(ones_b4[:], 1.0)
            e.memset(ones1284[:], 1.0)
            e.memset(zrhsM4[:], 0.0)
            e.memset(ccol4[0:1, 0:1], RMS_EPS)
            return e.tensor_single_scalar(ident_f4[:], iotap4[:], 0.0,
                                          op=ALU.is_equal)
        s_con4 = sc.add("vector", fn_con4, [s_io4], name="con4")

        ld_wor = sc.dma("gpsimd", "wt", worb[:], i_wor, name="l4wor")
        ld_gw = sc.dma("gpsimd", "wt", gwb[:], gw_rr, name="l4gw")
        ld_fnw = sc.dma("sync", "ld", fnwC[:], i_fnw, name="l4fnw")

        xbh_guard = [None]
        hid2_guard = [None]
        h2h_guard = [None]
        wbuf_guard = {0: None, 1: None}
        wdb_guard = {0: None, 1: None}
        zrhs_guard = [s_con4]
        comb_guard = [None]
        prev_gev = [None]
        prev_sexp = [None]
        st_steps = []

        for hh in range(2):
            hcols = slice(hh * 1024, (hh + 1) * 1024)
            ldx1 = sc.dma("sync", "ld", xbh[:], x1_r[:, :, hcols],
                          [xbh_guard[0]], name=f"ldx1_{hh}")
            ldxn1 = sc.dma("sync", "ld", hid2[:], xn1_r[:, :, hcols],
                           [hid2_guard[0]], name=f"ldxn1_{hh}")
            ldat = sc.dma("sync", "ld", abuf[:], attn_in_r[:, :, hcols],
                          [prev_gev[0]], name=f"ldat{hh}")
            ldst = sc.dma("gpsimd", "wt", stb[:], i_st[:, hcols],
                          [xbh_guard[0]], name=f"ldst{hh}")

            wor_adds = {}
            for j in range(NDT):
                for hb in range(2):
                    bsl = slice(hb * BAND, (hb + 1) * BAND)
                    wmm = sc.add("tensor", (lambda j, bsl: lambda e:
                                 e.matmul(ps[4][:],
                                          worb[:, j * 128:(j + 1) * 128],
                                          stb[:, bsl], start=True,
                                          stop=True))(j, bsl),
                                 [ld_wor, ldst] + bk.dep_w(4),
                                 name=f"wm{hh}_{j}_{hb}")
                    bk.wr(4, wmm)
                    xa = sc.add("vector", (lambda j, bsl: lambda e:
                                e.tensor_add(xbh[:, j, bsl],
                                             xbh[:, j, bsl], ps[4][:]))(
                                    j, bsl), [wmm, ldx1],
                                name=f"wa{hh}_{j}_{hb}")
                    bk.rd(4, xa)
                    wor_adds[(j, hb)] = xa

            hv3s = {}
            for j in range(NDT):
                slot = j % 2
                ldw = sc.dma("gpsimd", "wt", wo2p[:, slot],
                             i_wo2[j].rearrange("(k p) c -> p k c", p=128),
                             [wbuf_guard[slot]], name=f"lo{hh}_{j}")
                glast = None
                for hb in range(2):
                    bsl = slice(hb * BAND, (hb + 1) * BAND)
                    bank = hb

                    def om(e, slot=slot, bsl=bsl, bank=bank):
                        i = None
                        for kt in range(NDT):
                            i = e.matmul(ps[bank][:], wo2p[:, slot, kt, :],
                                         abuf[:, kt, bsl],
                                         start=(kt == 0),
                                         stop=(kt == NDT - 1))
                        return i
                    omm = sc.add("tensor", om,
                                 [ldw, ldat] + bk.dep_w(bank),
                                 name=f"om{hh}_{j}_{hb}")
                    bk.wr(bank, omm)
                    glast = omm
                    hv1 = sc.add("vector", (lambda j, bsl, bank: lambda e:
                                 e.tensor_add(xtmp4[:], ps[bank][:],
                                              hid2[:, j, bsl]))(
                                     j, bsl, bank), [omm, ldxn1],
                                 name=f"h1{hh}_{j}_{hb}")
                    bk.rd(bank, hv1)
                    hv2 = sc.add("vector", (lambda j, bsl: lambda e:
                                 e.tensor_add(xbh[:, j, bsl],
                                              xbh[:, j, bsl], xtmp4[:]))(
                                     j, bsl), [hv1, wor_adds[(j, hb)]],
                                 name=f"h2{hh}_{j}_{hb}")
                    hv3 = sc.add("vector", (lambda j, bsl: lambda e:
                                 e.tensor_copy(hid2[:, j, bsl], xtmp4[:]))(
                                     j, bsl), [hv2], name=f"h3{hh}_{j}_{hb}")
                    hv3s[(j, hb)] = hv3
                wbuf_guard[slot] = glast

            h2_steps = {}
            for hb in range(2):
                bsl = slice(hb * BAND, (hb + 1) * BAND)
                mmh = []
                for kt in range(NDT):
                    s_ = sc.add("scalar", (lambda kt, bsl: lambda e:
                                e.square(sqS[:], hid2[:, kt, bsl]))(kt, bsl),
                                [hv3s[(kt, hb)],
                                 mmh[kt - 2] if kt >= 2 else prev_sexp[0]],
                                name=f"hsq{hh}_{hb}_{kt}")
                    m_ = sc.add("tensor", (lambda kt: lambda e: e.matmul(
                        ps_ra4[:], ones_b4[:], sqS[:], start=(kt == 0),
                        stop=(kt == NDT - 1)))(kt),
                        [s_] + (bk.dep_w(5) if kt == 0 else []),
                        name=f"hmm{hh}_{hb}_{kt}")
                    mmh.append(m_)
                bk.wr(5, mmh[-1])
                rhA = sc.add("scalar", (lambda hb: lambda e:
                    e.activation(rows4[0:1, hb * BAND:(hb + 1) * BAND],
                                 ps_ra4[:], AF.Ln, scale=1.0 / D,
                                 bias=ccol4[0:1, 0:1]))(hb), [mmh[-1]],
                    name=f"rhA{hh}_{hb}")
                rh = sc.add("scalar", (lambda hb: lambda e:
                    e.activation(rows4[0:1, hb * BAND:(hb + 1) * BAND],
                                 rows4[0:1, hb * BAND:(hb + 1) * BAND],
                                 AF.Exp, scale=-0.5))(hb), [rhA],
                    name=f"rh{hh}_{hb}")
                prev_sexp[0] = rh
                bk.rd(5, rh)
                zch = sc.add("vector", (lambda hb: lambda e: e.tensor_copy(
                    zrhsM4[0:1, :],
                    rows4[0:1, hb * BAND:(hb + 1) * BAND]))(hb),
                    [rh, zrhs_guard[0]], name=f"zch{hh}_{hb}")
                bch = sc.add("tensor", lambda e: e.matmul(
                    ps[3][:], ones1284[:], zrhsM4[:], start=True,
                    stop=True), [zch] + bk.dep_w(3), name=f"bch{hh}_{hb}")
                bk.wr(3, bch)
                zrhs_guard[0] = bch
                rbh = sc.add("vector", lambda e: e.tensor_copy(
                    rbS4[:], ps[3][:]), [bch], name=f"rbh{hh}_{hb}")
                bk.rd(3, rbh)
                for kt in range(NDT):
                    def fn(e, kt=kt, bsl=bsl):
                        e.tensor_mul(xtmp4b[:], hid2[:, kt, bsl], rbS4[:])
                        return e.tensor_single_scalar(
                            h2h[:, kt, bsl], xtmp4b[:], fnwC[:, kt:kt + 1],
                            op=ALU.mult)
                    h2_steps[(kt, hb)] = sc.add(
                        "vector", fn, [rbh, ld_fnw, h2h_guard[0]],
                        name=f"h2s{hh}_{hb}_{kt}")
            h2_all = list(h2_steps.values())

            gmm = []
            for band in range(2):
                def g(e, band=band):
                    i = None
                    for kt in range(NDT):
                        i = e.matmul(ps[band][0:8, :], gwb[:, kt, :],
                                     h2h[:, kt,
                                         band * BAND:(band + 1) * BAND],
                                     start=(kt == 0), stop=(kt == NDT - 1))
                    return i
                gm = sc.add("tensor", g,
                            [h2_steps[(kt, band)] for kt in range(NDT)] +
                            [ld_gw] + bk.dep_w(band), name=f"gmm{hh}_{band}")
                bk.wr(band, gm)
                gmm.append(gm)

            def lv(e):
                e.tensor_copy(logit[0:8, 0:BAND], ps[0][0:8, :])
                return e.tensor_copy(logit[0:8, BAND:2 * BAND], ps[1][0:8, :])
            lev = sc.add("vector", lv, gmm + [comb_guard[0]],
                         name=f"lev{hh}")
            bk.rd(0, lev)
            bk.rd(1, lev)

            comb_done = []
            prev_cev = None
            prev_vt = None
            prev_btr = None
            for tt in range(8):
                c0 = tt * 128
                ftr = sc.add("tensor", (lambda c0: lambda e: e.transpose(
                    ps[2][0:128, 0:8], logit[0:8, c0:c0 + 128],
                    ident_f4[0:8, 0:8]))(c0), [lev, prev_cev] + bk.dep_w(2),
                    name=f"ftr{hh}_{tt}")
                bk.wr(2, ftr)
                smax0 = sc.add("vector", lambda e: e.reduce_max(
                    gcol[:, 0:1], ps[2][0:128, 0:8], axis=AX.X),
                    [ftr, prev_sexp[0]], name=f"vmax0{hh}_{tt}")
                smax = sc.add("vector", lambda e: e.tensor_single_scalar(
                    gcol[:, 1:2], gcol[:, 0:1], -1.0, op=ALU.mult), [smax0],
                    name=f"vmax{hh}_{tt}")
                sexp = sc.add("scalar", lambda e: e.activation(
                    gsA[:], ps[2][0:128, 0:8], AF.Exp, bias=gcol[:, 1:2],
                    accum_out=gcol[:, 2:3]), [smax, prev_vt],
                    name=f"sexp{hh}_{tt}")
                prev_sexp[0] = sexp
                bk.rd(2, sexp)
                v1 = sc.add("vector", lambda e: e.reciprocal(
                    gcol[:, 2:3], gcol[:, 2:3]), [sexp, prev_btr],
                    name=f"vt1{hh}_{tt}")
                v2 = sc.add("vector", lambda e: e.tensor_single_scalar(
                    gsA[:], gsA[:], gcol[:, 2:3], op=ALU.mult), [v1],
                    name=f"vt2{hh}_{tt}")
                v3 = sc.add("vector", lambda e: e.reduce_max(
                    gcol[:, 0:1], gsA[:], axis=AX.X), [v2],
                    name=f"vt3{hh}_{tt}")
                v4 = sc.add("vector", lambda e: e.tensor_single_scalar(
                    gsB[:], gsA[:], gcol[:, 0:1], op=ALU.is_equal), [v3],
                    name=f"vt4{hh}_{tt}")
                v5 = sc.add("vector", lambda e: e.scalar_tensor_tensor(
                    gsB[:], gsB[:], NEG_BIG, gsA[:], op0=ALU.mult,
                    op1=ALU.add), [v4], name=f"vt5{hh}_{tt}")
                v6 = sc.add("vector", lambda e: e.reduce_max(
                    gcol[:, 3:4], gsB[:], axis=AX.X), [v5],
                    name=f"vt6{hh}_{tt}")
                v7 = sc.add("vector", lambda e: e.tensor_single_scalar(
                    gsB[:], gsA[:], gcol[:, 3:4], op=ALU.is_ge), [v6],
                    name=f"vt7{hh}_{tt}")
                v8 = sc.add("vector", lambda e: e.tensor_mul(
                    gsB[:], gsB[:], gsA[:]), [v7], name=f"vt8{hh}_{tt}")
                v9 = sc.add("vector", lambda e: e.reduce_sum(
                    gcol[:, 4:5], gsB[:], axis=AX.X), [v8],
                    name=f"vt9{hh}_{tt}")
                v10 = sc.add("vector", lambda e: e.tensor_single_scalar(
                    gcol[:, 4:5], gcol[:, 4:5], 1e-9, op=ALU.add), [v9],
                    name=f"vtA{hh}_{tt}")
                v11 = sc.add("vector", lambda e: e.reciprocal(
                    gcol[:, 4:5], gcol[:, 4:5]), [v10], name=f"vtB{hh}_{tt}")
                vt = sc.add("vector", lambda e: e.tensor_single_scalar(
                    gsB[:], gsB[:], gcol[:, 4:5], op=ALU.mult), [v11],
                    name=f"vtop{hh}_{tt}")
                prev_vt = vt
                btr = sc.add("tensor", lambda e: e.transpose(
                    ps[3][0:8, 0:128], gsB[:], ident_f4[:]),
                    [vt] + bk.dep_w(3), name=f"btr{hh}_{tt}")
                bk.wr(3, btr)
                prev_btr = btr
                cev = sc.add("vector", (lambda c0: lambda e: e.tensor_copy(
                    combT[0:8, c0:c0 + 128], ps[3][0:8, 0:128]))(c0), [btr],
                    name=f"cev{hh}_{tt}")
                bk.rd(3, cev)
                prev_cev = cev
                comb_done.append(cev)
            st_c = sc.dma("sync", "st",
                          d_comb4.rearrange("o (e t) -> (o e) t", e=8),
                          combT[:], comb_done, name=f"stc{hh}")
            prev_cbm = [None]

            x_add_steps = []
            gS_guard = {0: None, 1: None}
            for e_idx in range(8):
                hfs = [list(range(0, 6)), list(range(6, 11))]
                wg_rr = i_wg2[e_idx].rearrange("(k p) h -> p k h", p=128)
                wu_rr = i_wu2[e_idx].rearrange("(k p) h -> p k h", p=128)
                wd_rr = i_wd2[e_idx].rearrange("(j p) d -> p j d", p=128)

                cb = []
                for band in range(2):
                    srcap = d_comb4[0:1, e_idx * 1024 + band * BAND:
                                    e_idx * 1024 + (band + 1) * BAND]
                    cp = sc.dma("sync", "ld", zrhsM4[0:1, :], srcap,
                                [st_c, prev_cbm[0], zrhs_guard[0]],
                                name=f"cbc{hh}_{e_idx}_{band}")
                    mm = sc.add("tensor", lambda e: e.matmul(
                        ps[4][:], ones1284[:], zrhsM4[:], start=True,
                        stop=True), [cp] + bk.dep_w(4),
                        name=f"cbm{hh}_{e_idx}_{band}")
                    prev_cbm[0] = mm
                    zrhs_guard[0] = mm
                    bk.wr(4, mm)
                    ev = sc.add("vector", (lambda band: lambda e:
                                e.tensor_copy(combB[:, band, :], ps[4][:])
                                )(band), [mm],
                                name=f"cbe{hh}_{e_idx}_{band}")
                    bk.rd(4, ev)
                    cb.append(ev)

                prev_ld = {0: wbuf_guard[0], 1: wbuf_guard[1]}
                for hf_i, jl in enumerate(hfs):
                    gS_ready = {}
                    for jj, j in enumerate(jl):
                        slot = j % 2
                        ldg = sc.dma("gpsimd", "wt", wga[:, slot],
                                     wg_rr[:, :, j * 128:(j + 1) * 128],
                                     [prev_ld[slot]],
                                     name=f"lg{hh}_{e_idx}_{j}")
                        ldu = sc.dma("gpsimd", "wt", wub[:, slot],
                                     wu_rr[:, :, j * 128:(j + 1) * 128],
                                     [prev_ld[slot]],
                                     name=f"lu{hh}_{e_idx}_{j}")
                        glast = None
                        for band in range(2):
                            bg = band * 2
                            bu = band * 2 + 1

                            def gg(e, slot=slot, bg=bg, band=band):
                                i = None
                                for kt in range(NDT):
                                    i = e.matmul(
                                        ps[bg][:], wga[:, slot, kt, :],
                                        h2h[:, kt,
                                            band * BAND:(band + 1) * BAND],
                                        start=(kt == 0), stop=(kt == NDT - 1))
                                return i

                            def gu(e, slot=slot, bu=bu, band=band):
                                i = None
                                for kt in range(NDT):
                                    i = e.matmul(
                                        ps[bu][:], wub[:, slot, kt, :],
                                        h2h[:, kt,
                                            band * BAND:(band + 1) * BAND],
                                        start=(kt == 0), stop=(kt == NDT - 1))
                                return i
                            depg = [ldg] + h2_all + bk.dep_w(bg)
                            depu = [ldu] + bk.dep_w(bu)
                            g1 = sc.add("tensor", gg, depg,
                                        name=f"g{hh}_{e_idx}_{j}_{band}")
                            bk.wr(bg, g1)
                            g2 = sc.add("tensor", gu, depu,
                                        name=f"u{hh}_{e_idx}_{j}_{band}")
                            bk.wr(bu, g2)
                            glast = g2
                            sil = sc.add("scalar", (lambda bg: lambda e:
                                         e.activation(sgb[:], ps[bg][:],
                                                      AF.Silu))(bg),
                                         [g1, prev_gev[0], ldat],
                                         name=f"sil{hh}_{e_idx}_{j}_{band}")
                            bk.rd(bg, sil)

                            def gev(e, jj=jj, band=band, bu=bu):
                                e.tensor_mul(gtmp[:], sgb[:], ps[bu][:])
                                return e.tensor_mul(
                                    gS[:, jj, band * BAND:(band + 1) * BAND],
                                    gtmp[:], combB[:, band, :])
                            gevs = sc.add("vector", gev,
                                          [sil, g2, gS_guard[band]] + cb,
                                          name=f"ge{hh}_{e_idx}_{j}_{band}")
                            bk.rd(bu, gevs)
                            prev_gev[0] = gevs
                            gS_ready[(jj, band)] = gevs
                        prev_ld[slot] = glast
                        wbuf_guard[slot] = glast

                    jn = len(jl)
                    prev_wd = {0: wdb_guard[0], 1: wdb_guard[1]}
                    for dt in range(NDT):
                        slot = dt % 2
                        lw = sc.dma(
                            "gpsimd", "wt", wdb[:, slot, 0:jn, :],
                            wd_rr[:, hf_i * 6:hf_i * 6 + jn,
                                  dt * 128:(dt + 1) * 128],
                            [prev_wd[slot]],
                            name=f"lw{hh}_{e_idx}_{hf_i}_{dt}")
                        for band in range(2):
                            bank = band

                            def dn(e, slot=slot, jn=jn, band=band,
                                   bank=bank):
                                i = None
                                for jj in range(jn):
                                    i = e.matmul(
                                        ps[bank][:], wdb[:, slot, jj, :],
                                        gS[:, jj,
                                           band * BAND:(band + 1) * BAND],
                                        start=(jj == 0), stop=(jj == jn - 1))
                                return i
                            ddeps = [lw] + bk.dep_w(bank) + [
                                gS_ready[(jj, band)] for jj in range(jn)]
                            dg = sc.add("tensor", dn, ddeps,
                                        name=f"d{hh}_{e_idx}_{hf_i}_{dt}_{band}")
                            bk.wr(bank, dg)
                            if dt >= NDT - 2:
                                gS_guard[band] = dg

                            def xadd(e, dt=dt, band=band, bank=bank):
                                return e.tensor_add(
                                    xbh[:, dt,
                                        band * BAND:(band + 1) * BAND],
                                    xbh[:, dt,
                                        band * BAND:(band + 1) * BAND],
                                    ps[bank][:])
                            xa = sc.add("vector", xadd, [dg],
                                        name=f"xa{hh}_{e_idx}_{hf_i}_{dt}_{band}")
                            bk.rd(bank, xa)
                            x_add_steps.append(xa)
                        prev_wd[slot] = dg
                        wdb_guard[slot] = dg

            stx = sc.dma("sync", "st", o_x_r[:, :, hcols], xbh[:],
                         x_add_steps, name=f"stx{hh}")
            st_steps.append(stx)
            xbh_guard[0] = stx
            hid2_guard[0] = h2_all[-1]
            h2h_guard[0] = x_add_steps[-1]
            comb_guard[0] = prev_cbm[0]

        fin4 = sc.add("vector", lambda e: e.engine_nop(), st_steps,
                      name="final4")
        sc.emit(nc, sems)
    return nc



def _prep_launch3_inputs(inputs, l2_results):
    td_c = np.ascontiguousarray(
        np.asarray(inputs["L1_attn_time_decay"], np.float32).reshape(64, 1))
    kf = {c: np.asarray(l2_results[c]["kf_o"]) for c in range(NCORE)}
    vf = {c: np.asarray(l2_results[c]["vf_o"]) for c in range(NCORE)}
    ksq = {c: np.asarray(l2_results[c]["ksq_o"]) for c in range(NCORE)}
    sloc = {c: np.asarray(l2_results[c]["sloc_o"]) for c in range(NCORE)}
    bf = kf[0].dtype
    in_maps = []
    for c in range(NCORE):
        b, r = c // 4, c % 4
        kf_all = np.zeros((4, 128, TSEQ), bf)
        vf_all = np.zeros((4, TSEQ, 128), bf)
        ksq_all = np.zeros((4, TSEQ), np.float32)
        for g in range(NBANDS):
            oc = 4 * b + g % 4
            lb = g // 4
            gs = slice(g * BAND, (g + 1) * BAND)
            ls = slice(lb * BAND, (lb + 1) * BAND)
            for kvh in range(4):
                kf_all[kvh, :, gs] = kf[oc][kvh * 128:(kvh + 1) * 128, ls]
                vf_all[kvh, gs, :] = vf[oc][kvh, ls, :]
                ksq_all[kvh, gs] = ksq[oc][kvh, ls]
        sfin = np.zeros((LBANDS, 64, 4), np.float32)
        for l in range(LBANDS):
            g = 4 * l + r
            for s in range(4):
                gi = g - 1 - s
                if gi >= 0:
                    oc = 4 * b + gi % 4
                    lb = gi // 4
                    sfin[l, :, s] = sloc[oc][:, lb * BAND + BAND - 1]
        bpos = np.zeros((128, LBANDS), np.float32)
        for l in range(LBANDS):
            bpos[:, l] = 512.0 * (4 * l + r)
        in_maps.append({
            "q_t": np.asarray(l2_results[c]["q_o"]),
            "kf_all": kf_all, "vf_all": vf_all,
            "ksq_all": np.ascontiguousarray(ksq_all.reshape(4, 128, 64)),
            "sloc_t": sloc[c], "sfin_in": sfin, "td_c": td_c,
            "bpos_c": bpos,
        })
    return in_maps


def run_launch3(inputs, l2_results):
    nc = build_launch3()
    in_maps = _prep_launch3_inputs(inputs, l2_results)
    res = run_bass_kernel_spmd(nc, in_maps, core_ids=list(range(NCORE)))
    return res.results


def _prep_launch4_inputs(inputs, l3_results, l2_results):
    wor_t = np.ascontiguousarray(
        np.asarray(inputs["L1_attn_W_output"], np.float32).T)
    ow = np.asarray(inputs["L1_ffn_o_w"], np.float32)
    wo2_t = np.ascontiguousarray(
        ow.T.reshape(D, NDT, 128).transpose(1, 0, 2))
    fnw_c = np.ascontiguousarray(
        np.asarray(inputs["L1_ffn_ffn_norm_w"], np.float32).reshape(
            NDT, 128).T)
    gw2_t = np.ascontiguousarray(
        np.asarray(inputs["L1_ffn_gate_w"], np.float32).T)
    wg2 = np.ascontiguousarray(
        np.asarray(inputs["L1_ffn_wg"], np.float32).transpose(0, 2, 1))
    wu2 = np.ascontiguousarray(
        np.asarray(inputs["L1_ffn_wu"], np.float32).transpose(0, 2, 1))
    wd2 = np.ascontiguousarray(
        np.asarray(inputs["L1_ffn_wd"], np.float32).transpose(0, 2, 1))
    in_maps = []
    for c in range(NCORE):
        in_maps.append({
            "x1_t": np.asarray(l2_results[c]["x1_o"]),
            "xn1_t": np.asarray(l2_results[c]["xn1_o"]),
            "attn_t": np.asarray(l3_results[c]["attn_o"]),
            "states_t": np.asarray(l3_results[c]["states_o"]),
            "wor_t": wor_t, "wo2_t": wo2_t, "fnw_c": fnw_c,
            "gw2_t": gw2_t, "wg2": wg2, "wu2": wu2, "wd2": wd2,
        })
    return in_maps


def run_launch4(inputs, l3_results, l2_results):
    nc = build_launch4()
    in_maps = _prep_launch4_inputs(inputs, l3_results, l2_results)
    res = run_bass_kernel_spmd(nc, in_maps, core_ids=list(range(NCORE)))
    return res.results


def kernel(**inputs):
    l1 = run_launch1(inputs)
    l2 = run_launch2(inputs, l1)
    l3 = run_launch3(inputs, l2)
    l4 = run_launch4(inputs, l3, l2)
    out = np.zeros((BATCH, TSEQ, D), np.float32)
    for c in range(NCORE):
        b, r = c // 4, c % 4
        xo = np.asarray(l4[c]["xo_t"], np.float32)
        for l in range(LBANDS):
            g = 4 * l + r
            out[b, g * BAND:(g + 1) * BAND, :] = \
                xo[:, l * BAND:(l + 1) * BAND].T
    return out
